# version 03
# baseline (speedup 1.0000x reference)
import contextlib
import numpy as np
import concourse.bass as bass
import concourse.mybir as mybir
from concourse.bass_utils import run_bass_kernel_spmd

F32 = mybir.dt.float32
BF16 = mybir.dt.bfloat16
I32 = mybir.dt.int32
AF = mybir.ActivationFunctionType
OP = mybir.AluOpType
AX = mybir.AxisListType

T = 2048
C = 1024
NTT = 16
NFC = 8
NCORES = 8

GROUPS = [[0, 1, 2, 3, 4, 5]]

ENGINES = ("t", "v", "s", "g", "y")
SEMS = ("T", "V", "S", "DWI", "DW0", "DW1", "DWM", "DA", "DB1", "DB2")


class Prog:
    def __init__(self):
        self.ir = {e: [] for e in ENGINES}
        self.cnt = {s: 0 for s in SEMS}
        self.lastwait = {}

    def op(self, eng, fn, inc=None, k=1):
        ev = None
        if inc is not None:
            self.cnt[inc] += k
            ev = (inc, self.cnt[inc])
        self.ir[eng].append(("op", fn, inc, k))
        return ev

    def drain(self, eng):
        self.ir[eng].append(("drain",))

    def wait(self, eng, ev):
        if ev is None:
            return
        sem, val = ev
        key = (eng, sem)
        if self.lastwait.get(key, 0) >= val:
            return
        self.lastwait[key] = val
        self.ir[eng].append(("wait", sem, val))

    def waitall(self, eng, evs):
        for ev in evs:
            self.wait(eng, ev)

    def barrier(self):
        evs = [(s, self.cnt[s]) for s in SEMS if self.cnt[s] > 0]
        for e in ENGINES:
            for ev in evs:
                self.wait(e, ev)


class BankTrack:

    def __init__(self, prog):
        self.prog = prog
        self.w = {}
        self.r = {}

    def pe_begin(self, pi, halves):
        for h in halves:
            for ev in self.r.pop((pi, h), []):
                self.prog.wait("t", ev)

    def pe_done(self, pi, halves, ev):
        for h in halves:
            self.w[(pi, h)] = ev

    def rd_begin(self, eng, pi, halves):
        for h in halves:
            self.prog.wait(eng, self.w.get((pi, h)))

    def rd_done(self, pi, halves, ev):
        for h in halves:
            self.r.setdefault((pi, h), []).append(ev)


DEBUG_TAPS = []

INPUT_SHAPES = {
    "x": (T, C), "ident": (128, 128), "masksh": (2, 16, 128, 4224), "pbrows": (2, 16, 4095),
    "l0wq_t": (C, C), "l0wkv_t": (C, 512), "l0wproj_t": (C, C), "l0bproj": (C,), "l0wgate": (C, 8),
    "l0wgu": (8, C, 8192), "l0wdown": (8, 4096, C),
    "l1wq_t": (C, C), "l1wk_t": (C, C), "l1wv_t": (C, C), "l1wo_t": (C, C),
    "l1bqkv": (128, 24), "l1bv": (C,), "l1bo": (C,),
    "l1routerW": (C, 8), "l1routerb": (8,),
    "l1w1": (8, C, 4096), "l1b1t": (128, 256), "l1w2": (8, 4096, C), "l1b2": (8, C),
}


class LazyIns(dict):
    def __init__(self, nc):
        super().__init__()
        self.nc = nc

    def __missing__(self, name):
        ap = self.nc.dram_tensor(name, list(INPUT_SHAPES[name]), F32, kind="ExternalInput").ap()
        self[name] = ap
        return ap


def build_program(sub_ids):
    nc = bass.Bass("TRN2", target_bir_lowering=False)
    d = LazyIns(nc)
    yout = nc.dram_tensor("y", [T, C], F32, kind="ExternalOutput").ap()

    with contextlib.ExitStack() as st:
        t = {}

        def sb(name, cols, dtype):
            t[name] = st.enter_context(nc.sbuf_tensor(name, [128, cols], dtype))

        sb("XT", NTT * C, F32)
        sb("LNF", NFC * T, BF16)
        sb("LNT", 2 * C, BF16)
        sb("XSQ", C, F32)
        sb("STAT", 32, F32)
        sb("ONESB", 128, BF16)
        sb("ONESF", 128, F32)
        sb("IDENT", 128, BF16)
        sb("ROWS", 2176, F32)
        sb("BQKV", 24, F32)
        sb("B1T", 256, F32)
        sb("B2B", 1024, BF16)
        sb("WGATE0", 64, BF16)
        sb("WGATE1", 64, BF16)
        sb("WS0", 12288, BF16)
        sb("WS1", 12288, BF16)
        sb("SCR", 18688, BF16)
        sb("BSC", 16, F32)
        sb("NEGBT", 4, F32)
        sb("NB16", 4, BF16)
        sb("PBMXT", 32, F32)

        P = [st.enter_context(nc.psum_tensor(f"P{i}", [128, 1024], F32)) for i in range(4)]
        sem = {s: st.enter_context(nc.semaphore(s)) for s in SEMS}

        SCR = t["SCR"]
        t["ACT"] = SCR[:, 0:4096]
        t["SG0"] = SCR[:, 4096:5120].bitcast(F32)
        t["SG1"] = SCR[:, 5120:6144].bitcast(F32)
        t["GT"] = SCR[:, 6144:6400].bitcast(F32)
        t["GTB"] = SCR[:, 6400:6408]
        t["GSC"] = SCR[:, 6464:6592].bitcast(F32)
        t["GTT"] = SCR[:, 6656:8704]
        t["MSB"] = SCR[:, 0:5888]
        t["AWS"] = SCR[:, 5888:9984]
        t["TMPS"] = SCR[:, 5888:6912]
        t["VT"] = SCR[:, 9984:14144]
        t["EXPT"] = SCR[:, 14144:16192]
        t["QSQ"] = SCR[:, 14144:16192]
        t["RECB"] = SCR[:, 16192:18240].bitcast(F32)

        prog = Prog()
        bk = BankTrack(prog)
        XT, LNF, ACT = t["XT"], t["LNF"], t["ACT"]

        def TOP(fn, inc=None):
            return prog.op("t", fn, inc)

        def VOP(fn, inc=None):
            return prog.op("v", fn, inc)

        def SOP(fn, inc=None):
            return prog.op("s", fn, inc)

        def GDMA(fn, sm="DWI"):
            return prog.op("g", fn, sm, 16)

        def YDMA(fn):
            return prog.op("y", fn, "DA", 16)

        def BDMA(fn, sm="DB2"):
            return prog.op("s", fn, sm, 16)

        def emit_init():
            VOP(lambda v: v.memset(t["ONESB"][:], 1.0))
            VOP(lambda v: v.memset(t["ONESF"][:], 1.0))
            VOP(lambda v: v.memset(t["STAT"][:, 16:17], 1e-5))
            ev_id = GDMA(lambda g: g.dma_start(out=t["IDENT"][:], in_=d["ident"]))
            YDMA(lambda y: y.dma_start(
                out=XT[:].rearrange("p (tt c) -> p tt c", tt=NTT),
                in_=d["x"].rearrange("(tt p) c -> p tt c", p=128)))
            if any(s in (3, 5) for s in sub_ids):
                YDMA(lambda y: y.dma_start(out=t["ROWS"][0:1, 2048:2056], in_=d["l1routerb"][None, :]))
                YDMA(lambda y: y.dma_start(out=t["BQKV"][:], in_=d["l1bqkv"]))
                YDMA(lambda y: y.dma_start(out=t["B1T"][:], in_=d["l1b1t"]))
                GDMA(lambda g: g.dma_start(out=t["B2B"][0:8, :], in_=d["l1b2"]))
                GDMA(lambda g: g.dma_start(
                    out=t["WGATE1"][:].rearrange("p (ci e) -> p ci e", ci=8),
                    in_=d["l1routerW"].rearrange("(ci p) e -> p ci e", p=128)))
            if any(s in (0, 2) for s in sub_ids):
                GDMA(lambda g: g.dma_start(
                    out=t["WGATE0"][:].rearrange("p (ci e) -> p ci e", ci=8),
                    in_=d["l0wgate"].rearrange("(ci p) e -> p ci e", p=128)))
            attn_ls = [0 if s == 1 else 1 for s in sub_ids if s in (1, 4)]
            for L in sorted(set(attn_ls)):
                GDMA(lambda g, L=L: g.dma_start(
                    out=t["MSB"][0:16, 0:4095], in_=d["pbrows"][L]))
                prog.wait("v", ("DWI", prog.cnt["DWI"]))
                ev_r = VOP(lambda v: v.tensor_reduce(
                    out=t["MSB"][0:16, 5000:5001], in_=t["MSB"][0:16, 0:4095],
                    axis=AX.X, op=OP.max), inc="V")
                bk.pe_begin(0, (0,))
                prog.wait("t", ev_r)
                prog.wait("t", ("DWI", prog.cnt["DWI"]))
                ev_t = TOP(lambda tt_: tt_.transpose(
                    P[0].bitcast(BF16)[0:1, 0:16], t["MSB"][0:16, 5000:5001],
                    t["IDENT"][0:16, 0:16]), inc="T")
                bk.pe_done(0, (0,), ev_t)
                bk.rd_begin("v", 0, (0,))
                ev_v = VOP(lambda v, L=L: v.tensor_copy(
                    t["PBMXT"][0:1, L * 16:L * 16 + 16],
                    P[0].bitcast(BF16)[0:1, 0:16]), inc="V")
                bk.rd_done(0, (0,), ev_v)
                prog.wait("g", ev_v)

        def emit_ln():
            lnf_evs = []
            lnt_tread = [None, None]
            stat_sread = [None, None]
            xsq_vread = [None]
            for tt in range(NTT):
                xs = XT[:, tt * C:(tt + 1) * C]
                r = tt % 2
                stat = t["STAT"][:, r * 8:r * 8 + 8]
                prog.wait("v", stat_sread[r])
                VOP(lambda v, xs=xs, stat=stat: v.reduce_sum(out=stat[:, 0:1], in_=xs, axis=AX.X))
                prog.drain("v")
                ev0 = VOP(lambda v, stat=stat: v.tensor_scalar(
                    out=stat[:, 2:3], in0=stat[:, 0:1], scalar1=1.0 / C, scalar2=None,
                    op0=OP.mult), inc="V")
                prog.drain("v")
                VOP(lambda v, stat=stat: v.tensor_tensor(
                    out=stat[:, 4:5], in0=stat[:, 2:3], in1=stat[:, 2:3], op=OP.mult))
                prog.wait("s", xsq_vread[0])
                ev_sq = SOP(lambda s, xs=xs: s.activation(out=t["XSQ"][:], in_=xs, func=AF.Square), inc="S")
                prog.wait("v", ev_sq)
                ev_xv = VOP(lambda v, stat=stat: v.reduce_sum(out=stat[:, 1:2], in_=t["XSQ"][:], axis=AX.X), inc="V")
                xsq_vread[0] = ev_xv
                prog.drain("v")
                VOP(lambda v, stat=stat: v.tensor_scalar(
                    out=stat[:, 3:4], in0=stat[:, 1:2], scalar1=1.0 / C, scalar2=None, op0=OP.mult))
                prog.drain("v")
                ev_var = VOP(lambda v, stat=stat: v.tensor_tensor(
                    out=stat[:, 5:6], in0=stat[:, 3:4], in1=stat[:, 4:5], op=OP.subtract), inc="V")
                prog.wait("s", ev_var)
                SOP(lambda s, stat=stat: s.activation(
                    out=stat[:, 6:7], in_=stat[:, 5:6], func=AF.Ln, bias=t["STAT"][:, 16:17]))
                prog.drain("s")
                ev_rstd = SOP(lambda s, stat=stat: s.activation(
                    out=stat[:, 7:8], in_=stat[:, 6:7], func=AF.Exp, scale=-0.5), inc="S")
                stat_sread[r] = ev_rstd
                prog.wait("v", ev_rstd)
                lt = t["LNT"][:, r * C:(r + 1) * C]
                prog.wait("v", lnt_tread[r])
                ev_lnt = VOP(lambda v, xs=xs, stat=stat, lt=lt: v.tensor_scalar(
                    out=lt, in0=xs, scalar1=stat[:, 2:3], scalar2=stat[:, 7:8],
                    op0=OP.subtract, op1=OP.mult), inc="V")
                pi = tt % 2
                bk.pe_begin(pi, (0,))
                prog.wait("t", ev_lnt)
                pt = P[pi].bitcast(BF16)
                for fc in range(NFC):
                    last = fc == NFC - 1
                    TOP(lambda tt_, pt=pt, lt=lt, fc=fc: tt_.transpose(
                        pt[:, fc * 128:(fc + 1) * 128], lt[:, fc * 128:(fc + 1) * 128],
                        t["IDENT"][:]), inc="T" if last else None)
                ev_tr = ("T", prog.cnt["T"])
                lnt_tread[r] = ev_tr
                bk.pe_done(pi, (0,), ev_tr)
                bk.rd_begin("v", pi, (0,))
                dst = LNF[:].rearrange("p (fc tx) -> p fc tx", fc=NFC)[:, :, tt * 128:(tt + 1) * 128]
                ev_ev = VOP(lambda v, pt=pt, dst=dst: v.tensor_copy(
                    dst, pt[:, 0:1024].rearrange("p (fc q) -> p fc q", fc=8)), inc="V")
                bk.rd_done(pi, (0,), ev_ev)
                lnf_evs.append(ev_ev)
            return lnf_evs

        def emit_gates(L, lnf_evs):
            wg = t["WGATE0"] if L == 0 else t["WGATE1"]
            gts = t["GT"]
            ev_g = None
            gsc_sread = [None]
            gsc_vdone = [None]
            gtb_tread = [None]
            for tt in range(NTT):
                pi = 2 + tt % 2
                bk.pe_begin(pi, (0,))
                prog.wait("t", lnf_evs[tt])
                first = True
                if L == 1:
                    TOP(lambda tt_, pi=pi: tt_.matmul(
                        P[pi][:, 0:8], t["ONESF"][0:1, 0:128], t["ROWS"][0:1, 2048:2056],
                        start=True, stop=False))
                    first = False
                for ci in range(8):
                    last = ci == 7
                    TOP(lambda tt_, pi=pi, ci=ci, tt=tt, wg=wg, first=first, last=last: tt_.matmul(
                        P[pi][:, 0:8],
                        LNF[:, ci * T + tt * 128:ci * T + (tt + 1) * 128],
                        wg[:, ci * 8:(ci + 1) * 8],
                        start=(first and ci == 0), stop=last), inc="T" if last else None)
                bk.pe_done(pi, (0,), ("T", prog.cnt["T"]))
                bk.rd_begin("v", pi, (0,))
                g = t["GSC"]
                prog.wait("v", gsc_sread[0])
                ev_m = VOP(lambda v, pi=pi: v.reduce_max(
                    out=g[:, 0:1], in_=P[pi][:, 0:8], axis=AX.X, negate=True), inc="V")
                bk.rd_done(pi, (0,), ev_m)
                bk.rd_begin("s", pi, (0,))
                prog.wait("s", ev_m)
                prog.wait("s", gsc_vdone[0])
                ev_e = SOP(lambda s, pi=pi: s.activation(
                    out=g[:, 8:16], in_=P[pi][:, 0:8], func=AF.Exp, bias=g[:, 0:1]), inc="S")
                gsc_sread[0] = ev_e
                bk.rd_done(pi, (0,), ev_e)
                prog.wait("v", ev_e)
                VOP(lambda v: v.reduce_sum(out=g[:, 1:2], in_=g[:, 8:16], axis=AX.X))
                prog.drain("v")
                VOP(lambda v: v.reciprocal(out=g[:, 2:3], in_=g[:, 1:2]))
                prog.drain("v")
                VOP(lambda v: v.tensor_scalar(
                    out=g[:, 16:24], in0=g[:, 8:16], scalar1=g[:, 2:3], scalar2=None, op0=OP.mult))
                prog.drain("v")
                pr = g[:, 16:24]
                if L == 0:
                    VOP(lambda v, pr=pr: v.reduce_max(out=g[:, 3:4], in_=pr, axis=AX.X))
                    prog.drain("v")
                    VOP(lambda v, pr=pr: v.tensor_scalar(
                        out=g[:, 24:32], in0=pr, scalar1=g[:, 3:4], scalar2=None, op0=OP.is_ge))
                    prog.drain("v")
                    VOP(lambda v, pr=pr: v.tensor_tensor(out=g[:, 32:40], in0=pr, in1=g[:, 24:32], op=OP.mult))
                    prog.drain("v")
                    VOP(lambda v, pr=pr: v.tensor_tensor(out=g[:, 40:48], in0=pr, in1=g[:, 32:40], op=OP.subtract))
                    prog.drain("v")
                    VOP(lambda v: v.reduce_max(out=g[:, 4:5], in_=g[:, 40:48], axis=AX.X))
                    prog.drain("v")
                    VOP(lambda v, pr=pr: v.tensor_scalar(
                        out=g[:, 48:56], in0=pr, scalar1=g[:, 4:5], scalar2=None, op0=OP.is_ge))
                    prog.drain("v")
                    VOP(lambda v, pr=pr: v.tensor_tensor(out=g[:, 56:64], in0=pr, in1=g[:, 48:56], op=OP.mult))
                    prog.drain("v")
                    VOP(lambda v: v.reduce_sum(out=g[:, 5:6], in_=g[:, 56:64], axis=AX.X))
                    prog.drain("v")
                    VOP(lambda v: v.tensor_scalar(out=g[:, 6:7], in0=g[:, 5:6], scalar1=1e-9, scalar2=None, op0=OP.max))
                    prog.drain("v")
                    VOP(lambda v: v.reciprocal(out=g[:, 7:8], in_=g[:, 6:7]))
                    prog.drain("v")
                    VOP(lambda v: v.tensor_scalar(out=g[:, 7:8], in0=g[:, 7:8], scalar1=0.5, scalar2=None, op0=OP.mult))
                    prog.drain("v")
                    ev_g = VOP(lambda v, tt=tt: v.tensor_scalar(
                        out=gts[:, tt * 8:(tt + 1) * 8], in0=g[:, 56:64],
                        scalar1=g[:, 7:8], scalar2=None, op0=OP.mult), inc="V")
                    gsc_vdone[0] = ev_g
                else:
                    VOP(lambda v, pr=pr: v.reduce_max(out=g[:, 3:4], in_=pr, axis=AX.X))
                    prog.drain("v")
                    VOP(lambda v, pr=pr: v.tensor_scalar(
                        out=g[:, 24:32], in0=pr, scalar1=g[:, 3:4], scalar2=None, op0=OP.is_ge))
                    prog.drain("v")
                    VOP(lambda v, pr=pr: v.tensor_tensor(out=g[:, 32:40], in0=pr, in1=g[:, 24:32], op=OP.mult))
                    prog.drain("v")
                    ev_g = VOP(lambda v, tt=tt: v.tensor_scalar(
                        out=gts[:, tt * 8:(tt + 1) * 8], in0=g[:, 32:40],
                        scalar1=0.5, scalar2=None, op0=OP.mult), inc="V")
                    gsc_vdone[0] = ev_g
                    prog.drain("v")
                    prog.wait("v", gtb_tread[0])
                    ev_c = VOP(lambda v, tt=tt: v.tensor_copy(
                        t["GTB"][:, 0:8], gts[:, tt * 8:(tt + 1) * 8]), inc="V")
                    bk.pe_begin(pi, (1,))
                    prog.wait("t", ev_c)
                    ev_tr = TOP(lambda tt_, pi=pi: tt_.transpose(
                        P[pi].bitcast(BF16)[0:8, 1024:1152], t["GTB"][:, 0:8],
                        t["IDENT"][:]), inc="T")
                    gtb_tread[0] = ev_tr
                    bk.pe_done(pi, (1,), ev_tr)
                    bk.rd_begin("v", pi, (1,))
                    ev_gt = VOP(lambda v, tt=tt, pi=pi: v.tensor_copy(
                        t["GTT"][0:8, tt * 128:(tt + 1) * 128],
                        P[pi].bitcast(BF16)[0:8, 1024:1152]), inc="V")
                    bk.rd_done(pi, (1,), ev_gt)
                    ev_g = ev_gt
            return ev_g

        def emit_ffn(L, ev_gates):
            E = 8
            NWIN, HCW = 8, 4
            ws = [t["WS0"], t["WS1"]]
            ws_consumed = [None, None]
            ws_ready = [None, None]

            def issue_window(idx):
                e, win = idx // NWIN, idx % NWIN
                buf = idx % 2
                sm = "DW%d" % buf
                w = ws[buf]
                prog.wait("g", ws_consumed[buf])
                if L == 0:
                    GDMA(lambda g, e=e, win=win, w=w: g.dma_start(
                        out=w[:, 0:4096].rearrange("p (ci q) -> p ci q", ci=8),
                        in_=d["l0wgu"][e, :, win * 512:(win + 1) * 512]
                        .rearrange("(ci p) q -> p ci q", p=128)), sm)
                    GDMA(lambda g, e=e, win=win, w=w: g.dma_start(
                        out=w[:, 4096:8192].rearrange("p (ci q) -> p ci q", ci=8),
                        in_=d["l0wgu"][e, :, 4096 + win * 512:4096 + (win + 1) * 512]
                        .rearrange("(ci p) q -> p ci q", p=128)), sm)
                    ev = GDMA(lambda g, e=e, win=win, w=w: g.dma_start(
                        out=w[:, 8192:12288].rearrange("p (hc q) -> p hc q", hc=4),
                        in_=d["l0wdown"][e, win * 512:(win + 1) * 512, :]
                        .rearrange("(hc p) q -> p hc q", p=128)), sm)
                else:
                    GDMA(lambda g, e=e, win=win, w=w: g.dma_start(
                        out=w[:, 0:4096].rearrange("p (ci q) -> p ci q", ci=8),
                        in_=d["l1w1"][e, :, win * 512:(win + 1) * 512]
                        .rearrange("(ci p) q -> p ci q", p=128)), sm)
                    ev = GDMA(lambda g, e=e, win=win, w=w: g.dma_start(
                        out=w[:, 4096:8192].rearrange("p (hc q) -> p hc q", hc=4),
                        in_=d["l1w2"][e, win * 512:(win + 1) * 512, :]
                        .rearrange("(hc p) q -> p hc q", p=128)), sm)
                ws_ready[buf] = ev

            evict_ev = [None] * NTT
            if L == 1:
                prog.wait("t", ev_gates)
                for grp in range(8):
                    for i in range(2):
                        tt = grp * 2 + i
                        bk.pe_begin(i, (0, 1))
                        for dh in range(2):
                            last = dh == 1
                            TOP(lambda tt_, i=i, tt=tt, dh=dh: tt_.matmul(
                                P[i][:, dh * 512:(dh + 1) * 512],
                                t["GTT"][0:8, tt * 128:(tt + 1) * 128],
                                t["B2B"][0:8, dh * 512:(dh + 1) * 512],
                                start=True, stop=True), inc="T" if last else None)
                        bk.pe_done(i, (0, 1), ("T", prog.cnt["T"]))
                        bk.rd_begin("v", i, (0, 1))
                        ev_e = VOP(lambda v, i=i, tt=tt: v.tensor_tensor(
                            out=XT[:, tt * C:(tt + 1) * C], in0=P[i][:, 0:1024],
                            in1=XT[:, tt * C:(tt + 1) * C], op=OP.add), inc="V")
                        evict_ev[tt] = ev_e
                        bk.rd_done(i, (0, 1), ev_e)

            issue_window(0)
            act_read = [None] * 8
            sg_read = [None, None]
            for e in range(E):
                for win in range(NWIN):
                    idx = e * NWIN + win
                    buf = idx % 2
                    if idx + 1 < E * NWIN:
                        issue_window(idx + 1)
                    w = ws[buf]
                    prog.wait("t", ws_ready[buf])
                    for t4 in range(4):
                        for hc in range(HCW):
                            rot = hc % 2
                            pi = 2 + rot
                            if L == 0:
                                bk.pe_begin(pi, (0,))
                                for ci in range(8):
                                    last = ci == 7
                                    TOP(lambda tt_, w=w, ci=ci, hc=hc, t4=t4, pi=pi, last=last: tt_.matmul(
                                        P[pi][:, 0:512],
                                        w[:, ci * 512 + hc * 128:ci * 512 + (hc + 1) * 128],
                                        LNF[:, ci * T + t4 * 512:ci * T + (t4 + 1) * 512],
                                        start=(ci == 0), stop=last), inc="T" if last else None)
                                bk.pe_done(pi, (0,), ("T", prog.cnt["T"]))
                                bk.pe_begin(pi, (1,))
                                for ci in range(8):
                                    last = ci == 7
                                    TOP(lambda tt_, w=w, ci=ci, hc=hc, t4=t4, pi=pi, last=last: tt_.matmul(
                                        P[pi][:, 512:1024],
                                        w[:, 4096 + ci * 512 + hc * 128:4096 + ci * 512 + (hc + 1) * 128],
                                        LNF[:, ci * T + t4 * 512:ci * T + (t4 + 1) * 512],
                                        start=(ci == 0), stop=last), inc="T" if last else None)
                                bk.pe_done(pi, (1,), ("T", prog.cnt["T"]))
                                bk.rd_begin("s", pi, (0,))
                                prog.wait("s", sg_read[rot])
                                sg = t["SG%d" % rot]
                                ev_sg = SOP(lambda s, pi=pi, sg=sg: s.activation(
                                    out=sg[:], in_=P[pi][:, 0:512], func=AF.Silu), inc="S")
                                bk.rd_done(pi, (0,), ev_sg)
                                bk.rd_begin("v", pi, (1,))
                                prog.wait("v", ev_sg)
                                prog.wait("v", act_read[hc])
                                ev_act = VOP(lambda v, pi=pi, sg=sg, hc=hc: v.tensor_tensor(
                                    out=ACT[:, hc * 512:(hc + 1) * 512],
                                    in0=P[pi][:, 512:1024], in1=sg[:], op=OP.mult), inc="V")
                                sg_read[rot] = ev_act
                                bk.rd_done(pi, (1,), ev_act)
                                act_ready_ev = ev_act
                            else:
                                bk.pe_begin(pi, (0,))
                                for ci in range(8):
                                    last = ci == 7
                                    TOP(lambda tt_, w=w, ci=ci, hc=hc, t4=t4, pi=pi, last=last: tt_.matmul(
                                        P[pi][:, 0:512],
                                        w[:, ci * 512 + hc * 128:ci * 512 + (hc + 1) * 128],
                                        LNF[:, ci * T + t4 * 512:ci * T + (t4 + 1) * 512],
                                        start=(ci == 0), stop=last), inc="T" if last else None)
                                bk.pe_done(pi, (0,), ("T", prog.cnt["T"]))
                                bk.rd_begin("s", pi, (0,))
                                prog.wait("s", act_read[hc])
                                hcg = win * HCW + hc
                                ev_act = SOP(lambda s, pi=pi, hc=hc, e=e, hcg=hcg: s.activation(
                                    out=ACT[:, hc * 512:(hc + 1) * 512],
                                    in_=P[pi][:, 0:512], func=AF.Relu,
                                    bias=t["B1T"][:, e * 32 + hcg:e * 32 + hcg + 1]), inc="S")
                                bk.rd_done(pi, (0,), ev_act)
                                act_ready_ev = ev_act
                            if hc == HCW - 1:
                                last_act_ev = act_ready_ev
                        prog.wait("t", last_act_ev)
                        for half in range(2):
                            bk.pe_begin(0, (0, 1))
                            bk.pe_begin(1, (0, 1))
                            for hc in range(HCW):
                                for i in range(2):
                                    ttl = half * 2 + i
                                    for dh in range(2):
                                        last = hc == HCW - 1 and i == 1 and dh == 1
                                        wdo = 8192 if L == 0 else 4096
                                        TOP(lambda tt_, w=w, hc=hc, i=i, ttl=ttl, dh=dh, wdo=wdo: tt_.matmul(
                                            P[i][:, dh * 512:(dh + 1) * 512],
                                            ACT[:, hc * 512 + ttl * 128:hc * 512 + (ttl + 1) * 128],
                                            w[:, wdo + hc * 1024 + dh * 512:wdo + hc * 1024 + (dh + 1) * 512],
                                            start=(hc == 0), stop=(hc == HCW - 1)),
                                            inc="T" if last else None)
                            ev_dn = ("T", prog.cnt["T"])
                            bk.pe_done(0, (0, 1), ev_dn)
                            bk.pe_done(1, (0, 1), ev_dn)
                            if half == 1:
                                for hc in range(HCW):
                                    act_read[hc] = ev_dn
                            prog.wait("v", ev_gates)
                            for i in range(2):
                                tt = t4 * 4 + half * 2 + i
                                bk.rd_begin("v", i, (0, 1))
                                prog.wait("v", evict_ev[tt])
                                ev_e = VOP(lambda v, i=i, tt=tt, e=e: v.scalar_tensor_tensor(
                                    out=XT[:, tt * C:(tt + 1) * C], in0=P[i][:, 0:1024],
                                    scalar=t["GT"][:, tt * 8 + e:tt * 8 + e + 1],
                                    in1=XT[:, tt * C:(tt + 1) * C],
                                    op0=OP.mult, op1=OP.add), inc="V")
                                evict_ev[tt] = ev_e
                                bk.rd_done(i, (0, 1), ev_e)
                    ws_consumed[buf] = ("T", prog.cnt["T"])

        def emit_attn(L, lnf_evs):
            AWS, MSB, VT, EXPT, QSQ, RECB = t["AWS"], t["MSB"], t["VT"], t["EXPT"], t["QSQ"], t["RECB"]
            BSC, BSCI = t["BSC"], t["BSC"].bitcast(I32)
            wq = d["l0wq_t"] if L == 0 else d["l1wq_t"]
            wo = d["l0wproj_t"] if L == 0 else d["l1wo_t"]
            def OUTC(fc):
                return t["WS0"][:, fc * 2048:(fc + 1) * 2048] if fc < 6 else t["WS1"][:, (fc - 6) * 2048:(fc - 5) * 2048]
            QW = t["WS1"][:, 4096:8192]
            KW = t["WS1"][:, 8192:12288]

            YDMA(lambda y: y.dma_start(
                out=t["ROWS"][0:1, 0:1024],
                in_=(d["l0bproj"] if L == 0 else d["l1bo"])[None, :]))
            if L == 1:
                YDMA(lambda y: y.dma_start(out=t["ROWS"][0:1, 1024:2048], in_=d["l1bv"][None, :]))
            ev_rows = ("DA", prog.cnt["DA"])
            ev_bv = ev_rows if L == 1 else None

            aws_tread = [None, None]
            tmps_dread = None
            wave_tread = None
            negbt_sread = None
            nb16_tread = [None]
            expt_tread = [None, None]
            msb_vread = None
            out_evs = []

            def load_aws(buf, col0, src_ap, ncols):
                prog.wait("g", aws_tread[buf])
                if buf == 0:
                    prog.wait("g", tmps_dread)
                return GDMA(lambda g, col0=col0, src_ap=src_ap, ncols=ncols: g.dma_start(
                    out=AWS[:, buf * 2048 + col0:buf * 2048 + col0 + ncols]
                    .rearrange("p (ci q) -> p ci q", ci=8),
                    in_=src_ap), "DW%d" % buf)

            for w in range(4):
                if L == 1:
                    ev_q = load_aws(0, 0, wq[:, w * 256:(w + 1) * 256].rearrange("(ci p) q -> p ci q", p=128), 2048)
                    ev_k = load_aws(1, 0, d["l1wk_t"][:, w * 256:(w + 1) * 256].rearrange("(ci p) q -> p ci q", p=128), 2048)
                else:
                    ev_q = load_aws(0, 0, wq[:, w * 256:(w + 1) * 256].rearrange("(ci p) q -> p ci q", p=128), 2048)
                    ev_k = load_aws(1, 0, d["l0wkv_t"][:, 0:256].rearrange("(ci p) q -> p ci q", p=128), 2048) if w == 0 else None
                prog.waitall("t", lnf_evs)
                prog.wait("v", wave_tread)
                prog.wait("t", ev_q)
                for pc in range(2):
                    qc = 2 * w + pc
                    for tq in range(4):
                        pi, half = pc, tq % 2
                        bk.pe_begin(pi, (half,))
                        for ci in range(8):
                            last = ci == 7
                            TOP(lambda tt_, pi=pi, half=half, ci=ci, pc=pc, tq=tq, last=last: tt_.matmul(
                                P[pi][:, half * 512:(half + 1) * 512],
                                AWS[:, ci * 256 + pc * 128:ci * 256 + (pc + 1) * 128],
                                LNF[:, ci * T + tq * 512:ci * T + (tq + 1) * 512],
                                start=(ci == 0), stop=last), inc="T" if last else None)
                        bk.pe_done(pi, (half,), ("T", prog.cnt["T"]))
                        bk.rd_begin("v", pi, (half,))
                        if L == 1:
                            ev_e = VOP(lambda v, pi=pi, half=half, pc=pc, tq=tq, qc=qc: v.tensor_scalar(
                                out=QW[:, pc * 2048 + tq * 512:pc * 2048 + (tq + 1) * 512],
                                in0=P[pi][:, half * 512:(half + 1) * 512],
                                scalar1=t["BQKV"][:, qc:qc + 1], scalar2=0.125,
                                op0=OP.add, op1=OP.mult), inc="V")
                        else:
                            ev_e = VOP(lambda v, pi=pi, half=half, pc=pc, tq=tq: v.tensor_scalar(
                                out=QW[:, pc * 2048 + tq * 512:pc * 2048 + (tq + 1) * 512],
                                in0=P[pi][:, half * 512:(half + 1) * 512],
                                scalar1=0.125, scalar2=None, op0=OP.mult), inc="V")
                        bk.rd_done(pi, (half,), ev_e)
                ev_qprod = ("T", prog.cnt["T"])
                aws_tread[0] = ev_qprod
                if L == 1 or w == 0:
                    prog.wait("t", ev_k)
                    for pc in range(2):
                        qc = 2 * w + pc
                        for tq in range(4):
                            pi, half = pc, tq % 2
                            bk.pe_begin(pi, (half,))
                            for ci in range(8):
                                last = ci == 7
                                TOP(lambda tt_, pi=pi, half=half, ci=ci, pc=pc, tq=tq, last=last: tt_.matmul(
                                    P[pi][:, half * 512:(half + 1) * 512],
                                    AWS[:, 2048 + ci * 256 + pc * 128:2048 + ci * 256 + (pc + 1) * 128],
                                    LNF[:, ci * T + tq * 512:ci * T + (tq + 1) * 512],
                                    start=(ci == 0), stop=last), inc="T" if last else None)
                            bk.pe_done(pi, (half,), ("T", prog.cnt["T"]))
                            bk.rd_begin("v", pi, (half,))
                            if L == 1:
                                ev_e = VOP(lambda v, pi=pi, half=half, pc=pc, tq=tq, qc=qc: v.tensor_scalar(
                                    out=KW[:, pc * 2048 + tq * 512:pc * 2048 + (tq + 1) * 512],
                                    in0=P[pi][:, half * 512:(half + 1) * 512],
                                    scalar1=t["BQKV"][:, 8 + qc:8 + qc + 1], scalar2=None,
                                    op0=OP.add), inc="V")
                            else:
                                ev_e = VOP(lambda v, pi=pi, half=half, pc=pc, tq=tq: v.tensor_copy(
                                    KW[:, pc * 2048 + tq * 512:pc * 2048 + (tq + 1) * 512],
                                    P[pi][:, half * 512:(half + 1) * 512]), inc="V")
                            bk.rd_done(pi, (half,), ev_e)
                    aws_tread[1] = ("T", prog.cnt["T"])
                if L == 1:
                    ev_v = load_aws(0, 0, d["l1wv_t"][:, w * 256:(w + 1) * 256].rearrange("(ci p) q -> p ci q", p=128), 2048)
                    dv = 256
                elif w == 0:
                    ev_v = load_aws(1, 0, d["l0wkv_t"][:, 256:512].rearrange("(ci p) q -> p ci q", p=128), 2048)
                    dv = 256
                else:
                    ev_v = None
                if ev_v is not None:
                    vbuf = 0 if L == 1 else 1
                    if ev_bv is not None:
                        prog.wait("t", ev_bv)
                    prog.wait("v", wave_tread)
                    ev_ones = VOP(lambda v: v.memset(
                        VT.rearrange("p (tt s c) -> p tt s c", tt=16, s=4)[:, :, :, 64:65], 1.0), inc="V")
                    prog.wait("t", ev_v)
                    for tt in range(NTT):
                        pi, half = tt % 2, (tt // 2) % 2
                        bk.pe_begin(pi, (half,))
                        first = True
                        if L == 1:
                            TOP(lambda tt_, pi=pi, half=half, w=w: tt_.matmul(
                                P[pi][:, half * 512:half * 512 + 256],
                                t["ONESF"][0:1, 0:128],
                                t["ROWS"][0:1, 1024 + w * 256:1024 + (w + 1) * 256],
                                start=True, stop=False))
                            first = False
                        for ci in range(8):
                            last = ci == 7
                            TOP(lambda tt_, pi=pi, half=half, ci=ci, tt=tt, first=first, vbuf=vbuf, last=last: tt_.matmul(
                                P[pi][:, half * 512:half * 512 + 256],
                                LNF[:, ci * T + tt * 128:ci * T + (tt + 1) * 128],
                                AWS[:, vbuf * 2048 + ci * 256:vbuf * 2048 + (ci + 1) * 256],
                                start=(first and ci == 0), stop=last), inc="T" if last else None)
                        bk.pe_done(pi, (half,), ("T", prog.cnt["T"]))
                        bk.rd_begin("v", pi, (half,))
                        prog.wait("v", ev_ones)
                        dst = VT.rearrange("p (tt2 s c) -> p tt2 s c", tt2=16, s=4)[:, tt, 0:4, 0:64]
                        ev_e = VOP(lambda v, pi=pi, half=half, dst=dst: v.tensor_copy(
                            dst, P[pi][:, half * 512:half * 512 + 256]
                            .rearrange("p (s c) -> p s c", c=64)), inc="V")
                        bk.rd_done(pi, (half,), ev_e)
                    ev_vprod = ("T", prog.cnt["T"])
                    aws_tread[vbuf] = ev_vprod
                ev_qkv_v = ("V", prog.cnt["V"])

                for pc in range(2):
                    hA = 4 * w + 2 * pc
                    hB = hA + 1
                    oc = 2 * w + pc
                    prog.wait("t", ev_qkv_v)
                    prog.waitall("s", [expt_tread[0], expt_tread[1], ev_qkv_v])
                    ev_sq = SOP(lambda s, pc=pc: s.activation(
                        out=QSQ[:, 0:2048], in_=QW[:, pc * 2048:(pc + 1) * 2048], func=AF.Square), inc="S")
                    bk.pe_begin(2, (0, 1))
                    bk.pe_begin(3, (0, 1))
                    prog.wait("t", ev_sq)
                    for j in range(4):
                        pi = 2 + j // 2
                        TOP(lambda tt_, pi=pi, j=j: tt_.matmul(
                            P[pi][0:1, (j % 2) * 512:(j % 2 + 1) * 512],
                            t["ONESB"][:, 0:1],
                            QSQ[:, j * 512:(j + 1) * 512],
                            start=True, stop=True), inc="T" if j == 3 else None)
                    ev_qn = ("T", prog.cnt["T"])
                    bk.pe_done(2, (0, 1), ev_qn)
                    bk.pe_done(3, (0, 1), ev_qn)
                    bk.rd_begin("v", 2, (0, 1))
                    bk.rd_begin("v", 3, (0, 1))
                    VOP(lambda v: v.reduce_max(out=BSC[0:1, 0:1], in_=P[2][0:1, 0:1024], axis=AX.X))
                    ev_r = VOP(lambda v: v.reduce_max(out=BSC[0:1, 1:2], in_=P[3][0:1, 0:1024], axis=AX.X), inc="V")
                    bk.rd_done(2, (0, 1), ev_r)
                    bk.rd_done(3, (0, 1), ev_r)
                    prog.drain("v")
                    VOP(lambda v: v.tensor_tensor(out=BSC[0:1, 2:3], in0=BSC[0:1, 0:1], in1=BSC[0:1, 1:2], op=OP.max))
                    prog.wait("s", ev_qn)
                    kwc = pc if L == 1 else (2 * w + pc) % 2
                    ev_sqk = SOP(lambda s, kwc=kwc: s.activation(
                        out=QSQ[:, 0:2048], in_=KW[:, kwc * 2048:(kwc + 1) * 2048],
                        func=AF.Square), inc="S")
                    bk.pe_begin(2, (0, 1))
                    bk.pe_begin(3, (0, 1))
                    prog.wait("t", ev_sqk)
                    for j in range(4):
                        pi = 2 + j // 2
                        TOP(lambda tt_, pi=pi, j=j: tt_.matmul(
                            P[pi][0:1, (j % 2) * 512:(j % 2 + 1) * 512],
                            t["ONESB"][:, 0:1],
                            QSQ[:, j * 512:(j + 1) * 512],
                            start=True, stop=True), inc="T" if j == 3 else None)
                    ev_kn = ("T", prog.cnt["T"])
                    bk.pe_done(2, (0, 1), ev_kn)
                    bk.pe_done(3, (0, 1), ev_kn)
                    bk.rd_begin("v", 2, (0, 1))
                    bk.rd_begin("v", 3, (0, 1))
                    VOP(lambda v: v.reduce_max(out=BSC[0:1, 3:4], in_=P[2][0:1, 0:1024], axis=AX.X))
                    ev_r2 = VOP(lambda v: v.reduce_max(out=BSC[0:1, 4:5], in_=P[3][0:1, 0:1024], axis=AX.X), inc="V")
                    bk.rd_done(2, (0, 1), ev_r2)
                    bk.rd_done(3, (0, 1), ev_r2)
                    prog.drain("v")
                    VOP(lambda v: v.tensor_tensor(out=BSC[0:1, 5:6], in0=BSC[0:1, 3:4], in1=BSC[0:1, 4:5], op=OP.max))
                    prog.drain("v")
                    VOP(lambda v: v.tensor_tensor(out=BSC[0:1, 6:7], in0=BSC[0:1, 2:3], in1=BSC[0:1, 5:6], op=OP.mult))
                    prog.drain("v")
                    VOP(lambda v: v.tensor_scalar(out=BSCI[0:1, 7:8], in0=BSCI[0:1, 6:7], scalar1=1, scalar2=None, op0=OP.arith_shift_right))
                    prog.drain("v")
                    VOP(lambda v: v.tensor_scalar(out=BSCI[0:1, 8:9], in0=BSCI[0:1, 7:8], scalar1=0x1FC00000, scalar2=None, op0=OP.add))
                    prog.drain("v")
                    VOP(lambda v: v.tensor_scalar(out=BSC[0:1, 9:10], in0=BSC[0:1, 8:9], scalar1=1.06, scalar2=None, op0=OP.mult))
                    VOP(lambda v, hA=hA, hB=hB, L=L: v.tensor_tensor(
                        out=BSC[0:1, 10:11], in0=t["PBMXT"][0:1, L * 16 + hA:L * 16 + hA + 1],
                        in1=t["PBMXT"][0:1, L * 16 + hB:L * 16 + hB + 1], op=OP.max))
                    prog.drain("v")
                    VOP(lambda v: v.tensor_tensor(out=BSC[0:1, 11:12], in0=BSC[0:1, 9:10], in1=BSC[0:1, 10:11], op=OP.add))
                    prog.drain("v")
                    VOP(lambda v: v.tensor_scalar(out=BSC[0:1, 12:13], in0=BSC[0:1, 11:12], scalar1=-1.0, scalar2=None, op0=OP.mult))
                    prog.drain("v")
                    prog.wait("v", nb16_tread[0])
                    ev_nb16 = VOP(lambda v: v.tensor_copy(t["NB16"][0:1, 0:1], BSC[0:1, 12:13]), inc="V")
                    bk.pe_begin(2, (0,))
                    prog.wait("t", ev_nb16)
                    ev_bb = TOP(lambda tt_: tt_.matmul(
                        P[2][:, 0:1], t["ONESB"][0:1, 0:128], t["NB16"][0:1, 0:1],
                        start=True, stop=True), inc="T")
                    nb16_tread[0] = ev_bb
                    bk.pe_done(2, (0,), ev_bb)
                    bk.rd_begin("v", 2, (0,))
                    prog.wait("v", negbt_sread)
                    ev_negbt = VOP(lambda v: v.tensor_copy(t["NEGBT"][:, 0:1], P[2][:, 0:1]), inc="V")
                    bk.rd_done(2, (0,), ev_negbt)

                    for qh in range(2):
                        base = 127 + 1024 * qh
                        prog.wait("g", msb_vread)
                        GDMA(lambda g, L=L, hA=hA, base=base: g.dma_start(
                            out=MSB[:, 0:2944], in_=d["masksh"][L, hA][:, base:base + 2944]), "DWM")
                        ev_msb = GDMA(lambda g, L=L, hB=hB, base=base: g.dma_start(
                            out=MSB[:, 2944:5888], in_=d["masksh"][L, hB][:, base:base + 2944]), "DWM")
                        ev_scores = [None, None]
                        ev_avT = [None, None]
                        for sc in range(16):
                            c0w = 1920 - 128 * sc
                            for side in range(2):
                                pi = side
                                h = hA if side == 0 else hB
                                rows = slice(0, 64) if side == 0 else slice(64, 128)
                                bk.pe_begin(pi, (0, 1))
                                prog.wait("t", ev_scores[side])
                                kw_pc = pc if L == 1 else (2 * w + pc) % 2
                                for jj in range(2):
                                    last = jj == 1
                                    TOP(lambda tt_, pi=pi, rows=rows, sc=sc, jj=jj, qh=qh, kw_pc=kw_pc, pc=pc: tt_.matmul(
                                        P[pi][:, jj * 512:(jj + 1) * 512],
                                        KW[rows, kw_pc * 2048 + sc * 128:kw_pc * 2048 + (sc + 1) * 128],
                                        QW[rows, pc * 2048 + qh * 1024 + jj * 512:pc * 2048 + qh * 1024 + (jj + 1) * 512],
                                        start=True, stop=True), inc="T" if last else None)
                                ev_mm = ("T", prog.cnt["T"])
                                bk.pe_done(pi, (0, 1), ev_mm)
                                bk.rd_begin("v", pi, (0, 1))
                                prog.wait("v", ev_msb)
                                ev_mask = VOP(lambda v, pi=pi, side=side, c0w=c0w: v.tensor_tensor(
                                    out=P[pi][:, 0:1024], in0=P[pi][:, 0:1024],
                                    in1=MSB[:, side * 2944 + c0w:side * 2944 + c0w + 1024],
                                    op=OP.add), inc="V")
                                msb_vread = ev_mask
                                prog.wait("s", ev_mask)
                                prog.wait("s", ev_negbt)
                                prog.wait("s", expt_tread[side])
                                ev_exp = SOP(lambda s, pi=pi, side=side: s.activation(
                                    out=EXPT[:, side * 1024:(side + 1) * 1024], in_=P[pi][:, 0:1024],
                                    func=AF.Exp, bias=t["NEGBT"][:, 0:1]), inc="S")
                                negbt_sread = ev_exp
                                ev_scores[side] = ev_exp
                                bk.rd_done(pi, (0, 1), ev_exp)
                                pav = 2 + side
                                if sc == 0:
                                    bk.pe_begin(pav, (0, 1))
                                prog.wait("t", ev_exp)
                                slot = (h - 4 * w) if L == 1 else (h % 4)
                                for jj in range(2):
                                    last = jj == 1
                                    TOP(lambda tt_, pav=pav, sc=sc, slot=slot, side=side, jj=jj: tt_.matmul(
                                        P[pav][0:65, jj * 512:(jj + 1) * 512],
                                        VT[:, sc * 260 + slot * 65:sc * 260 + slot * 65 + 65],
                                        EXPT[:, side * 1024 + jj * 512:side * 1024 + (jj + 1) * 512],
                                        start=(sc == 0), stop=(sc == 15)),
                                        inc="T" if last else None)
                                ev_av = ("T", prog.cnt["T"])
                                expt_tread[side] = ev_av
                                if sc == 15:
                                    bk.pe_done(pav, (0, 1), ev_av)
                        for side in range(2):
                            pav = 2 + side
                            psc = side
                            bk.rd_begin("v", pav, (0, 1))
                            ev_rec = VOP(lambda v, pav=pav: v.reciprocal(
                                out=RECB[64:65, 0:1024], in_=P[pav][64:65, 0:1024]), inc="V")
                            bk.rd_done(pav, (0, 1), ev_rec)
                            bk.pe_begin(psc, (0, 1))
                            prog.wait("t", ev_rec)
                            for jj in range(2):
                                last = jj == 1
                                TOP(lambda tt_, psc=psc, jj=jj: tt_.matmul(
                                    P[psc][:, jj * 512:(jj + 1) * 512],
                                    t["ONESF"][64:65, 0:128],
                                    RECB[64:65, jj * 512:(jj + 1) * 512],
                                    start=True, stop=True), inc="T" if last else None)
                            ev_rb = ("T", prog.cnt["T"])
                            bk.pe_done(psc, (0, 1), ev_rb)
                            bk.rd_begin("v", psc, (0, 1))
                            ev_rc = VOP(lambda v, psc=psc: v.tensor_copy(
                                RECB[0:128, 0:1024], P[psc][:, 0:1024]), inc="V")
                            bk.rd_done(psc, (0, 1), ev_rc)
                            ev_scores[side] = None
                            prog.wait("v", ev_rc)
                            bk.rd_begin("v", pav, (0, 1))
                            if side == 0:
                                ev_ev = VOP(lambda v, pav=pav, oc=oc, qh=qh: v.tensor_tensor(
                                    out=OUTC(oc)[0:64, qh * 1024:(qh + 1) * 1024],
                                    in0=P[pav][0:64, 0:1024], in1=RECB[0:64, 0:1024],
                                    op=OP.mult), inc="V")
                                out_evs.append(ev_ev)
                            else:
                                prog.wait("v", tmps_dread)
                                ev_ev = VOP(lambda v, pav=pav: v.tensor_tensor(
                                    out=t["TMPS"][0:64, 0:1024],
                                    in0=P[pav][0:64, 0:1024], in1=RECB[0:64, 0:1024],
                                    op=OP.mult), inc="V")
                                prog.wait("s", ev_ev)
                                ev_sh = BDMA(lambda s, oc=oc, qh=qh: s.dma_start(
                                    out=OUTC(oc)[64:128, qh * 1024:(qh + 1) * 1024],
                                    in_=t["TMPS"][0:64, 0:1024]))
                                tmps_dread = ev_sh
                                out_evs.append(ev_sh)
                            bk.rd_done(pav, (0, 1), ev_ev)
                wave_tread = ("T", prog.cnt["T"])

            prog.waitall("t", out_evs)
            prog.wait("t", ev_rows)
            for eh in range(2):
                prog.wait("g", aws_tread[0])
                prog.wait("g", aws_tread[1])
                prog.wait("g", tmps_dread)
                ev_w = GDMA(lambda g, eh=eh: g.dma_start(
                    out=AWS[:, 0:4096].rearrange("p (ci q) -> p ci q", ci=8),
                    in_=wo[:, eh * 512:(eh + 1) * 512].rearrange("(ci p) q -> p ci q", p=128)), "DW0")
                prog.wait("t", ev_w)
                for tt in range(NTT):
                    pi, half = tt % 2, (tt // 2) % 2
                    bk.pe_begin(pi, (half,))
                    TOP(lambda tt_, pi=pi, half=half, eh=eh: tt_.matmul(
                        P[pi][:, half * 512:(half + 1) * 512],
                        t["ONESF"][0:1, 0:128],
                        t["ROWS"][0:1, eh * 512:(eh + 1) * 512],
                        start=True, stop=False))
                    for fc in range(8):
                        last = fc == 7
                        TOP(lambda tt_, pi=pi, half=half, fc=fc, tt=tt, last=last: tt_.matmul(
                            P[pi][:, half * 512:(half + 1) * 512],
                            OUTC(fc)[:, tt * 128:(tt + 1) * 128],
                            AWS[:, fc * 512:(fc + 1) * 512],
                            start=False, stop=last), inc="T" if last else None)
                    bk.pe_done(pi, (half,), ("T", prog.cnt["T"]))
                    bk.rd_begin("v", pi, (half,))
                    ev_e = VOP(lambda v, pi=pi, half=half, tt=tt, eh=eh: v.tensor_tensor(
                        out=XT[:, tt * C + eh * 512:tt * C + (eh + 1) * 512],
                        in0=P[pi][:, half * 512:(half + 1) * 512],
                        in1=XT[:, tt * C + eh * 512:tt * C + (eh + 1) * 512],
                        op=OP.add), inc="V")
                    bk.rd_done(pi, (half,), ev_e)
                aws_tread[0] = ("T", prog.cnt["T"])
                aws_tread[1] = aws_tread[0]

        emit_init()
        prog.barrier()
        for sub in sub_ids:
            lnf_evs = emit_ln()
            if sub in (0, 2):
                ev_g = emit_gates(0, lnf_evs)
                emit_ffn(0, ev_g)
            elif sub in (3, 5):
                ev_g = emit_gates(1, lnf_evs)
                emit_ffn(1, ev_g)
            else:
                emit_attn(0 if sub == 1 else 1, lnf_evs)
            prog.barrier()
        ev_y = YDMA(lambda y: y.dma_start(
            out=yout.rearrange("(tt p) c -> p tt c", p=128),
            in_=XT[:].rearrange("p (tt c) -> p tt c", tt=NTT)))
        prog.wait("y", ev_y)
        for nm in DEBUG_TAPS:
            ap = t[nm][:] if hasattr(t[nm], '__enter__') or 'SBTensor' in type(t[nm]).__name__ else t[nm]
            dt_out = nc.dram_tensor("dbg_" + nm, [128, ap.shape[1]], F32, kind="ExternalOutput").ap()
            if ap.dtype == F32:
                ev_d = YDMA(lambda y, ap=ap, dt_out=dt_out: y.dma_start(out=dt_out, in_=ap))
                prog.wait("y", ev_d)
            else:
                ev_d = prog.op("g", lambda g, ap=ap, dt_out=dt_out: g.dma_start(out=dt_out, in_=ap), "DWI", 16)
                prog.wait("g", ev_d)

        with nc.Block() as block:
            def mk(engname):
                def _(eng):
                    for item in prog.ir[engname]:
                        if item[0] == "wait":
                            eng.wait_ge(sem[item[1]], item[2])
                        elif item[0] == "drain":
                            eng.drain()
                        else:
                            _, fn, inc, k = item
                            r = fn(eng)
                            if inc is not None:
                                r.then_inc(sem[inc], k)
                return _
            block.tensor(mk("t"))
            block.vector(mk("v"))
            block.scalar(mk("s"))
            block.gpsimd(mk("g"))
            block.sync(mk("y"))
    nc._lazy_input_names = list(d.keys())
    return nc


def _host_prepare(inputs):
    f = np.float32
    out = {}
    out["ident"] = np.eye(128, dtype=f)
    n_head = 16
    start = 2.0 ** (-(2.0 ** (-(np.log2(n_head) - 3.0))))
    slopes = (start ** np.arange(1, n_head + 1)).astype(np.float64)
    idx = np.arange(4095, dtype=np.float64)
    rel = 2047.0 - idx
    row = np.minimum(rel, 0.0)
    pb0 = (slopes[:, None] * row[None, :] * (64.0 ** -0.5)).astype(f)
    pb1 = np.asarray(inputs["L1_attn_pos_bias"], dtype=f)
    pbrows = np.stack([pb0, pb1], 0)
    pad = np.zeros((2, 16, 127 + 4224), dtype=f)
    pad[:, :, 127:127 + 4095] = pbrows
    masksh = np.empty((2, 16, 128, 4224), dtype=f)
    for p in range(128):
        masksh[:, :, p, :] = pad[:, :, 127 - p:127 - p + 4224]
    out["masksh"] = masksh
    out["pbrows"] = pbrows
    tr = lambda a: np.ascontiguousarray(np.asarray(a, dtype=f).T)
    out["l0wq_t"] = tr(inputs["L0_attn_Wq"])
    out["l0wkv_t"] = tr(inputs["L0_attn_Wkv"])
    out["l0wproj_t"] = tr(inputs["L0_attn_Wproj"])
    out["l0bproj"] = np.asarray(inputs["L0_attn_bproj"], f)
    out["l0wgate"] = np.asarray(inputs["L0_ffn_w_gate"], f)
    out["l0wgu"] = np.asarray(inputs["L0_ffn_w_gate_up"], f)
    out["l0wdown"] = np.asarray(inputs["L0_ffn_w_down"], f)
    out["l1wq_t"] = tr(inputs["L1_attn_Wq"])
    out["l1wk_t"] = tr(inputs["L1_attn_Wk"])
    out["l1wv_t"] = tr(inputs["L1_attn_Wv"])
    out["l1wo_t"] = tr(inputs["L1_attn_Wo"])
    bqkv = np.stack([
        np.asarray(inputs["L1_attn_bq"], f).reshape(8, 128).T,
        np.asarray(inputs["L1_attn_bk"], f).reshape(8, 128).T,
        np.asarray(inputs["L1_attn_bv"], f).reshape(8, 128).T,
    ], 1)
    out["l1bqkv"] = np.ascontiguousarray(bqkv.reshape(128, 24))
    out["l1bv"] = np.asarray(inputs["L1_attn_bv"], f)
    out["l1bo"] = np.asarray(inputs["L1_attn_bo"], f)
    out["l1routerW"] = np.asarray(inputs["L1_ffn_router_W"], f)
    out["l1routerb"] = np.asarray(inputs["L1_ffn_router_b"], f)
    out["l1w1"] = np.asarray(inputs["L1_ffn_W1"], f)
    out["l1b1t"] = np.ascontiguousarray(
        np.asarray(inputs["L1_ffn_b1"], f).reshape(8, 32, 128).transpose(2, 0, 1).reshape(128, 256))
    out["l1w2"] = np.asarray(inputs["L1_ffn_W2"], f)
    out["l1b2"] = np.asarray(inputs["L1_ffn_b2"], f)
    return out


_PROG_CACHE = {}


def kernel(**inputs):
    aux = _host_prepare(inputs)
    x = np.asarray(inputs["L0_attn_x"], np.float32)
    xs = [np.ascontiguousarray(x[b]) for b in range(NCORES)]
    for grp in GROUPS:
        key = tuple(grp)
        if key not in _PROG_CACHE:
            ncb = build_program(list(grp))
            _PROG_CACHE[key] = (ncb, list(ncb._lazy_input_names))
        nc, names = _PROG_CACHE[key]
        in_maps = []
        for b in range(NCORES):
            m = {nm: aux[nm] for nm in names if nm != "x"}
            m["x"] = xs[b]
            in_maps.append(m)
        res = run_bass_kernel_spmd(nc, in_maps, core_ids=list(range(NCORES)))
        xs = [np.asarray(res.results[b]["y"], np.float32) for b in range(NCORES)]
    return np.stack(xs, 0)


# version 04
# speedup vs baseline: 1.0074x; 1.0074x over previous
import contextlib
import numpy as np
import concourse.bass as bass
import concourse.mybir as mybir
from concourse.bass_utils import run_bass_kernel_spmd

F32 = mybir.dt.float32
BF16 = mybir.dt.bfloat16
I32 = mybir.dt.int32
AF = mybir.ActivationFunctionType
OP = mybir.AluOpType
AX = mybir.AxisListType

T = 2048
C = 1024
NTT = 16
NFC = 8
NCORES = 8

GROUPS = [[0, 1, 2, 3, 4, 5]]

ENGINES = ("t", "v", "s", "g", "y")
SEMS = ("T", "V", "S", "DWI", "DW0", "DW1", "DWM", "DA", "DB1", "DB2")


class Prog:
    def __init__(self):
        self.ir = {e: [] for e in ENGINES}
        self.cnt = {s: 0 for s in SEMS}
        self.lastwait = {}

    def op(self, eng, fn, inc=None, k=1):
        ev = None
        if inc is not None:
            self.cnt[inc] += k
            ev = (inc, self.cnt[inc])
        self.ir[eng].append(("op", fn, inc, k))
        return ev

    def drain(self, eng):
        self.ir[eng].append(("drain",))

    def wait(self, eng, ev):
        if ev is None:
            return
        sem, val = ev
        key = (eng, sem)
        if self.lastwait.get(key, 0) >= val:
            return
        self.lastwait[key] = val
        self.ir[eng].append(("wait", sem, val))

    def waitall(self, eng, evs):
        for ev in evs:
            self.wait(eng, ev)

    def barrier(self):
        evs = [(s, self.cnt[s]) for s in SEMS if self.cnt[s] > 0]
        for e in ENGINES:
            for ev in evs:
                self.wait(e, ev)


class BankTrack:

    def __init__(self, prog):
        self.prog = prog
        self.w = {}
        self.r = {}

    def pe_begin(self, pi, halves):
        for h in halves:
            for ev in self.r.pop((pi, h), []):
                self.prog.wait("t", ev)

    def pe_done(self, pi, halves, ev):
        for h in halves:
            self.w[(pi, h)] = ev

    def rd_begin(self, eng, pi, halves):
        for h in halves:
            self.prog.wait(eng, self.w.get((pi, h)))

    def rd_done(self, pi, halves, ev):
        for h in halves:
            self.r.setdefault((pi, h), []).append(ev)


DEBUG_TAPS = []

INPUT_SHAPES = {
    "x": (T, C), "ident": (128, 128), "masksh": (2, 16, 128, 4224), "pbrows": (2, 16, 4095),
    "l0wq_t": (C, C), "l0wkv_t": (C, 512), "l0wproj_t": (C, C), "l0bproj": (C,), "l0wgate": (C, 8),
    "l0wgu": (8, C, 8192), "l0wdown": (8, 4096, C),
    "l1wq_t": (C, C), "l1wk_t": (C, C), "l1wv_t": (C, C), "l1wo_t": (C, C),
    "l1bqkv": (128, 24), "l1bv": (C,), "l1bo": (C,),
    "l1routerW": (C, 8), "l1routerb": (8,),
    "l1w1": (8, C, 4096), "l1b1t": (128, 256), "l1w2": (8, 4096, C), "l1b2": (8, C),
}


class LazyIns(dict):
    def __init__(self, nc):
        super().__init__()
        self.nc = nc

    def __missing__(self, name):
        ap = self.nc.dram_tensor(name, list(INPUT_SHAPES[name]), F32, kind="ExternalInput").ap()
        self[name] = ap
        return ap


def build_program(sub_ids):
    nc = bass.Bass("TRN2", target_bir_lowering=False)
    d = LazyIns(nc)
    yout = nc.dram_tensor("y", [T, C], F32, kind="ExternalOutput").ap()

    with contextlib.ExitStack() as st:
        t = {}

        def sb(name, cols, dtype):
            t[name] = st.enter_context(nc.sbuf_tensor(name, [128, cols], dtype))

        sb("XT", NTT * C, F32)
        sb("LNF", NFC * T, BF16)
        sb("LNT", 2 * C, BF16)
        sb("XSQ", C, F32)
        sb("STAT", 32, F32)
        sb("ONESB", 128, BF16)
        sb("ONESF", 128, F32)
        sb("IDENT", 128, BF16)
        sb("ROWS", 2176, F32)
        sb("BQKV", 24, F32)
        sb("B1T", 256, F32)
        sb("B2B", 1024, BF16)
        sb("WGATE0", 64, BF16)
        sb("WGATE1", 64, BF16)
        sb("WS0", 12288, BF16)
        sb("WS1", 12288, BF16)
        sb("SCR", 18688, BF16)
        sb("BSC", 16, F32)
        sb("NEGBT", 4, F32)
        sb("NB16", 4, BF16)
        sb("PBMXT", 32, F32)

        P = [st.enter_context(nc.psum_tensor(f"P{i}", [128, 1024], F32)) for i in range(4)]
        sem = {s: st.enter_context(nc.semaphore(s)) for s in SEMS}

        SCR = t["SCR"]
        t["ACT"] = SCR[:, 0:4096]
        t["SG0"] = SCR[:, 4096:5120].bitcast(F32)
        t["SG1"] = SCR[:, 5120:6144].bitcast(F32)
        t["GT"] = SCR[:, 6144:6400].bitcast(F32)
        t["GTB"] = SCR[:, 6400:6408]
        t["GSC"] = SCR[:, 6464:6592].bitcast(F32)
        t["GTT"] = SCR[:, 6656:8704]
        t["MSB"] = SCR[:, 0:5888]
        t["AWS"] = SCR[:, 5888:9984]
        t["TMPS"] = SCR[:, 5888:6912]
        t["VT"] = SCR[:, 9984:14144]
        t["EXPT"] = SCR[:, 14144:16192]
        t["QSQ"] = SCR[:, 14144:16192]
        t["RECB"] = SCR[:, 16192:18240].bitcast(F32)

        prog = Prog()
        bk = BankTrack(prog)
        XT, LNF, ACT = t["XT"], t["LNF"], t["ACT"]

        def TOP(fn, inc=None):
            return prog.op("t", fn, inc)

        def VOP(fn, inc=None):
            return prog.op("v", fn, inc)

        def SOP(fn, inc=None):
            return prog.op("s", fn, inc)

        def GDMA(fn, sm="DWI"):
            return prog.op("g", fn, sm, 16)

        def YDMA(fn):
            return prog.op("y", fn, "DA", 16)

        def BDMA(fn, sm="DB2"):
            return prog.op("s", fn, sm, 16)

        def emit_init():
            VOP(lambda v: v.memset(t["ONESB"][:], 1.0))
            VOP(lambda v: v.memset(t["ONESF"][:], 1.0))
            VOP(lambda v: v.memset(t["STAT"][:, 16:17], 1e-5))
            ev_id = GDMA(lambda g: g.dma_start(out=t["IDENT"][:], in_=d["ident"]))
            YDMA(lambda y: y.dma_start(
                out=XT[:].rearrange("p (tt c) -> p tt c", tt=NTT),
                in_=d["x"].rearrange("(tt p) c -> p tt c", p=128)))
            if any(s in (3, 5) for s in sub_ids):
                YDMA(lambda y: y.dma_start(out=t["ROWS"][0:1, 2048:2056], in_=d["l1routerb"][None, :]))
                YDMA(lambda y: y.dma_start(out=t["BQKV"][:], in_=d["l1bqkv"]))
                YDMA(lambda y: y.dma_start(out=t["B1T"][:], in_=d["l1b1t"]))
                GDMA(lambda g: g.dma_start(out=t["B2B"][0:8, :], in_=d["l1b2"]))
                GDMA(lambda g: g.dma_start(
                    out=t["WGATE1"][:].rearrange("p (ci e) -> p ci e", ci=8),
                    in_=d["l1routerW"].rearrange("(ci p) e -> p ci e", p=128)))
            if any(s in (0, 2) for s in sub_ids):
                GDMA(lambda g: g.dma_start(
                    out=t["WGATE0"][:].rearrange("p (ci e) -> p ci e", ci=8),
                    in_=d["l0wgate"].rearrange("(ci p) e -> p ci e", p=128)))
            attn_ls = [0 if s == 1 else 1 for s in sub_ids if s in (1, 4)]
            ev_t_prev = None
            for L in sorted(set(attn_ls)):
                prog.wait("g", ev_t_prev)
                GDMA(lambda g, L=L: g.dma_start(
                    out=t["MSB"][0:16, 0:4095], in_=d["pbrows"][L]))
                prog.wait("v", ("DWI", prog.cnt["DWI"]))
                prog.wait("v", ev_t_prev)
                ev_r = VOP(lambda v: v.tensor_reduce(
                    out=t["MSB"][0:16, 5000:5001], in_=t["MSB"][0:16, 0:4095],
                    axis=AX.X, op=OP.max), inc="V")
                bk.pe_begin(0, (0,))
                prog.wait("t", ev_r)
                prog.wait("t", ("DWI", prog.cnt["DWI"]))
                ev_t = TOP(lambda tt_: tt_.transpose(
                    P[0].bitcast(BF16)[0:1, 0:16], t["MSB"][0:16, 5000:5001],
                    t["IDENT"][0:16, 0:16]), inc="T")
                ev_t_prev = ev_t
                bk.pe_done(0, (0,), ev_t)
                bk.rd_begin("v", 0, (0,))
                ev_v = VOP(lambda v, L=L: v.tensor_copy(
                    t["PBMXT"][0:1, L * 16:L * 16 + 16],
                    P[0].bitcast(BF16)[0:1, 0:16]), inc="V")
                bk.rd_done(0, (0,), ev_v)
                prog.wait("g", ev_v)

        def emit_ln():
            lnf_evs = []
            lnt_tread = [None, None]
            stat_sread = [None, None]
            xsq_vread = [None]
            for tt in range(NTT):
                xs = XT[:, tt * C:(tt + 1) * C]
                r = tt % 2
                stat = t["STAT"][:, r * 8:r * 8 + 8]
                prog.wait("v", stat_sread[r])
                VOP(lambda v, xs=xs, stat=stat: v.reduce_sum(out=stat[:, 0:1], in_=xs, axis=AX.X))
                prog.drain("v")
                ev0 = VOP(lambda v, stat=stat: v.tensor_scalar(
                    out=stat[:, 2:3], in0=stat[:, 0:1], scalar1=1.0 / C, scalar2=None,
                    op0=OP.mult), inc="V")
                prog.drain("v")
                VOP(lambda v, stat=stat: v.tensor_tensor(
                    out=stat[:, 4:5], in0=stat[:, 2:3], in1=stat[:, 2:3], op=OP.mult))
                prog.wait("s", xsq_vread[0])
                ev_sq = SOP(lambda s, xs=xs: s.activation(out=t["XSQ"][:], in_=xs, func=AF.Square), inc="S")
                prog.wait("v", ev_sq)
                ev_xv = VOP(lambda v, stat=stat: v.reduce_sum(out=stat[:, 1:2], in_=t["XSQ"][:], axis=AX.X), inc="V")
                xsq_vread[0] = ev_xv
                prog.drain("v")
                VOP(lambda v, stat=stat: v.tensor_scalar(
                    out=stat[:, 3:4], in0=stat[:, 1:2], scalar1=1.0 / C, scalar2=None, op0=OP.mult))
                prog.drain("v")
                ev_var = VOP(lambda v, stat=stat: v.tensor_tensor(
                    out=stat[:, 5:6], in0=stat[:, 3:4], in1=stat[:, 4:5], op=OP.subtract), inc="V")
                prog.wait("s", ev_var)
                SOP(lambda s, stat=stat: s.activation(
                    out=stat[:, 6:7], in_=stat[:, 5:6], func=AF.Ln, bias=t["STAT"][:, 16:17]))
                prog.drain("s")
                ev_rstd = SOP(lambda s, stat=stat: s.activation(
                    out=stat[:, 7:8], in_=stat[:, 6:7], func=AF.Exp, scale=-0.5), inc="S")
                stat_sread[r] = ev_rstd
                prog.wait("v", ev_rstd)
                lt = t["LNT"][:, r * C:(r + 1) * C]
                prog.wait("v", lnt_tread[r])
                ev_lnt = VOP(lambda v, xs=xs, stat=stat, lt=lt: v.tensor_scalar(
                    out=lt, in0=xs, scalar1=stat[:, 2:3], scalar2=stat[:, 7:8],
                    op0=OP.subtract, op1=OP.mult), inc="V")
                pi = tt % 2
                bk.pe_begin(pi, (0,))
                prog.wait("t", ev_lnt)
                pt = P[pi].bitcast(BF16)
                for fc in range(NFC):
                    last = fc == NFC - 1
                    TOP(lambda tt_, pt=pt, lt=lt, fc=fc: tt_.transpose(
                        pt[:, fc * 128:(fc + 1) * 128], lt[:, fc * 128:(fc + 1) * 128],
                        t["IDENT"][:]), inc="T" if last else None)
                ev_tr = ("T", prog.cnt["T"])
                lnt_tread[r] = ev_tr
                bk.pe_done(pi, (0,), ev_tr)
                bk.rd_begin("v", pi, (0,))
                dst = LNF[:].rearrange("p (fc tx) -> p fc tx", fc=NFC)[:, :, tt * 128:(tt + 1) * 128]
                ev_ev = VOP(lambda v, pt=pt, dst=dst: v.tensor_copy(
                    dst, pt[:, 0:1024].rearrange("p (fc q) -> p fc q", fc=8)), inc="V")
                bk.rd_done(pi, (0,), ev_ev)
                lnf_evs.append(ev_ev)
            return lnf_evs

        def emit_gates(L, lnf_evs):
            wg = t["WGATE0"] if L == 0 else t["WGATE1"]
            gts = t["GT"]
            ev_g = None
            gsc_sread = [None]
            gsc_vdone = [None]
            gtb_tread = [None]
            for tt in range(NTT):
                pi = 2 + tt % 2
                bk.pe_begin(pi, (0,))
                prog.wait("t", lnf_evs[tt])
                first = True
                if L == 1:
                    TOP(lambda tt_, pi=pi: tt_.matmul(
                        P[pi][:, 0:8], t["ONESF"][0:1, 0:128], t["ROWS"][0:1, 2048:2056],
                        start=True, stop=False))
                    first = False
                for ci in range(8):
                    last = ci == 7
                    TOP(lambda tt_, pi=pi, ci=ci, tt=tt, wg=wg, first=first, last=last: tt_.matmul(
                        P[pi][:, 0:8],
                        LNF[:, ci * T + tt * 128:ci * T + (tt + 1) * 128],
                        wg[:, ci * 8:(ci + 1) * 8],
                        start=(first and ci == 0), stop=last), inc="T" if last else None)
                bk.pe_done(pi, (0,), ("T", prog.cnt["T"]))
                bk.rd_begin("v", pi, (0,))
                g = t["GSC"]
                prog.wait("v", gsc_sread[0])
                ev_m = VOP(lambda v, pi=pi: v.reduce_max(
                    out=g[:, 0:1], in_=P[pi][:, 0:8], axis=AX.X, negate=True), inc="V")
                bk.rd_done(pi, (0,), ev_m)
                bk.rd_begin("s", pi, (0,))
                prog.wait("s", ev_m)
                prog.wait("s", gsc_vdone[0])
                ev_e = SOP(lambda s, pi=pi: s.activation(
                    out=g[:, 8:16], in_=P[pi][:, 0:8], func=AF.Exp, bias=g[:, 0:1]), inc="S")
                gsc_sread[0] = ev_e
                bk.rd_done(pi, (0,), ev_e)
                prog.wait("v", ev_e)
                VOP(lambda v: v.reduce_sum(out=g[:, 1:2], in_=g[:, 8:16], axis=AX.X))
                prog.drain("v")
                VOP(lambda v: v.reciprocal(out=g[:, 2:3], in_=g[:, 1:2]))
                prog.drain("v")
                VOP(lambda v: v.tensor_scalar(
                    out=g[:, 16:24], in0=g[:, 8:16], scalar1=g[:, 2:3], scalar2=None, op0=OP.mult))
                prog.drain("v")
                pr = g[:, 16:24]
                if L == 0:
                    VOP(lambda v, pr=pr: v.reduce_max(out=g[:, 3:4], in_=pr, axis=AX.X))
                    prog.drain("v")
                    VOP(lambda v, pr=pr: v.tensor_scalar(
                        out=g[:, 24:32], in0=pr, scalar1=g[:, 3:4], scalar2=None, op0=OP.is_ge))
                    prog.drain("v")
                    VOP(lambda v, pr=pr: v.tensor_tensor(out=g[:, 32:40], in0=pr, in1=g[:, 24:32], op=OP.mult))
                    prog.drain("v")
                    VOP(lambda v, pr=pr: v.tensor_tensor(out=g[:, 40:48], in0=pr, in1=g[:, 32:40], op=OP.subtract))
                    prog.drain("v")
                    VOP(lambda v: v.reduce_max(out=g[:, 4:5], in_=g[:, 40:48], axis=AX.X))
                    prog.drain("v")
                    VOP(lambda v, pr=pr: v.tensor_scalar(
                        out=g[:, 48:56], in0=pr, scalar1=g[:, 4:5], scalar2=None, op0=OP.is_ge))
                    prog.drain("v")
                    VOP(lambda v, pr=pr: v.tensor_tensor(out=g[:, 56:64], in0=pr, in1=g[:, 48:56], op=OP.mult))
                    prog.drain("v")
                    VOP(lambda v: v.reduce_sum(out=g[:, 5:6], in_=g[:, 56:64], axis=AX.X))
                    prog.drain("v")
                    VOP(lambda v: v.tensor_scalar(out=g[:, 6:7], in0=g[:, 5:6], scalar1=1e-9, scalar2=None, op0=OP.max))
                    prog.drain("v")
                    VOP(lambda v: v.reciprocal(out=g[:, 7:8], in_=g[:, 6:7]))
                    prog.drain("v")
                    VOP(lambda v: v.tensor_scalar(out=g[:, 7:8], in0=g[:, 7:8], scalar1=0.5, scalar2=None, op0=OP.mult))
                    prog.drain("v")
                    ev_g = VOP(lambda v, tt=tt: v.tensor_scalar(
                        out=gts[:, tt * 8:(tt + 1) * 8], in0=g[:, 56:64],
                        scalar1=g[:, 7:8], scalar2=None, op0=OP.mult), inc="V")
                    gsc_vdone[0] = ev_g
                else:
                    VOP(lambda v, pr=pr: v.reduce_max(out=g[:, 3:4], in_=pr, axis=AX.X))
                    prog.drain("v")
                    VOP(lambda v, pr=pr: v.tensor_scalar(
                        out=g[:, 24:32], in0=pr, scalar1=g[:, 3:4], scalar2=None, op0=OP.is_ge))
                    prog.drain("v")
                    VOP(lambda v, pr=pr: v.tensor_tensor(out=g[:, 32:40], in0=pr, in1=g[:, 24:32], op=OP.mult))
                    prog.drain("v")
                    ev_g = VOP(lambda v, tt=tt: v.tensor_scalar(
                        out=gts[:, tt * 8:(tt + 1) * 8], in0=g[:, 32:40],
                        scalar1=0.5, scalar2=None, op0=OP.mult), inc="V")
                    gsc_vdone[0] = ev_g
                    prog.drain("v")
                    prog.wait("v", gtb_tread[0])
                    ev_c = VOP(lambda v, tt=tt: v.tensor_copy(
                        t["GTB"][:, 0:8], gts[:, tt * 8:(tt + 1) * 8]), inc="V")
                    bk.pe_begin(pi, (1,))
                    prog.wait("t", ev_c)
                    ev_tr = TOP(lambda tt_, pi=pi: tt_.transpose(
                        P[pi].bitcast(BF16)[0:8, 1024:1152], t["GTB"][:, 0:8],
                        t["IDENT"][:]), inc="T")
                    gtb_tread[0] = ev_tr
                    bk.pe_done(pi, (1,), ev_tr)
                    bk.rd_begin("v", pi, (1,))
                    ev_gt = VOP(lambda v, tt=tt, pi=pi: v.tensor_copy(
                        t["GTT"][0:8, tt * 128:(tt + 1) * 128],
                        P[pi].bitcast(BF16)[0:8, 1024:1152]), inc="V")
                    bk.rd_done(pi, (1,), ev_gt)
                    ev_g = ev_gt
            return ev_g

        def emit_ffn(L, ev_gates):
            E = 8
            NWIN, HCW = 8, 4
            ws = [t["WS0"], t["WS1"]]
            ws_consumed = [None, None]
            ws_ready = [None, None]

            def issue_window(idx):
                e, win = idx // NWIN, idx % NWIN
                buf = idx % 2
                sm = "DW%d" % buf
                w = ws[buf]
                prog.wait("g", ws_consumed[buf])
                if L == 0:
                    GDMA(lambda g, e=e, win=win, w=w: g.dma_start(
                        out=w[:, 0:4096].rearrange("p (ci q) -> p ci q", ci=8),
                        in_=d["l0wgu"][e, :, win * 512:(win + 1) * 512]
                        .rearrange("(ci p) q -> p ci q", p=128)), sm)
                    GDMA(lambda g, e=e, win=win, w=w: g.dma_start(
                        out=w[:, 4096:8192].rearrange("p (ci q) -> p ci q", ci=8),
                        in_=d["l0wgu"][e, :, 4096 + win * 512:4096 + (win + 1) * 512]
                        .rearrange("(ci p) q -> p ci q", p=128)), sm)
                    ev = GDMA(lambda g, e=e, win=win, w=w: g.dma_start(
                        out=w[:, 8192:12288].rearrange("p (hc q) -> p hc q", hc=4),
                        in_=d["l0wdown"][e, win * 512:(win + 1) * 512, :]
                        .rearrange("(hc p) q -> p hc q", p=128)), sm)
                else:
                    GDMA(lambda g, e=e, win=win, w=w: g.dma_start(
                        out=w[:, 0:4096].rearrange("p (ci q) -> p ci q", ci=8),
                        in_=d["l1w1"][e, :, win * 512:(win + 1) * 512]
                        .rearrange("(ci p) q -> p ci q", p=128)), sm)
                    ev = GDMA(lambda g, e=e, win=win, w=w: g.dma_start(
                        out=w[:, 4096:8192].rearrange("p (hc q) -> p hc q", hc=4),
                        in_=d["l1w2"][e, win * 512:(win + 1) * 512, :]
                        .rearrange("(hc p) q -> p hc q", p=128)), sm)
                ws_ready[buf] = ev

            evict_ev = [None] * NTT
            if L == 1:
                prog.wait("t", ev_gates)
                for grp in range(8):
                    for i in range(2):
                        tt = grp * 2 + i
                        bk.pe_begin(i, (0, 1))
                        for dh in range(2):
                            last = dh == 1
                            TOP(lambda tt_, i=i, tt=tt, dh=dh: tt_.matmul(
                                P[i][:, dh * 512:(dh + 1) * 512],
                                t["GTT"][0:8, tt * 128:(tt + 1) * 128],
                                t["B2B"][0:8, dh * 512:(dh + 1) * 512],
                                start=True, stop=True), inc="T" if last else None)
                        bk.pe_done(i, (0, 1), ("T", prog.cnt["T"]))
                        bk.rd_begin("v", i, (0, 1))
                        ev_e = VOP(lambda v, i=i, tt=tt: v.tensor_tensor(
                            out=XT[:, tt * C:(tt + 1) * C], in0=P[i][:, 0:1024],
                            in1=XT[:, tt * C:(tt + 1) * C], op=OP.add), inc="V")
                        evict_ev[tt] = ev_e
                        bk.rd_done(i, (0, 1), ev_e)

            issue_window(0)
            act_read = [None] * 8
            sg_read = [None, None]
            for e in range(E):
                for win in range(NWIN):
                    idx = e * NWIN + win
                    buf = idx % 2
                    if idx + 1 < E * NWIN:
                        issue_window(idx + 1)
                    w = ws[buf]
                    prog.wait("t", ws_ready[buf])
                    for t4 in range(4):
                        for hc in range(HCW):
                            rot = hc % 2
                            pi = 2 + rot
                            if L == 0:
                                bk.pe_begin(pi, (0,))
                                for ci in range(8):
                                    last = ci == 7
                                    TOP(lambda tt_, w=w, ci=ci, hc=hc, t4=t4, pi=pi, last=last: tt_.matmul(
                                        P[pi][:, 0:512],
                                        w[:, ci * 512 + hc * 128:ci * 512 + (hc + 1) * 128],
                                        LNF[:, ci * T + t4 * 512:ci * T + (t4 + 1) * 512],
                                        start=(ci == 0), stop=last), inc="T" if last else None)
                                bk.pe_done(pi, (0,), ("T", prog.cnt["T"]))
                                bk.pe_begin(pi, (1,))
                                for ci in range(8):
                                    last = ci == 7
                                    TOP(lambda tt_, w=w, ci=ci, hc=hc, t4=t4, pi=pi, last=last: tt_.matmul(
                                        P[pi][:, 512:1024],
                                        w[:, 4096 + ci * 512 + hc * 128:4096 + ci * 512 + (hc + 1) * 128],
                                        LNF[:, ci * T + t4 * 512:ci * T + (t4 + 1) * 512],
                                        start=(ci == 0), stop=last), inc="T" if last else None)
                                bk.pe_done(pi, (1,), ("T", prog.cnt["T"]))
                                bk.rd_begin("s", pi, (0,))
                                prog.wait("s", sg_read[rot])
                                sg = t["SG%d" % rot]
                                ev_sg = SOP(lambda s, pi=pi, sg=sg: s.activation(
                                    out=sg[:], in_=P[pi][:, 0:512], func=AF.Silu), inc="S")
                                bk.rd_done(pi, (0,), ev_sg)
                                bk.rd_begin("v", pi, (1,))
                                prog.wait("v", ev_sg)
                                prog.wait("v", act_read[hc])
                                ev_act = VOP(lambda v, pi=pi, sg=sg, hc=hc: v.tensor_tensor(
                                    out=ACT[:, hc * 512:(hc + 1) * 512],
                                    in0=P[pi][:, 512:1024], in1=sg[:], op=OP.mult), inc="V")
                                sg_read[rot] = ev_act
                                bk.rd_done(pi, (1,), ev_act)
                                act_ready_ev = ev_act
                            else:
                                bk.pe_begin(pi, (0,))
                                for ci in range(8):
                                    last = ci == 7
                                    TOP(lambda tt_, w=w, ci=ci, hc=hc, t4=t4, pi=pi, last=last: tt_.matmul(
                                        P[pi][:, 0:512],
                                        w[:, ci * 512 + hc * 128:ci * 512 + (hc + 1) * 128],
                                        LNF[:, ci * T + t4 * 512:ci * T + (t4 + 1) * 512],
                                        start=(ci == 0), stop=last), inc="T" if last else None)
                                bk.pe_done(pi, (0,), ("T", prog.cnt["T"]))
                                bk.rd_begin("s", pi, (0,))
                                prog.wait("s", act_read[hc])
                                hcg = win * HCW + hc
                                ev_act = SOP(lambda s, pi=pi, hc=hc, e=e, hcg=hcg: s.activation(
                                    out=ACT[:, hc * 512:(hc + 1) * 512],
                                    in_=P[pi][:, 0:512], func=AF.Relu,
                                    bias=t["B1T"][:, e * 32 + hcg:e * 32 + hcg + 1]), inc="S")
                                bk.rd_done(pi, (0,), ev_act)
                                act_ready_ev = ev_act
                            if hc == HCW - 1:
                                last_act_ev = act_ready_ev
                        prog.wait("t", last_act_ev)
                        for half in range(2):
                            bk.pe_begin(0, (0, 1))
                            bk.pe_begin(1, (0, 1))
                            for hc in range(HCW):
                                for i in range(2):
                                    ttl = half * 2 + i
                                    for dh in range(2):
                                        last = hc == HCW - 1 and i == 1 and dh == 1
                                        wdo = 8192 if L == 0 else 4096
                                        TOP(lambda tt_, w=w, hc=hc, i=i, ttl=ttl, dh=dh, wdo=wdo: tt_.matmul(
                                            P[i][:, dh * 512:(dh + 1) * 512],
                                            ACT[:, hc * 512 + ttl * 128:hc * 512 + (ttl + 1) * 128],
                                            w[:, wdo + hc * 1024 + dh * 512:wdo + hc * 1024 + (dh + 1) * 512],
                                            start=(hc == 0), stop=(hc == HCW - 1)),
                                            inc="T" if last else None)
                            ev_dn = ("T", prog.cnt["T"])
                            bk.pe_done(0, (0, 1), ev_dn)
                            bk.pe_done(1, (0, 1), ev_dn)
                            if half == 1:
                                for hc in range(HCW):
                                    act_read[hc] = ev_dn
                            prog.wait("v", ev_gates)
                            for i in range(2):
                                tt = t4 * 4 + half * 2 + i
                                bk.rd_begin("v", i, (0, 1))
                                prog.wait("v", evict_ev[tt])
                                ev_e = VOP(lambda v, i=i, tt=tt, e=e: v.scalar_tensor_tensor(
                                    out=XT[:, tt * C:(tt + 1) * C], in0=P[i][:, 0:1024],
                                    scalar=t["GT"][:, tt * 8 + e:tt * 8 + e + 1],
                                    in1=XT[:, tt * C:(tt + 1) * C],
                                    op0=OP.mult, op1=OP.add), inc="V")
                                evict_ev[tt] = ev_e
                                bk.rd_done(i, (0, 1), ev_e)
                    ws_consumed[buf] = ("T", prog.cnt["T"])

        def emit_attn(L, lnf_evs):
            AWS, MSB, VT, EXPT, QSQ, RECB = t["AWS"], t["MSB"], t["VT"], t["EXPT"], t["QSQ"], t["RECB"]
            BSC, BSCI = t["BSC"], t["BSC"].bitcast(I32)
            wq = d["l0wq_t"] if L == 0 else d["l1wq_t"]
            wo = d["l0wproj_t"] if L == 0 else d["l1wo_t"]
            def OUTC(fc):
                return t["WS0"][:, fc * 2048:(fc + 1) * 2048] if fc < 6 else t["WS1"][:, (fc - 6) * 2048:(fc - 5) * 2048]
            QW = t["WS1"][:, 4096:8192]
            KW = t["WS1"][:, 8192:12288]

            YDMA(lambda y: y.dma_start(
                out=t["ROWS"][0:1, 0:1024],
                in_=(d["l0bproj"] if L == 0 else d["l1bo"])[None, :]))
            if L == 1:
                YDMA(lambda y: y.dma_start(out=t["ROWS"][0:1, 1024:2048], in_=d["l1bv"][None, :]))
            ev_rows = ("DA", prog.cnt["DA"])
            ev_bv = ev_rows if L == 1 else None

            aws_tread = [None, None]
            tmps_dread = None
            wave_tread = None
            negbt_sread = None
            nb16_tread = [None]
            expt_tread = [None, None]
            msb_vread = None
            out_evs = []

            def load_aws(buf, col0, src_ap, ncols):
                prog.wait("g", aws_tread[buf])
                if buf == 0:
                    prog.wait("g", tmps_dread)
                return GDMA(lambda g, col0=col0, src_ap=src_ap, ncols=ncols: g.dma_start(
                    out=AWS[:, buf * 2048 + col0:buf * 2048 + col0 + ncols]
                    .rearrange("p (ci q) -> p ci q", ci=8),
                    in_=src_ap), "DW%d" % buf)

            for w in range(4):
                if L == 1:
                    ev_q = load_aws(0, 0, wq[:, w * 256:(w + 1) * 256].rearrange("(ci p) q -> p ci q", p=128), 2048)
                    ev_k = load_aws(1, 0, d["l1wk_t"][:, w * 256:(w + 1) * 256].rearrange("(ci p) q -> p ci q", p=128), 2048)
                else:
                    ev_q = load_aws(0, 0, wq[:, w * 256:(w + 1) * 256].rearrange("(ci p) q -> p ci q", p=128), 2048)
                    ev_k = load_aws(1, 0, d["l0wkv_t"][:, 0:256].rearrange("(ci p) q -> p ci q", p=128), 2048) if w == 0 else None
                prog.waitall("t", lnf_evs)
                prog.wait("v", wave_tread)
                prog.wait("t", ev_q)
                for pc in range(2):
                    qc = 2 * w + pc
                    for tq in range(4):
                        pi, half = pc, tq % 2
                        bk.pe_begin(pi, (half,))
                        for ci in range(8):
                            last = ci == 7
                            TOP(lambda tt_, pi=pi, half=half, ci=ci, pc=pc, tq=tq, last=last: tt_.matmul(
                                P[pi][:, half * 512:(half + 1) * 512],
                                AWS[:, ci * 256 + pc * 128:ci * 256 + (pc + 1) * 128],
                                LNF[:, ci * T + tq * 512:ci * T + (tq + 1) * 512],
                                start=(ci == 0), stop=last), inc="T" if last else None)
                        bk.pe_done(pi, (half,), ("T", prog.cnt["T"]))
                        bk.rd_begin("v", pi, (half,))
                        if L == 1:
                            ev_e = VOP(lambda v, pi=pi, half=half, pc=pc, tq=tq, qc=qc: v.tensor_scalar(
                                out=QW[:, pc * 2048 + tq * 512:pc * 2048 + (tq + 1) * 512],
                                in0=P[pi][:, half * 512:(half + 1) * 512],
                                scalar1=t["BQKV"][:, qc:qc + 1], scalar2=0.125,
                                op0=OP.add, op1=OP.mult), inc="V")
                        else:
                            ev_e = VOP(lambda v, pi=pi, half=half, pc=pc, tq=tq: v.tensor_scalar(
                                out=QW[:, pc * 2048 + tq * 512:pc * 2048 + (tq + 1) * 512],
                                in0=P[pi][:, half * 512:(half + 1) * 512],
                                scalar1=0.125, scalar2=None, op0=OP.mult), inc="V")
                        bk.rd_done(pi, (half,), ev_e)
                ev_qprod = ("T", prog.cnt["T"])
                aws_tread[0] = ev_qprod
                if L == 1 or w == 0:
                    prog.wait("t", ev_k)
                    for pc in range(2):
                        qc = 2 * w + pc
                        for tq in range(4):
                            pi, half = pc, tq % 2
                            bk.pe_begin(pi, (half,))
                            for ci in range(8):
                                last = ci == 7
                                TOP(lambda tt_, pi=pi, half=half, ci=ci, pc=pc, tq=tq, last=last: tt_.matmul(
                                    P[pi][:, half * 512:(half + 1) * 512],
                                    AWS[:, 2048 + ci * 256 + pc * 128:2048 + ci * 256 + (pc + 1) * 128],
                                    LNF[:, ci * T + tq * 512:ci * T + (tq + 1) * 512],
                                    start=(ci == 0), stop=last), inc="T" if last else None)
                            bk.pe_done(pi, (half,), ("T", prog.cnt["T"]))
                            bk.rd_begin("v", pi, (half,))
                            if L == 1:
                                ev_e = VOP(lambda v, pi=pi, half=half, pc=pc, tq=tq, qc=qc: v.tensor_scalar(
                                    out=KW[:, pc * 2048 + tq * 512:pc * 2048 + (tq + 1) * 512],
                                    in0=P[pi][:, half * 512:(half + 1) * 512],
                                    scalar1=t["BQKV"][:, 8 + qc:8 + qc + 1], scalar2=None,
                                    op0=OP.add), inc="V")
                            else:
                                ev_e = VOP(lambda v, pi=pi, half=half, pc=pc, tq=tq: v.tensor_copy(
                                    KW[:, pc * 2048 + tq * 512:pc * 2048 + (tq + 1) * 512],
                                    P[pi][:, half * 512:(half + 1) * 512]), inc="V")
                            bk.rd_done(pi, (half,), ev_e)
                    aws_tread[1] = ("T", prog.cnt["T"])
                if L == 1:
                    ev_v = load_aws(0, 0, d["l1wv_t"][:, w * 256:(w + 1) * 256].rearrange("(ci p) q -> p ci q", p=128), 2048)
                    dv = 256
                elif w == 0:
                    ev_v = load_aws(1, 0, d["l0wkv_t"][:, 256:512].rearrange("(ci p) q -> p ci q", p=128), 2048)
                    dv = 256
                else:
                    ev_v = None
                if ev_v is not None:
                    vbuf = 0 if L == 1 else 1
                    if ev_bv is not None:
                        prog.wait("t", ev_bv)
                    prog.wait("v", wave_tread)
                    ev_ones = VOP(lambda v: v.memset(
                        VT.rearrange("p (tt s c) -> p tt s c", tt=16, s=4)[:, :, :, 64:65], 1.0), inc="V")
                    prog.wait("t", ev_v)
                    for tt in range(NTT):
                        pi, half = tt % 2, (tt // 2) % 2
                        bk.pe_begin(pi, (half,))
                        first = True
                        if L == 1:
                            TOP(lambda tt_, pi=pi, half=half, w=w: tt_.matmul(
                                P[pi][:, half * 512:half * 512 + 256],
                                t["ONESF"][0:1, 0:128],
                                t["ROWS"][0:1, 1024 + w * 256:1024 + (w + 1) * 256],
                                start=True, stop=False))
                            first = False
                        for ci in range(8):
                            last = ci == 7
                            TOP(lambda tt_, pi=pi, half=half, ci=ci, tt=tt, first=first, vbuf=vbuf, last=last: tt_.matmul(
                                P[pi][:, half * 512:half * 512 + 256],
                                LNF[:, ci * T + tt * 128:ci * T + (tt + 1) * 128],
                                AWS[:, vbuf * 2048 + ci * 256:vbuf * 2048 + (ci + 1) * 256],
                                start=(first and ci == 0), stop=last), inc="T" if last else None)
                        bk.pe_done(pi, (half,), ("T", prog.cnt["T"]))
                        bk.rd_begin("v", pi, (half,))
                        prog.wait("v", ev_ones)
                        dst = VT.rearrange("p (tt2 s c) -> p tt2 s c", tt2=16, s=4)[:, tt, 0:4, 0:64]
                        ev_e = VOP(lambda v, pi=pi, half=half, dst=dst: v.tensor_copy(
                            dst, P[pi][:, half * 512:half * 512 + 256]
                            .rearrange("p (s c) -> p s c", c=64)), inc="V")
                        bk.rd_done(pi, (half,), ev_e)
                    ev_vprod = ("T", prog.cnt["T"])
                    aws_tread[vbuf] = ev_vprod
                ev_qkv_v = ("V", prog.cnt["V"])

                for pc in range(2):
                    hA = 4 * w + 2 * pc
                    hB = hA + 1
                    oc = 2 * w + pc
                    prog.wait("t", ev_qkv_v)
                    prog.waitall("s", [expt_tread[0], expt_tread[1], ev_qkv_v])
                    ev_sq = SOP(lambda s, pc=pc: s.activation(
                        out=QSQ[:, 0:2048], in_=QW[:, pc * 2048:(pc + 1) * 2048], func=AF.Square), inc="S")
                    bk.pe_begin(2, (0, 1))
                    bk.pe_begin(3, (0, 1))
                    prog.wait("t", ev_sq)
                    for j in range(4):
                        pi = 2 + j // 2
                        TOP(lambda tt_, pi=pi, j=j: tt_.matmul(
                            P[pi][0:1, (j % 2) * 512:(j % 2 + 1) * 512],
                            t["ONESB"][:, 0:1],
                            QSQ[:, j * 512:(j + 1) * 512],
                            start=True, stop=True), inc="T" if j == 3 else None)
                    ev_qn = ("T", prog.cnt["T"])
                    bk.pe_done(2, (0, 1), ev_qn)
                    bk.pe_done(3, (0, 1), ev_qn)
                    bk.rd_begin("v", 2, (0, 1))
                    bk.rd_begin("v", 3, (0, 1))
                    VOP(lambda v: v.reduce_max(out=BSC[0:1, 0:1], in_=P[2][0:1, 0:1024], axis=AX.X))
                    ev_r = VOP(lambda v: v.reduce_max(out=BSC[0:1, 1:2], in_=P[3][0:1, 0:1024], axis=AX.X), inc="V")
                    bk.rd_done(2, (0, 1), ev_r)
                    bk.rd_done(3, (0, 1), ev_r)
                    prog.drain("v")
                    VOP(lambda v: v.tensor_tensor(out=BSC[0:1, 2:3], in0=BSC[0:1, 0:1], in1=BSC[0:1, 1:2], op=OP.max))
                    prog.wait("s", ev_qn)
                    kwc = pc if L == 1 else (2 * w + pc) % 2
                    ev_sqk = SOP(lambda s, kwc=kwc: s.activation(
                        out=QSQ[:, 0:2048], in_=KW[:, kwc * 2048:(kwc + 1) * 2048],
                        func=AF.Square), inc="S")
                    bk.pe_begin(2, (0, 1))
                    bk.pe_begin(3, (0, 1))
                    prog.wait("t", ev_sqk)
                    for j in range(4):
                        pi = 2 + j // 2
                        TOP(lambda tt_, pi=pi, j=j: tt_.matmul(
                            P[pi][0:1, (j % 2) * 512:(j % 2 + 1) * 512],
                            t["ONESB"][:, 0:1],
                            QSQ[:, j * 512:(j + 1) * 512],
                            start=True, stop=True), inc="T" if j == 3 else None)
                    ev_kn = ("T", prog.cnt["T"])
                    bk.pe_done(2, (0, 1), ev_kn)
                    bk.pe_done(3, (0, 1), ev_kn)
                    bk.rd_begin("v", 2, (0, 1))
                    bk.rd_begin("v", 3, (0, 1))
                    VOP(lambda v: v.reduce_max(out=BSC[0:1, 3:4], in_=P[2][0:1, 0:1024], axis=AX.X))
                    ev_r2 = VOP(lambda v: v.reduce_max(out=BSC[0:1, 4:5], in_=P[3][0:1, 0:1024], axis=AX.X), inc="V")
                    bk.rd_done(2, (0, 1), ev_r2)
                    bk.rd_done(3, (0, 1), ev_r2)
                    prog.drain("v")
                    VOP(lambda v: v.tensor_tensor(out=BSC[0:1, 5:6], in0=BSC[0:1, 3:4], in1=BSC[0:1, 4:5], op=OP.max))
                    prog.drain("v")
                    VOP(lambda v: v.tensor_tensor(out=BSC[0:1, 6:7], in0=BSC[0:1, 2:3], in1=BSC[0:1, 5:6], op=OP.mult))
                    prog.drain("v")
                    VOP(lambda v: v.tensor_scalar(out=BSCI[0:1, 7:8], in0=BSCI[0:1, 6:7], scalar1=1, scalar2=None, op0=OP.arith_shift_right))
                    prog.drain("v")
                    VOP(lambda v: v.tensor_scalar(out=BSCI[0:1, 8:9], in0=BSCI[0:1, 7:8], scalar1=0x1FC00000, scalar2=None, op0=OP.add))
                    prog.drain("v")
                    VOP(lambda v: v.tensor_scalar(out=BSC[0:1, 9:10], in0=BSC[0:1, 8:9], scalar1=1.06, scalar2=None, op0=OP.mult))
                    VOP(lambda v, hA=hA, hB=hB, L=L: v.tensor_tensor(
                        out=BSC[0:1, 10:11], in0=t["PBMXT"][0:1, L * 16 + hA:L * 16 + hA + 1],
                        in1=t["PBMXT"][0:1, L * 16 + hB:L * 16 + hB + 1], op=OP.max))
                    prog.drain("v")
                    VOP(lambda v: v.tensor_tensor(out=BSC[0:1, 11:12], in0=BSC[0:1, 9:10], in1=BSC[0:1, 10:11], op=OP.add))
                    prog.drain("v")
                    VOP(lambda v: v.tensor_scalar(out=BSC[0:1, 12:13], in0=BSC[0:1, 11:12], scalar1=-1.0, scalar2=None, op0=OP.mult))
                    prog.drain("v")
                    prog.wait("v", nb16_tread[0])
                    ev_nb16 = VOP(lambda v: v.tensor_copy(t["NB16"][0:1, 0:1], BSC[0:1, 12:13]), inc="V")
                    bk.pe_begin(2, (0,))
                    prog.wait("t", ev_nb16)
                    ev_bb = TOP(lambda tt_: tt_.matmul(
                        P[2][:, 0:1], t["ONESB"][0:1, 0:128], t["NB16"][0:1, 0:1],
                        start=True, stop=True), inc="T")
                    nb16_tread[0] = ev_bb
                    bk.pe_done(2, (0,), ev_bb)
                    bk.rd_begin("v", 2, (0,))
                    prog.wait("v", negbt_sread)
                    ev_negbt = VOP(lambda v: v.tensor_copy(t["NEGBT"][:, 0:1], P[2][:, 0:1]), inc="V")
                    bk.rd_done(2, (0,), ev_negbt)

                    for qh in range(2):
                        base = 127 + 1024 * qh
                        prog.wait("g", msb_vread)
                        GDMA(lambda g, L=L, hA=hA, base=base: g.dma_start(
                            out=MSB[:, 0:2944], in_=d["masksh"][L, hA][:, base:base + 2944]), "DWM")
                        ev_msb = GDMA(lambda g, L=L, hB=hB, base=base: g.dma_start(
                            out=MSB[:, 2944:5888], in_=d["masksh"][L, hB][:, base:base + 2944]), "DWM")
                        ev_scores = [None, None]
                        ev_avT = [None, None]
                        for sc in range(16):
                            c0w = 1920 - 128 * sc
                            for side in range(2):
                                pi = side
                                h = hA if side == 0 else hB
                                rows = slice(0, 64) if side == 0 else slice(64, 128)
                                bk.pe_begin(pi, (0, 1))
                                prog.wait("t", ev_scores[side])
                                kw_pc = pc if L == 1 else (2 * w + pc) % 2
                                for jj in range(2):
                                    last = jj == 1
                                    TOP(lambda tt_, pi=pi, rows=rows, sc=sc, jj=jj, qh=qh, kw_pc=kw_pc, pc=pc: tt_.matmul(
                                        P[pi][:, jj * 512:(jj + 1) * 512],
                                        KW[rows, kw_pc * 2048 + sc * 128:kw_pc * 2048 + (sc + 1) * 128],
                                        QW[rows, pc * 2048 + qh * 1024 + jj * 512:pc * 2048 + qh * 1024 + (jj + 1) * 512],
                                        start=True, stop=True), inc="T" if last else None)
                                ev_mm = ("T", prog.cnt["T"])
                                bk.pe_done(pi, (0, 1), ev_mm)
                                bk.rd_begin("v", pi, (0, 1))
                                prog.wait("v", ev_msb)
                                ev_mask = VOP(lambda v, pi=pi, side=side, c0w=c0w: v.tensor_tensor(
                                    out=P[pi][:, 0:1024], in0=P[pi][:, 0:1024],
                                    in1=MSB[:, side * 2944 + c0w:side * 2944 + c0w + 1024],
                                    op=OP.add), inc="V")
                                msb_vread = ev_mask
                                prog.wait("s", ev_mask)
                                prog.wait("s", ev_negbt)
                                prog.wait("s", expt_tread[side])
                                ev_exp = SOP(lambda s, pi=pi, side=side: s.activation(
                                    out=EXPT[:, side * 1024:(side + 1) * 1024], in_=P[pi][:, 0:1024],
                                    func=AF.Exp, bias=t["NEGBT"][:, 0:1]), inc="S")
                                negbt_sread = ev_exp
                                ev_scores[side] = ev_exp
                                bk.rd_done(pi, (0, 1), ev_exp)
                                pav = 2 + side
                                if sc == 0:
                                    bk.pe_begin(pav, (0, 1))
                                prog.wait("t", ev_exp)
                                slot = (h - 4 * w) if L == 1 else (h % 4)
                                for jj in range(2):
                                    last = jj == 1
                                    TOP(lambda tt_, pav=pav, sc=sc, slot=slot, side=side, jj=jj: tt_.matmul(
                                        P[pav][0:65, jj * 512:(jj + 1) * 512],
                                        VT[:, sc * 260 + slot * 65:sc * 260 + slot * 65 + 65],
                                        EXPT[:, side * 1024 + jj * 512:side * 1024 + (jj + 1) * 512],
                                        start=(sc == 0), stop=(sc == 15)),
                                        inc="T" if last else None)
                                ev_av = ("T", prog.cnt["T"])
                                expt_tread[side] = ev_av
                                if sc == 15:
                                    bk.pe_done(pav, (0, 1), ev_av)
                        for side in range(2):
                            pav = 2 + side
                            psc = side
                            bk.rd_begin("v", pav, (0, 1))
                            ev_rec = VOP(lambda v, pav=pav: v.reciprocal(
                                out=RECB[64:65, 0:1024], in_=P[pav][64:65, 0:1024]), inc="V")
                            bk.rd_done(pav, (0, 1), ev_rec)
                            bk.pe_begin(psc, (0, 1))
                            prog.wait("t", ev_rec)
                            for jj in range(2):
                                last = jj == 1
                                TOP(lambda tt_, psc=psc, jj=jj: tt_.matmul(
                                    P[psc][:, jj * 512:(jj + 1) * 512],
                                    t["ONESF"][64:65, 0:128],
                                    RECB[64:65, jj * 512:(jj + 1) * 512],
                                    start=True, stop=True), inc="T" if last else None)
                            ev_rb = ("T", prog.cnt["T"])
                            bk.pe_done(psc, (0, 1), ev_rb)
                            bk.rd_begin("v", psc, (0, 1))
                            ev_rc = VOP(lambda v, psc=psc: v.tensor_copy(
                                RECB[0:128, 0:1024], P[psc][:, 0:1024]), inc="V")
                            bk.rd_done(psc, (0, 1), ev_rc)
                            ev_scores[side] = None
                            prog.wait("v", ev_rc)
                            bk.rd_begin("v", pav, (0, 1))
                            if side == 0:
                                ev_ev = VOP(lambda v, pav=pav, oc=oc, qh=qh: v.tensor_tensor(
                                    out=OUTC(oc)[0:64, qh * 1024:(qh + 1) * 1024],
                                    in0=P[pav][0:64, 0:1024], in1=RECB[0:64, 0:1024],
                                    op=OP.mult), inc="V")
                                out_evs.append(ev_ev)
                            else:
                                prog.wait("v", tmps_dread)
                                ev_ev = VOP(lambda v, pav=pav: v.tensor_tensor(
                                    out=t["TMPS"][0:64, 0:1024],
                                    in0=P[pav][0:64, 0:1024], in1=RECB[0:64, 0:1024],
                                    op=OP.mult), inc="V")
                                prog.wait("s", ev_ev)
                                ev_sh = BDMA(lambda s, oc=oc, qh=qh: s.dma_start(
                                    out=OUTC(oc)[64:128, qh * 1024:(qh + 1) * 1024],
                                    in_=t["TMPS"][0:64, 0:1024]))
                                tmps_dread = ev_sh
                                out_evs.append(ev_sh)
                            bk.rd_done(pav, (0, 1), ev_ev)
                wave_tread = ("T", prog.cnt["T"])

            prog.waitall("t", out_evs)
            prog.wait("t", ev_rows)
            for eh in range(2):
                prog.wait("g", aws_tread[0])
                prog.wait("g", aws_tread[1])
                prog.wait("g", tmps_dread)
                ev_w = GDMA(lambda g, eh=eh: g.dma_start(
                    out=AWS[:, 0:4096].rearrange("p (ci q) -> p ci q", ci=8),
                    in_=wo[:, eh * 512:(eh + 1) * 512].rearrange("(ci p) q -> p ci q", p=128)), "DW0")
                prog.wait("t", ev_w)
                for tt in range(NTT):
                    pi, half = tt % 2, (tt // 2) % 2
                    bk.pe_begin(pi, (half,))
                    TOP(lambda tt_, pi=pi, half=half, eh=eh: tt_.matmul(
                        P[pi][:, half * 512:(half + 1) * 512],
                        t["ONESF"][0:1, 0:128],
                        t["ROWS"][0:1, eh * 512:(eh + 1) * 512],
                        start=True, stop=False))
                    for fc in range(8):
                        last = fc == 7
                        TOP(lambda tt_, pi=pi, half=half, fc=fc, tt=tt, last=last: tt_.matmul(
                            P[pi][:, half * 512:(half + 1) * 512],
                            OUTC(fc)[:, tt * 128:(tt + 1) * 128],
                            AWS[:, fc * 512:(fc + 1) * 512],
                            start=False, stop=last), inc="T" if last else None)
                    bk.pe_done(pi, (half,), ("T", prog.cnt["T"]))
                    bk.rd_begin("v", pi, (half,))
                    ev_e = VOP(lambda v, pi=pi, half=half, tt=tt, eh=eh: v.tensor_tensor(
                        out=XT[:, tt * C + eh * 512:tt * C + (eh + 1) * 512],
                        in0=P[pi][:, half * 512:(half + 1) * 512],
                        in1=XT[:, tt * C + eh * 512:tt * C + (eh + 1) * 512],
                        op=OP.add), inc="V")
                    bk.rd_done(pi, (half,), ev_e)
                aws_tread[0] = ("T", prog.cnt["T"])
                aws_tread[1] = aws_tread[0]

        emit_init()
        prog.barrier()
        for sub in sub_ids:
            lnf_evs = emit_ln()
            if sub in (0, 2):
                ev_g = emit_gates(0, lnf_evs)
                emit_ffn(0, ev_g)
            elif sub in (3, 5):
                ev_g = emit_gates(1, lnf_evs)
                emit_ffn(1, ev_g)
            else:
                emit_attn(0 if sub == 1 else 1, lnf_evs)
            prog.barrier()
        ev_y = YDMA(lambda y: y.dma_start(
            out=yout.rearrange("(tt p) c -> p tt c", p=128),
            in_=XT[:].rearrange("p (tt c) -> p tt c", tt=NTT)))
        prog.wait("y", ev_y)
        for nm in DEBUG_TAPS:
            ap = t[nm][:] if hasattr(t[nm], '__enter__') or 'SBTensor' in type(t[nm]).__name__ else t[nm]
            dt_out = nc.dram_tensor("dbg_" + nm, [128, ap.shape[1]], F32, kind="ExternalOutput").ap()
            if ap.dtype == F32:
                ev_d = YDMA(lambda y, ap=ap, dt_out=dt_out: y.dma_start(out=dt_out, in_=ap))
                prog.wait("y", ev_d)
            else:
                ev_d = prog.op("g", lambda g, ap=ap, dt_out=dt_out: g.dma_start(out=dt_out, in_=ap), "DWI", 16)
                prog.wait("g", ev_d)

        with nc.Block() as block:
            def mk(engname):
                def _(eng):
                    for item in prog.ir[engname]:
                        if item[0] == "wait":
                            eng.wait_ge(sem[item[1]], item[2])
                        elif item[0] == "drain":
                            eng.drain()
                        else:
                            _, fn, inc, k = item
                            r = fn(eng)
                            if inc is not None:
                                r.then_inc(sem[inc], k)
                return _
            block.tensor(mk("t"))
            block.vector(mk("v"))
            block.scalar(mk("s"))
            block.gpsimd(mk("g"))
            block.sync(mk("y"))
    nc._lazy_input_names = list(d.keys())
    return nc


def _host_prepare(inputs):
    f = np.float32
    out = {}
    out["ident"] = np.eye(128, dtype=f)
    n_head = 16
    start = 2.0 ** (-(2.0 ** (-(np.log2(n_head) - 3.0))))
    slopes = (start ** np.arange(1, n_head + 1)).astype(np.float64)
    idx = np.arange(4095, dtype=np.float64)
    rel = 2047.0 - idx
    row = np.minimum(rel, 0.0)
    pb0 = (slopes[:, None] * row[None, :] * (64.0 ** -0.5)).astype(f)
    pb1 = np.asarray(inputs["L1_attn_pos_bias"], dtype=f)
    pbrows = np.stack([pb0, pb1], 0)
    pad = np.zeros((2, 16, 127 + 4224), dtype=f)
    pad[:, :, 127:127 + 4095] = pbrows
    masksh = np.empty((2, 16, 128, 4224), dtype=f)
    for p in range(128):
        masksh[:, :, p, :] = pad[:, :, 127 - p:127 - p + 4224]
    out["masksh"] = masksh
    out["pbrows"] = pbrows
    tr = lambda a: np.ascontiguousarray(np.asarray(a, dtype=f).T)
    out["l0wq_t"] = tr(inputs["L0_attn_Wq"])
    out["l0wkv_t"] = tr(inputs["L0_attn_Wkv"])
    out["l0wproj_t"] = tr(inputs["L0_attn_Wproj"])
    out["l0bproj"] = np.asarray(inputs["L0_attn_bproj"], f)
    out["l0wgate"] = np.asarray(inputs["L0_ffn_w_gate"], f)
    out["l0wgu"] = np.asarray(inputs["L0_ffn_w_gate_up"], f)
    out["l0wdown"] = np.asarray(inputs["L0_ffn_w_down"], f)
    out["l1wq_t"] = tr(inputs["L1_attn_Wq"])
    out["l1wk_t"] = tr(inputs["L1_attn_Wk"])
    out["l1wv_t"] = tr(inputs["L1_attn_Wv"])
    out["l1wo_t"] = tr(inputs["L1_attn_Wo"])
    bqkv = np.stack([
        np.asarray(inputs["L1_attn_bq"], f).reshape(8, 128).T,
        np.asarray(inputs["L1_attn_bk"], f).reshape(8, 128).T,
        np.asarray(inputs["L1_attn_bv"], f).reshape(8, 128).T,
    ], 1)
    out["l1bqkv"] = np.ascontiguousarray(bqkv.reshape(128, 24))
    out["l1bv"] = np.asarray(inputs["L1_attn_bv"], f)
    out["l1bo"] = np.asarray(inputs["L1_attn_bo"], f)
    out["l1routerW"] = np.asarray(inputs["L1_ffn_router_W"], f)
    out["l1routerb"] = np.asarray(inputs["L1_ffn_router_b"], f)
    out["l1w1"] = np.asarray(inputs["L1_ffn_W1"], f)
    out["l1b1t"] = np.ascontiguousarray(
        np.asarray(inputs["L1_ffn_b1"], f).reshape(8, 32, 128).transpose(2, 0, 1).reshape(128, 256))
    out["l1w2"] = np.asarray(inputs["L1_ffn_W2"], f)
    out["l1b2"] = np.asarray(inputs["L1_ffn_b2"], f)
    return out


_PROG_CACHE = {}


def kernel(**inputs):
    aux = _host_prepare(inputs)
    x = np.asarray(inputs["L0_attn_x"], np.float32)
    xs = [np.ascontiguousarray(x[b]) for b in range(NCORES)]
    for grp in GROUPS:
        key = tuple(grp)
        if key not in _PROG_CACHE:
            ncb = build_program(list(grp))
            _PROG_CACHE[key] = (ncb, list(ncb._lazy_input_names))
        nc, names = _PROG_CACHE[key]
        in_maps = []
        for b in range(NCORES):
            m = {nm: aux[nm] for nm in names if nm != "x"}
            m["x"] = xs[b]
            in_maps.append(m)
        res = run_bass_kernel_spmd(nc, in_maps, core_ids=list(range(NCORES)))
        xs = [np.asarray(res.results[b]["y"], np.float32) for b in range(NCORES)]
    return np.stack(xs, 0)


# version 05
# speedup vs baseline: 1.0165x; 1.0090x over previous
import contextlib
import numpy as np
import concourse.bass as bass
import concourse.mybir as mybir
from concourse.bass_utils import run_bass_kernel_spmd

F32 = mybir.dt.float32
BF16 = mybir.dt.bfloat16
I32 = mybir.dt.int32
AF = mybir.ActivationFunctionType
OP = mybir.AluOpType
AX = mybir.AxisListType

T = 2048
C = 1024
NTT = 16
NFC = 8
NCORES = 8

GROUPS = [[0, 1, 2, 3, 4, 5]]

ENGINES = ("t", "v", "s", "g", "y")
SEMS = ("T", "V", "S", "DWI", "DW0", "DW1", "DWM0", "DWM1", "DA", "DB1", "DB2")


class Prog:
    def __init__(self):
        self.ir = {e: [] for e in ENGINES}
        self.cnt = {s: 0 for s in SEMS}
        self.lastwait = {}

    def op(self, eng, fn, inc=None, k=1):
        ev = None
        if inc is not None:
            self.cnt[inc] += k
            ev = (inc, self.cnt[inc])
        self.ir[eng].append(("op", fn, inc, k))
        return ev

    def drain(self, eng):
        self.ir[eng].append(("drain",))

    def wait(self, eng, ev):
        if ev is None:
            return
        sem, val = ev
        key = (eng, sem)
        if self.lastwait.get(key, 0) >= val:
            return
        self.lastwait[key] = val
        self.ir[eng].append(("wait", sem, val))

    def waitall(self, eng, evs):
        for ev in evs:
            self.wait(eng, ev)

    def barrier(self):
        evs = [(s, self.cnt[s]) for s in SEMS if self.cnt[s] > 0]
        for e in ENGINES:
            for ev in evs:
                self.wait(e, ev)


class BankTrack:

    def __init__(self, prog):
        self.prog = prog
        self.w = {}
        self.r = {}

    def pe_begin(self, pi, halves):
        for h in halves:
            for ev in self.r.pop((pi, h), []):
                self.prog.wait("t", ev)

    def pe_done(self, pi, halves, ev):
        for h in halves:
            self.w[(pi, h)] = ev

    def rd_begin(self, eng, pi, halves):
        for h in halves:
            self.prog.wait(eng, self.w.get((pi, h)))

    def rd_done(self, pi, halves, ev):
        for h in halves:
            self.r.setdefault((pi, h), []).append(ev)


DEBUG_TAPS = []

INPUT_SHAPES = {
    "x": (T, C), "ident": (128, 128), "masksh": (2, 16, 128, 4224), "pbrows": (2, 16, 4095),
    "l0wq_t": (C, C), "l0wkv_t": (C, 512), "l0wproj_t": (C, C), "l0bproj": (C,), "l0wgate": (C, 8),
    "l0wgu": (8, C, 8192), "l0wdown": (8, 4096, C),
    "l1wq_t": (C, C), "l1wk_t": (C, C), "l1wv_t": (C, C), "l1wo_t": (C, C),
    "l1bqkv": (128, 24), "l1bv": (C,), "l1bo": (C,),
    "l1routerW": (C, 8), "l1routerb": (8,),
    "l1w1": (8, C, 4096), "l1b1t": (128, 256), "l1w2": (8, 4096, C), "l1b2": (8, C),
}


class LazyIns(dict):
    def __init__(self, nc):
        super().__init__()
        self.nc = nc

    def __missing__(self, name):
        ap = self.nc.dram_tensor(name, list(INPUT_SHAPES[name]), F32, kind="ExternalInput").ap()
        self[name] = ap
        return ap


def build_program(sub_ids):
    nc = bass.Bass("TRN2", target_bir_lowering=False)
    d = LazyIns(nc)
    yout = nc.dram_tensor("y", [T, C], F32, kind="ExternalOutput").ap()

    with contextlib.ExitStack() as st:
        t = {}

        def sb(name, cols, dtype):
            t[name] = st.enter_context(nc.sbuf_tensor(name, [128, cols], dtype))

        sb("XT", NTT * C, F32)
        sb("LNF", NFC * T, BF16)
        sb("LNT", 2 * C, BF16)
        sb("XSQ", C, F32)
        sb("STAT", 32, F32)
        sb("ONESB", 128, BF16)
        sb("ONESF", 128, F32)
        sb("IDENT", 128, BF16)
        sb("ROWS", 2176, F32)
        sb("BQKV", 24, F32)
        sb("B1T", 256, F32)
        sb("B2B", 1024, BF16)
        sb("WGATE0", 64, BF16)
        sb("WGATE1", 64, BF16)
        sb("WS0", 12288, BF16)
        sb("WS1", 12288, BF16)
        sb("SCR", 20800, BF16)
        sb("BSC", 16, F32)
        sb("NEGBT", 4, F32)
        sb("NB16", 4, BF16)
        sb("PBMXT", 32, F32)

        P = [st.enter_context(nc.psum_tensor(f"P{i}", [128, 1024], F32)) for i in range(4)]
        sem = {s: st.enter_context(nc.semaphore(s)) for s in SEMS}

        SCR = t["SCR"]
        t["ACT"] = SCR[:, 0:4096]
        t["SG0"] = SCR[:, 4096:5120].bitcast(F32)
        t["SG1"] = SCR[:, 5120:6144].bitcast(F32)
        t["GT"] = SCR[:, 6144:6400].bitcast(F32)
        t["GTB"] = SCR[:, 6400:6408]
        t["GSC"] = SCR[:, 6464:6592].bitcast(F32)
        t["GTT"] = SCR[:, 6656:8704]
        t["MSB"] = SCR[:, 0:8448]
        t["AWS"] = SCR[:, 8448:12544]
        t["TMPS"] = SCR[:, 8448:9472]
        t["VT"] = SCR[:, 12544:16704]
        t["EXPT"] = SCR[:, 16704:18752]
        t["QSQ"] = SCR[:, 16704:18752]
        t["RECB"] = SCR[:, 18752:20800].bitcast(F32)

        prog = Prog()
        bk = BankTrack(prog)
        XT, LNF, ACT = t["XT"], t["LNF"], t["ACT"]

        def TOP(fn, inc=None):
            return prog.op("t", fn, inc)

        def VOP(fn, inc=None):
            return prog.op("v", fn, inc)

        def SOP(fn, inc=None):
            return prog.op("s", fn, inc)

        def GDMA(fn, sm="DWI"):
            return prog.op("g", fn, sm, 16)

        def YDMA(fn):
            return prog.op("y", fn, "DA", 16)

        def BDMA(fn, sm="DB2"):
            return prog.op("s", fn, sm, 16)

        def emit_init():
            VOP(lambda v: v.memset(t["ONESB"][:], 1.0))
            VOP(lambda v: v.memset(t["ONESF"][:], 1.0))
            VOP(lambda v: v.memset(t["STAT"][:, 16:17], 1e-5))
            ev_id = GDMA(lambda g: g.dma_start(out=t["IDENT"][:], in_=d["ident"]))
            YDMA(lambda y: y.dma_start(
                out=XT[:].rearrange("p (tt c) -> p tt c", tt=NTT),
                in_=d["x"].rearrange("(tt p) c -> p tt c", p=128)))
            if any(s in (3, 4, 5) for s in sub_ids):
                YDMA(lambda y: y.dma_start(out=t["BQKV"][:], in_=d["l1bqkv"]))
            if any(s in (3, 5) for s in sub_ids):
                YDMA(lambda y: y.dma_start(out=t["ROWS"][0:1, 2048:2056], in_=d["l1routerb"][None, :]))
                YDMA(lambda y: y.dma_start(out=t["B1T"][:], in_=d["l1b1t"]))
                GDMA(lambda g: g.dma_start(out=t["B2B"][0:8, :], in_=d["l1b2"]))
                GDMA(lambda g: g.dma_start(
                    out=t["WGATE1"][:].rearrange("p (ci e) -> p ci e", ci=8),
                    in_=d["l1routerW"].rearrange("(ci p) e -> p ci e", p=128)))
            if any(s in (0, 2) for s in sub_ids):
                GDMA(lambda g: g.dma_start(
                    out=t["WGATE0"][:].rearrange("p (ci e) -> p ci e", ci=8),
                    in_=d["l0wgate"].rearrange("(ci p) e -> p ci e", p=128)))
            attn_ls = [0 if s == 1 else 1 for s in sub_ids if s in (1, 4)]
            ev_t_prev = None
            for L in sorted(set(attn_ls)):
                prog.wait("g", ev_t_prev)
                GDMA(lambda g, L=L: g.dma_start(
                    out=t["MSB"][0:16, 0:4095], in_=d["pbrows"][L]))
                prog.wait("v", ("DWI", prog.cnt["DWI"]))
                prog.wait("v", ev_t_prev)
                ev_r = VOP(lambda v: v.tensor_reduce(
                    out=t["MSB"][0:16, 6000:6001], in_=t["MSB"][0:16, 0:4095],
                    axis=AX.X, op=OP.max), inc="V")
                bk.pe_begin(0, (0,))
                prog.wait("t", ev_r)
                prog.wait("t", ("DWI", prog.cnt["DWI"]))
                ev_t = TOP(lambda tt_: tt_.transpose(
                    P[0].bitcast(BF16)[0:1, 0:16], t["MSB"][0:16, 6000:6001],
                    t["IDENT"][0:16, 0:16]), inc="T")
                ev_t_prev = ev_t
                bk.pe_done(0, (0,), ev_t)
                bk.rd_begin("v", 0, (0,))
                ev_v = VOP(lambda v, L=L: v.tensor_copy(
                    t["PBMXT"][0:1, L * 16:L * 16 + 16],
                    P[0].bitcast(BF16)[0:1, 0:16]), inc="V")
                bk.rd_done(0, (0,), ev_v)
                prog.wait("g", ev_v)

        def emit_ln():
            lnf_evs = []
            lnt_tread = [None, None]
            stat_sread = [None, None]
            xsq_vread = [None]
            for tt in range(NTT):
                xs = XT[:, tt * C:(tt + 1) * C]
                r = tt % 2
                stat = t["STAT"][:, r * 8:r * 8 + 8]
                prog.wait("v", stat_sread[r])
                VOP(lambda v, xs=xs, stat=stat: v.reduce_sum(out=stat[:, 0:1], in_=xs, axis=AX.X))
                prog.drain("v")
                ev0 = VOP(lambda v, stat=stat: v.tensor_scalar(
                    out=stat[:, 2:3], in0=stat[:, 0:1], scalar1=1.0 / C, scalar2=None,
                    op0=OP.mult), inc="V")
                prog.drain("v")
                VOP(lambda v, stat=stat: v.tensor_tensor(
                    out=stat[:, 4:5], in0=stat[:, 2:3], in1=stat[:, 2:3], op=OP.mult))
                prog.wait("s", xsq_vread[0])
                ev_sq = SOP(lambda s, xs=xs: s.activation(out=t["XSQ"][:], in_=xs, func=AF.Square), inc="S")
                prog.wait("v", ev_sq)
                ev_xv = VOP(lambda v, stat=stat: v.reduce_sum(out=stat[:, 1:2], in_=t["XSQ"][:], axis=AX.X), inc="V")
                xsq_vread[0] = ev_xv
                prog.drain("v")
                VOP(lambda v, stat=stat: v.tensor_scalar(
                    out=stat[:, 3:4], in0=stat[:, 1:2], scalar1=1.0 / C, scalar2=None, op0=OP.mult))
                prog.drain("v")
                ev_var = VOP(lambda v, stat=stat: v.tensor_tensor(
                    out=stat[:, 5:6], in0=stat[:, 3:4], in1=stat[:, 4:5], op=OP.subtract), inc="V")
                prog.wait("s", ev_var)
                SOP(lambda s, stat=stat: s.activation(
                    out=stat[:, 6:7], in_=stat[:, 5:6], func=AF.Ln, bias=t["STAT"][:, 16:17]))
                prog.drain("s")
                ev_rstd = SOP(lambda s, stat=stat: s.activation(
                    out=stat[:, 7:8], in_=stat[:, 6:7], func=AF.Exp, scale=-0.5), inc="S")
                stat_sread[r] = ev_rstd
                prog.wait("v", ev_rstd)
                lt = t["LNT"][:, r * C:(r + 1) * C]
                prog.wait("v", lnt_tread[r])
                ev_lnt = VOP(lambda v, xs=xs, stat=stat, lt=lt: v.tensor_scalar(
                    out=lt, in0=xs, scalar1=stat[:, 2:3], scalar2=stat[:, 7:8],
                    op0=OP.subtract, op1=OP.mult), inc="V")
                pi = tt % 2
                bk.pe_begin(pi, (0,))
                prog.wait("t", ev_lnt)
                pt = P[pi].bitcast(BF16)
                for fc in range(NFC):
                    last = fc == NFC - 1
                    TOP(lambda tt_, pt=pt, lt=lt, fc=fc: tt_.transpose(
                        pt[:, fc * 128:(fc + 1) * 128], lt[:, fc * 128:(fc + 1) * 128],
                        t["IDENT"][:]), inc="T" if last else None)
                ev_tr = ("T", prog.cnt["T"])
                lnt_tread[r] = ev_tr
                bk.pe_done(pi, (0,), ev_tr)
                bk.rd_begin("v", pi, (0,))
                dst = LNF[:].rearrange("p (fc tx) -> p fc tx", fc=NFC)[:, :, tt * 128:(tt + 1) * 128]
                ev_ev = VOP(lambda v, pt=pt, dst=dst: v.tensor_copy(
                    dst, pt[:, 0:1024].rearrange("p (fc q) -> p fc q", fc=8)), inc="V")
                bk.rd_done(pi, (0,), ev_ev)
                lnf_evs.append(ev_ev)
            return lnf_evs

        def emit_gates(L, lnf_evs):
            wg = t["WGATE0"] if L == 0 else t["WGATE1"]
            gts = t["GT"]
            ev_g = None
            gsc_sread = [None]
            gsc_vdone = [None]
            gtb_tread = [None]
            for tt in range(NTT):
                pi = 2 + tt % 2
                bk.pe_begin(pi, (0,))
                prog.wait("t", lnf_evs[tt])
                first = True
                if L == 1:
                    TOP(lambda tt_, pi=pi: tt_.matmul(
                        P[pi][:, 0:8], t["ONESF"][0:1, 0:128], t["ROWS"][0:1, 2048:2056],
                        start=True, stop=False))
                    first = False
                for ci in range(8):
                    last = ci == 7
                    TOP(lambda tt_, pi=pi, ci=ci, tt=tt, wg=wg, first=first, last=last: tt_.matmul(
                        P[pi][:, 0:8],
                        LNF[:, ci * T + tt * 128:ci * T + (tt + 1) * 128],
                        wg[:, ci * 8:(ci + 1) * 8],
                        start=(first and ci == 0), stop=last), inc="T" if last else None)
                bk.pe_done(pi, (0,), ("T", prog.cnt["T"]))
                bk.rd_begin("v", pi, (0,))
                g = t["GSC"]
                prog.wait("v", gsc_sread[0])
                ev_m = VOP(lambda v, pi=pi: v.reduce_max(
                    out=g[:, 0:1], in_=P[pi][:, 0:8], axis=AX.X, negate=True), inc="V")
                bk.rd_done(pi, (0,), ev_m)
                bk.rd_begin("s", pi, (0,))
                prog.wait("s", ev_m)
                prog.wait("s", gsc_vdone[0])
                ev_e = SOP(lambda s, pi=pi: s.activation(
                    out=g[:, 8:16], in_=P[pi][:, 0:8], func=AF.Exp, bias=g[:, 0:1]), inc="S")
                gsc_sread[0] = ev_e
                bk.rd_done(pi, (0,), ev_e)
                prog.wait("v", ev_e)
                VOP(lambda v: v.reduce_sum(out=g[:, 1:2], in_=g[:, 8:16], axis=AX.X))
                prog.drain("v")
                VOP(lambda v: v.reciprocal(out=g[:, 2:3], in_=g[:, 1:2]))
                prog.drain("v")
                VOP(lambda v: v.tensor_scalar(
                    out=g[:, 16:24], in0=g[:, 8:16], scalar1=g[:, 2:3], scalar2=None, op0=OP.mult))
                prog.drain("v")
                pr = g[:, 16:24]
                if L == 0:
                    VOP(lambda v, pr=pr: v.reduce_max(out=g[:, 3:4], in_=pr, axis=AX.X))
                    prog.drain("v")
                    VOP(lambda v, pr=pr: v.tensor_scalar(
                        out=g[:, 24:32], in0=pr, scalar1=g[:, 3:4], scalar2=None, op0=OP.is_ge))
                    prog.drain("v")
                    VOP(lambda v, pr=pr: v.tensor_tensor(out=g[:, 32:40], in0=pr, in1=g[:, 24:32], op=OP.mult))
                    prog.drain("v")
                    VOP(lambda v, pr=pr: v.tensor_tensor(out=g[:, 40:48], in0=pr, in1=g[:, 32:40], op=OP.subtract))
                    prog.drain("v")
                    VOP(lambda v: v.reduce_max(out=g[:, 4:5], in_=g[:, 40:48], axis=AX.X))
                    prog.drain("v")
                    VOP(lambda v, pr=pr: v.tensor_scalar(
                        out=g[:, 48:56], in0=pr, scalar1=g[:, 4:5], scalar2=None, op0=OP.is_ge))
                    prog.drain("v")
                    VOP(lambda v, pr=pr: v.tensor_tensor(out=g[:, 56:64], in0=pr, in1=g[:, 48:56], op=OP.mult))
                    prog.drain("v")
                    VOP(lambda v: v.reduce_sum(out=g[:, 5:6], in_=g[:, 56:64], axis=AX.X))
                    prog.drain("v")
                    VOP(lambda v: v.tensor_scalar(out=g[:, 6:7], in0=g[:, 5:6], scalar1=1e-9, scalar2=None, op0=OP.max))
                    prog.drain("v")
                    VOP(lambda v: v.reciprocal(out=g[:, 7:8], in_=g[:, 6:7]))
                    prog.drain("v")
                    VOP(lambda v: v.tensor_scalar(out=g[:, 7:8], in0=g[:, 7:8], scalar1=0.5, scalar2=None, op0=OP.mult))
                    prog.drain("v")
                    ev_g = VOP(lambda v, tt=tt: v.tensor_scalar(
                        out=gts[:, tt * 8:(tt + 1) * 8], in0=g[:, 56:64],
                        scalar1=g[:, 7:8], scalar2=None, op0=OP.mult), inc="V")
                    gsc_vdone[0] = ev_g
                else:
                    VOP(lambda v, pr=pr: v.reduce_max(out=g[:, 3:4], in_=pr, axis=AX.X))
                    prog.drain("v")
                    VOP(lambda v, pr=pr: v.tensor_scalar(
                        out=g[:, 24:32], in0=pr, scalar1=g[:, 3:4], scalar2=None, op0=OP.is_ge))
                    prog.drain("v")
                    VOP(lambda v, pr=pr: v.tensor_tensor(out=g[:, 32:40], in0=pr, in1=g[:, 24:32], op=OP.mult))
                    prog.drain("v")
                    ev_g = VOP(lambda v, tt=tt: v.tensor_scalar(
                        out=gts[:, tt * 8:(tt + 1) * 8], in0=g[:, 32:40],
                        scalar1=0.5, scalar2=None, op0=OP.mult), inc="V")
                    gsc_vdone[0] = ev_g
                    prog.drain("v")
                    prog.wait("v", gtb_tread[0])
                    ev_c = VOP(lambda v, tt=tt: v.tensor_copy(
                        t["GTB"][:, 0:8], gts[:, tt * 8:(tt + 1) * 8]), inc="V")
                    bk.pe_begin(pi, (1,))
                    prog.wait("t", ev_c)
                    ev_tr = TOP(lambda tt_, pi=pi: tt_.transpose(
                        P[pi].bitcast(BF16)[0:8, 1024:1152], t["GTB"][:, 0:8],
                        t["IDENT"][:]), inc="T")
                    gtb_tread[0] = ev_tr
                    bk.pe_done(pi, (1,), ev_tr)
                    bk.rd_begin("v", pi, (1,))
                    ev_gt = VOP(lambda v, tt=tt, pi=pi: v.tensor_copy(
                        t["GTT"][0:8, tt * 128:(tt + 1) * 128],
                        P[pi].bitcast(BF16)[0:8, 1024:1152]), inc="V")
                    bk.rd_done(pi, (1,), ev_gt)
                    ev_g = ev_gt
            return ev_g

        def emit_ffn(L, ev_gates):
            E = 8
            NWIN, HCW = 8, 4
            ws = [t["WS0"], t["WS1"]]
            ws_consumed = [None, None]
            ws_ready = [None, None]

            def issue_window(idx):
                e, win = idx // NWIN, idx % NWIN
                buf = idx % 2
                sm = "DW%d" % buf
                w = ws[buf]
                prog.wait("g", ws_consumed[buf])
                if L == 0:
                    GDMA(lambda g, e=e, win=win, w=w: g.dma_start(
                        out=w[:, 0:4096].rearrange("p (ci q) -> p ci q", ci=8),
                        in_=d["l0wgu"][e, :, win * 512:(win + 1) * 512]
                        .rearrange("(ci p) q -> p ci q", p=128)), sm)
                    GDMA(lambda g, e=e, win=win, w=w: g.dma_start(
                        out=w[:, 4096:8192].rearrange("p (ci q) -> p ci q", ci=8),
                        in_=d["l0wgu"][e, :, 4096 + win * 512:4096 + (win + 1) * 512]
                        .rearrange("(ci p) q -> p ci q", p=128)), sm)
                    ev = GDMA(lambda g, e=e, win=win, w=w: g.dma_start(
                        out=w[:, 8192:12288].rearrange("p (hc q) -> p hc q", hc=4),
                        in_=d["l0wdown"][e, win * 512:(win + 1) * 512, :]
                        .rearrange("(hc p) q -> p hc q", p=128)), sm)
                else:
                    GDMA(lambda g, e=e, win=win, w=w: g.dma_start(
                        out=w[:, 0:4096].rearrange("p (ci q) -> p ci q", ci=8),
                        in_=d["l1w1"][e, :, win * 512:(win + 1) * 512]
                        .rearrange("(ci p) q -> p ci q", p=128)), sm)
                    ev = GDMA(lambda g, e=e, win=win, w=w: g.dma_start(
                        out=w[:, 4096:8192].rearrange("p (hc q) -> p hc q", hc=4),
                        in_=d["l1w2"][e, win * 512:(win + 1) * 512, :]
                        .rearrange("(hc p) q -> p hc q", p=128)), sm)
                ws_ready[buf] = ev

            evict_ev = [None] * NTT
            if L == 1:
                prog.wait("t", ev_gates)
                for grp in range(8):
                    for i in range(2):
                        tt = grp * 2 + i
                        bk.pe_begin(i, (0, 1))
                        for dh in range(2):
                            last = dh == 1
                            TOP(lambda tt_, i=i, tt=tt, dh=dh: tt_.matmul(
                                P[i][:, dh * 512:(dh + 1) * 512],
                                t["GTT"][0:8, tt * 128:(tt + 1) * 128],
                                t["B2B"][0:8, dh * 512:(dh + 1) * 512],
                                start=True, stop=True), inc="T" if last else None)
                        bk.pe_done(i, (0, 1), ("T", prog.cnt["T"]))
                        bk.rd_begin("v", i, (0, 1))
                        ev_e = VOP(lambda v, i=i, tt=tt: v.tensor_tensor(
                            out=XT[:, tt * C:(tt + 1) * C], in0=P[i][:, 0:1024],
                            in1=XT[:, tt * C:(tt + 1) * C], op=OP.add), inc="V")
                        evict_ev[tt] = ev_e
                        bk.rd_done(i, (0, 1), ev_e)

            issue_window(0)
            act_read = [None] * 8
            sg_read = [None, None]
            for e in range(E):
                for win in range(NWIN):
                    idx = e * NWIN + win
                    buf = idx % 2
                    if idx + 1 < E * NWIN:
                        issue_window(idx + 1)
                    w = ws[buf]
                    prog.wait("t", ws_ready[buf])
                    for t4 in range(4):
                        for hc in range(HCW):
                            rot = hc % 2
                            pi = 2 + rot
                            if L == 0:
                                bk.pe_begin(pi, (0,))
                                for ci in range(8):
                                    last = ci == 7
                                    TOP(lambda tt_, w=w, ci=ci, hc=hc, t4=t4, pi=pi, last=last: tt_.matmul(
                                        P[pi][:, 0:512],
                                        w[:, ci * 512 + hc * 128:ci * 512 + (hc + 1) * 128],
                                        LNF[:, ci * T + t4 * 512:ci * T + (t4 + 1) * 512],
                                        start=(ci == 0), stop=last), inc="T" if last else None)
                                bk.pe_done(pi, (0,), ("T", prog.cnt["T"]))
                                bk.pe_begin(pi, (1,))
                                for ci in range(8):
                                    last = ci == 7
                                    TOP(lambda tt_, w=w, ci=ci, hc=hc, t4=t4, pi=pi, last=last: tt_.matmul(
                                        P[pi][:, 512:1024],
                                        w[:, 4096 + ci * 512 + hc * 128:4096 + ci * 512 + (hc + 1) * 128],
                                        LNF[:, ci * T + t4 * 512:ci * T + (t4 + 1) * 512],
                                        start=(ci == 0), stop=last), inc="T" if last else None)
                                bk.pe_done(pi, (1,), ("T", prog.cnt["T"]))
                                bk.rd_begin("s", pi, (0,))
                                prog.wait("s", sg_read[rot])
                                sg = t["SG%d" % rot]
                                ev_sg = SOP(lambda s, pi=pi, sg=sg: s.activation(
                                    out=sg[:], in_=P[pi][:, 0:512], func=AF.Silu), inc="S")
                                bk.rd_done(pi, (0,), ev_sg)
                                bk.rd_begin("v", pi, (1,))
                                prog.wait("v", ev_sg)
                                prog.wait("v", act_read[hc])
                                ev_act = VOP(lambda v, pi=pi, sg=sg, hc=hc: v.tensor_tensor(
                                    out=ACT[:, hc * 512:(hc + 1) * 512],
                                    in0=P[pi][:, 512:1024], in1=sg[:], op=OP.mult), inc="V")
                                sg_read[rot] = ev_act
                                bk.rd_done(pi, (1,), ev_act)
                                act_ready_ev = ev_act
                            else:
                                bk.pe_begin(pi, (0,))
                                for ci in range(8):
                                    last = ci == 7
                                    TOP(lambda tt_, w=w, ci=ci, hc=hc, t4=t4, pi=pi, last=last: tt_.matmul(
                                        P[pi][:, 0:512],
                                        w[:, ci * 512 + hc * 128:ci * 512 + (hc + 1) * 128],
                                        LNF[:, ci * T + t4 * 512:ci * T + (t4 + 1) * 512],
                                        start=(ci == 0), stop=last), inc="T" if last else None)
                                bk.pe_done(pi, (0,), ("T", prog.cnt["T"]))
                                bk.rd_begin("s", pi, (0,))
                                prog.wait("s", act_read[hc])
                                hcg = win * HCW + hc
                                ev_act = SOP(lambda s, pi=pi, hc=hc, e=e, hcg=hcg: s.activation(
                                    out=ACT[:, hc * 512:(hc + 1) * 512],
                                    in_=P[pi][:, 0:512], func=AF.Relu,
                                    bias=t["B1T"][:, e * 32 + hcg:e * 32 + hcg + 1]), inc="S")
                                bk.rd_done(pi, (0,), ev_act)
                                act_ready_ev = ev_act
                            if hc == HCW - 1:
                                last_act_ev = act_ready_ev
                        prog.wait("t", last_act_ev)
                        for half in range(2):
                            bk.pe_begin(0, (0, 1))
                            bk.pe_begin(1, (0, 1))
                            for hc in range(HCW):
                                for i in range(2):
                                    ttl = half * 2 + i
                                    for dh in range(2):
                                        last = hc == HCW - 1 and i == 1 and dh == 1
                                        wdo = 8192 if L == 0 else 4096
                                        TOP(lambda tt_, w=w, hc=hc, i=i, ttl=ttl, dh=dh, wdo=wdo: tt_.matmul(
                                            P[i][:, dh * 512:(dh + 1) * 512],
                                            ACT[:, hc * 512 + ttl * 128:hc * 512 + (ttl + 1) * 128],
                                            w[:, wdo + hc * 1024 + dh * 512:wdo + hc * 1024 + (dh + 1) * 512],
                                            start=(hc == 0), stop=(hc == HCW - 1)),
                                            inc="T" if last else None)
                            ev_dn = ("T", prog.cnt["T"])
                            bk.pe_done(0, (0, 1), ev_dn)
                            bk.pe_done(1, (0, 1), ev_dn)
                            if half == 1:
                                for hc in range(HCW):
                                    act_read[hc] = ev_dn
                            prog.wait("v", ev_gates)
                            for i in range(2):
                                tt = t4 * 4 + half * 2 + i
                                bk.rd_begin("v", i, (0, 1))
                                prog.wait("v", evict_ev[tt])
                                ev_e = VOP(lambda v, i=i, tt=tt, e=e: v.scalar_tensor_tensor(
                                    out=XT[:, tt * C:(tt + 1) * C], in0=P[i][:, 0:1024],
                                    scalar=t["GT"][:, tt * 8 + e:tt * 8 + e + 1],
                                    in1=XT[:, tt * C:(tt + 1) * C],
                                    op0=OP.mult, op1=OP.add), inc="V")
                                evict_ev[tt] = ev_e
                                bk.rd_done(i, (0, 1), ev_e)
                    ws_consumed[buf] = ("T", prog.cnt["T"])

        def emit_attn(L, lnf_evs):
            AWS, MSB, VT, EXPT, QSQ, RECB = t["AWS"], t["MSB"], t["VT"], t["EXPT"], t["QSQ"], t["RECB"]
            BSC, BSCI = t["BSC"], t["BSC"].bitcast(I32)
            wq = d["l0wq_t"] if L == 0 else d["l1wq_t"]
            wo = d["l0wproj_t"] if L == 0 else d["l1wo_t"]
            def OUTC(fc):
                return t["WS0"][:, fc * 2048:(fc + 1) * 2048] if fc < 6 else t["WS1"][:, (fc - 6) * 2048:(fc - 5) * 2048]
            QW = t["WS1"][:, 4096:8192]
            KW = t["WS1"][:, 8192:12288]

            YDMA(lambda y: y.dma_start(
                out=t["ROWS"][0:1, 0:1024],
                in_=(d["l0bproj"] if L == 0 else d["l1bo"])[None, :]))
            if L == 1:
                YDMA(lambda y: y.dma_start(out=t["ROWS"][0:1, 1024:2048], in_=d["l1bv"][None, :]))
            ev_rows = ("DA", prog.cnt["DA"])
            ev_bv = ev_rows if L == 1 else None

            aws_tread = [None, None]
            tmps_dread = None
            wave_tread = None
            negbt_sread = None
            nb16_tread = [None]
            expt_tread = [None, None]
            msb_vread = [None, None]
            out_evs = []

            def load_aws(buf, col0, src_ap, ncols):
                prog.wait("g", aws_tread[buf])
                if buf == 0:
                    prog.wait("g", tmps_dread)
                return GDMA(lambda g, col0=col0, src_ap=src_ap, ncols=ncols: g.dma_start(
                    out=AWS[:, buf * 2048 + col0:buf * 2048 + col0 + ncols]
                    .rearrange("p (ci q) -> p ci q", ci=8),
                    in_=src_ap), "DW%d" % buf)

            for w in range(4):
                if L == 1:
                    ev_q = load_aws(0, 0, wq[:, w * 256:(w + 1) * 256].rearrange("(ci p) q -> p ci q", p=128), 2048)
                    ev_k = load_aws(1, 0, d["l1wk_t"][:, w * 256:(w + 1) * 256].rearrange("(ci p) q -> p ci q", p=128), 2048)
                else:
                    ev_q = load_aws(0, 0, wq[:, w * 256:(w + 1) * 256].rearrange("(ci p) q -> p ci q", p=128), 2048)
                    ev_k = load_aws(1, 0, d["l0wkv_t"][:, 0:256].rearrange("(ci p) q -> p ci q", p=128), 2048) if w == 0 else None
                prog.waitall("t", lnf_evs)
                prog.wait("v", wave_tread)
                prog.wait("t", ev_q)
                for pc in range(2):
                    qc = 2 * w + pc
                    for tq in range(4):
                        pi, half = pc, tq % 2
                        bk.pe_begin(pi, (half,))
                        for ci in range(8):
                            last = ci == 7
                            TOP(lambda tt_, pi=pi, half=half, ci=ci, pc=pc, tq=tq, last=last: tt_.matmul(
                                P[pi][:, half * 512:(half + 1) * 512],
                                AWS[:, ci * 256 + pc * 128:ci * 256 + (pc + 1) * 128],
                                LNF[:, ci * T + tq * 512:ci * T + (tq + 1) * 512],
                                start=(ci == 0), stop=last), inc="T" if last else None)
                        bk.pe_done(pi, (half,), ("T", prog.cnt["T"]))
                        bk.rd_begin("v", pi, (half,))
                        if L == 1:
                            ev_e = VOP(lambda v, pi=pi, half=half, pc=pc, tq=tq, qc=qc: v.tensor_scalar(
                                out=QW[:, pc * 2048 + tq * 512:pc * 2048 + (tq + 1) * 512],
                                in0=P[pi][:, half * 512:(half + 1) * 512],
                                scalar1=t["BQKV"][:, qc:qc + 1], scalar2=0.125,
                                op0=OP.add, op1=OP.mult), inc="V")
                        else:
                            ev_e = VOP(lambda v, pi=pi, half=half, pc=pc, tq=tq: v.tensor_scalar(
                                out=QW[:, pc * 2048 + tq * 512:pc * 2048 + (tq + 1) * 512],
                                in0=P[pi][:, half * 512:(half + 1) * 512],
                                scalar1=0.125, scalar2=None, op0=OP.mult), inc="V")
                        bk.rd_done(pi, (half,), ev_e)
                ev_qprod = ("T", prog.cnt["T"])
                aws_tread[0] = ev_qprod
                if L == 1 or w == 0:
                    prog.wait("t", ev_k)
                    for pc in range(2):
                        qc = 2 * w + pc
                        for tq in range(4):
                            pi, half = pc, tq % 2
                            bk.pe_begin(pi, (half,))
                            for ci in range(8):
                                last = ci == 7
                                TOP(lambda tt_, pi=pi, half=half, ci=ci, pc=pc, tq=tq, last=last: tt_.matmul(
                                    P[pi][:, half * 512:(half + 1) * 512],
                                    AWS[:, 2048 + ci * 256 + pc * 128:2048 + ci * 256 + (pc + 1) * 128],
                                    LNF[:, ci * T + tq * 512:ci * T + (tq + 1) * 512],
                                    start=(ci == 0), stop=last), inc="T" if last else None)
                            bk.pe_done(pi, (half,), ("T", prog.cnt["T"]))
                            bk.rd_begin("v", pi, (half,))
                            if L == 1:
                                ev_e = VOP(lambda v, pi=pi, half=half, pc=pc, tq=tq, qc=qc: v.tensor_scalar(
                                    out=KW[:, pc * 2048 + tq * 512:pc * 2048 + (tq + 1) * 512],
                                    in0=P[pi][:, half * 512:(half + 1) * 512],
                                    scalar1=t["BQKV"][:, 8 + qc:8 + qc + 1], scalar2=None,
                                    op0=OP.add), inc="V")
                            else:
                                ev_e = VOP(lambda v, pi=pi, half=half, pc=pc, tq=tq: v.tensor_copy(
                                    KW[:, pc * 2048 + tq * 512:pc * 2048 + (tq + 1) * 512],
                                    P[pi][:, half * 512:(half + 1) * 512]), inc="V")
                            bk.rd_done(pi, (half,), ev_e)
                    aws_tread[1] = ("T", prog.cnt["T"])
                if L == 1:
                    ev_v = load_aws(0, 0, d["l1wv_t"][:, w * 256:(w + 1) * 256].rearrange("(ci p) q -> p ci q", p=128), 2048)
                    dv = 256
                elif w == 0:
                    ev_v = load_aws(1, 0, d["l0wkv_t"][:, 256:512].rearrange("(ci p) q -> p ci q", p=128), 2048)
                    dv = 256
                else:
                    ev_v = None
                if ev_v is not None:
                    vbuf = 0 if L == 1 else 1
                    if ev_bv is not None:
                        prog.wait("t", ev_bv)
                    prog.wait("v", wave_tread)
                    ev_ones = VOP(lambda v: v.memset(
                        VT.rearrange("p (tt s c) -> p tt s c", tt=16, s=4)[:, :, :, 64:65], 1.0), inc="V")
                    prog.wait("t", ev_v)
                    for tt in range(NTT):
                        pi, half = tt % 2, (tt // 2) % 2
                        bk.pe_begin(pi, (half,))
                        first = True
                        if L == 1:
                            TOP(lambda tt_, pi=pi, half=half, w=w: tt_.matmul(
                                P[pi][:, half * 512:half * 512 + 256],
                                t["ONESF"][0:1, 0:128],
                                t["ROWS"][0:1, 1024 + w * 256:1024 + (w + 1) * 256],
                                start=True, stop=False))
                            first = False
                        for ci in range(8):
                            last = ci == 7
                            TOP(lambda tt_, pi=pi, half=half, ci=ci, tt=tt, first=first, vbuf=vbuf, last=last: tt_.matmul(
                                P[pi][:, half * 512:half * 512 + 256],
                                LNF[:, ci * T + tt * 128:ci * T + (tt + 1) * 128],
                                AWS[:, vbuf * 2048 + ci * 256:vbuf * 2048 + (ci + 1) * 256],
                                start=(first and ci == 0), stop=last), inc="T" if last else None)
                        bk.pe_done(pi, (half,), ("T", prog.cnt["T"]))
                        bk.rd_begin("v", pi, (half,))
                        prog.wait("v", ev_ones)
                        dst = VT.rearrange("p (tt2 s c) -> p tt2 s c", tt2=16, s=4)[:, tt, 0:4, 0:64]
                        ev_e = VOP(lambda v, pi=pi, half=half, dst=dst: v.tensor_copy(
                            dst, P[pi][:, half * 512:half * 512 + 256]
                            .rearrange("p (s c) -> p s c", c=64)), inc="V")
                        bk.rd_done(pi, (half,), ev_e)
                    ev_vprod = ("T", prog.cnt["T"])
                    aws_tread[vbuf] = ev_vprod
                ev_qkv_v = ("V", prog.cnt["V"])

                for hh in range(4):
                    h = 4 * w + hh
                    pc = hh // 2
                    side = hh % 2
                    oc = h // 2
                    rows = slice(side * 64, (side + 1) * 64)
                    kwc = pc if L == 1 else (2 * w + pc) % 2
                    pav = 2 + (h % 2)
                    prog.wait("t", ev_qkv_v)
                    mb = h % 2
                    prog.wait("g", msb_vread[mb])
                    ev_msb = GDMA(lambda g, L=L, h=h, mb=mb: g.dma_start(
                        out=MSB[:, mb * 4224:mb * 4224 + 4224],
                        in_=d["masksh"][L, h][:, 0:4224]), "DWM%d" % mb)
                    prog.waitall("s", [expt_tread[0], expt_tread[1], ev_qkv_v])
                    ev_sq = SOP(lambda s, pc=pc, rows=rows: s.activation(
                        out=QSQ[rows, 0:2048], in_=QW[rows, pc * 2048:(pc + 1) * 2048],
                        func=AF.Square), inc="S")
                    bk.pe_begin(2, (0, 1))
                    bk.pe_begin(3, (0, 1))
                    prog.wait("t", ev_sq)
                    for j in range(4):
                        pi = 2 + j // 2
                        TOP(lambda tt_, pi=pi, j=j, rows=rows, side=side: tt_.matmul(
                            P[pi][0:1, (j % 2) * 512:(j % 2 + 1) * 512],
                            t["ONESB"][rows, 0:1],
                            QSQ[rows, j * 512:(j + 1) * 512],
                            start=True, stop=True), inc="T" if j == 3 else None)
                    ev_qn = ("T", prog.cnt["T"])
                    bk.pe_done(2, (0, 1), ev_qn)
                    bk.pe_done(3, (0, 1), ev_qn)
                    bk.rd_begin("v", 2, (0, 1))
                    bk.rd_begin("v", 3, (0, 1))
                    VOP(lambda v: v.reduce_max(out=BSC[0:1, 0:1], in_=P[2][0:1, 0:1024], axis=AX.X))
                    ev_r = VOP(lambda v: v.reduce_max(out=BSC[0:1, 1:2], in_=P[3][0:1, 0:1024], axis=AX.X), inc="V")
                    bk.rd_done(2, (0, 1), ev_r)
                    bk.rd_done(3, (0, 1), ev_r)
                    prog.drain("v")
                    VOP(lambda v: v.tensor_tensor(out=BSC[0:1, 2:3], in0=BSC[0:1, 0:1], in1=BSC[0:1, 1:2], op=OP.max))
                    prog.wait("s", ev_qn)
                    ev_sqk = SOP(lambda s, kwc=kwc, rows=rows: s.activation(
                        out=QSQ[rows, 0:2048], in_=KW[rows, kwc * 2048:(kwc + 1) * 2048],
                        func=AF.Square), inc="S")
                    bk.pe_begin(2, (0, 1))
                    bk.pe_begin(3, (0, 1))
                    prog.wait("t", ev_sqk)
                    for j in range(4):
                        pi = 2 + j // 2
                        TOP(lambda tt_, pi=pi, j=j, rows=rows: tt_.matmul(
                            P[pi][0:1, (j % 2) * 512:(j % 2 + 1) * 512],
                            t["ONESB"][rows, 0:1],
                            QSQ[rows, j * 512:(j + 1) * 512],
                            start=True, stop=True), inc="T" if j == 3 else None)
                    ev_kn = ("T", prog.cnt["T"])
                    bk.pe_done(2, (0, 1), ev_kn)
                    bk.pe_done(3, (0, 1), ev_kn)
                    bk.rd_begin("v", 2, (0, 1))
                    bk.rd_begin("v", 3, (0, 1))
                    VOP(lambda v: v.reduce_max(out=BSC[0:1, 3:4], in_=P[2][0:1, 0:1024], axis=AX.X))
                    ev_r2 = VOP(lambda v: v.reduce_max(out=BSC[0:1, 4:5], in_=P[3][0:1, 0:1024], axis=AX.X), inc="V")
                    bk.rd_done(2, (0, 1), ev_r2)
                    bk.rd_done(3, (0, 1), ev_r2)
                    prog.drain("v")
                    VOP(lambda v: v.tensor_tensor(out=BSC[0:1, 5:6], in0=BSC[0:1, 3:4], in1=BSC[0:1, 4:5], op=OP.max))
                    prog.drain("v")
                    VOP(lambda v: v.tensor_tensor(out=BSC[0:1, 6:7], in0=BSC[0:1, 2:3], in1=BSC[0:1, 5:6], op=OP.mult))
                    prog.drain("v")
                    VOP(lambda v: v.tensor_scalar(out=BSCI[0:1, 7:8], in0=BSCI[0:1, 6:7], scalar1=1, scalar2=None, op0=OP.arith_shift_right))
                    prog.drain("v")
                    VOP(lambda v: v.tensor_scalar(out=BSCI[0:1, 8:9], in0=BSCI[0:1, 7:8], scalar1=0x1FC00000, scalar2=None, op0=OP.add))
                    prog.drain("v")
                    VOP(lambda v: v.tensor_scalar(out=BSC[0:1, 9:10], in0=BSC[0:1, 8:9], scalar1=1.06, scalar2=None, op0=OP.mult))
                    prog.drain("v")
                    VOP(lambda v, h=h, L=L: v.tensor_tensor(
                        out=BSC[0:1, 11:12], in0=BSC[0:1, 9:10],
                        in1=t["PBMXT"][0:1, L * 16 + h:L * 16 + h + 1], op=OP.add))
                    prog.drain("v")
                    VOP(lambda v: v.tensor_scalar(out=BSC[0:1, 12:13], in0=BSC[0:1, 11:12], scalar1=-1.0, scalar2=None, op0=OP.mult))
                    prog.drain("v")
                    prog.wait("v", nb16_tread[0])
                    ev_nb16 = VOP(lambda v: v.tensor_copy(t["NB16"][0:1, 0:1], BSC[0:1, 12:13]), inc="V")
                    bk.pe_begin(2, (0,))
                    prog.wait("t", ev_nb16)
                    ev_bb = TOP(lambda tt_: tt_.matmul(
                        P[2][:, 0:1], t["ONESB"][0:1, 0:128], t["NB16"][0:1, 0:1],
                        start=True, stop=True), inc="T")
                    nb16_tread[0] = ev_bb
                    bk.pe_done(2, (0,), ev_bb)
                    bk.rd_begin("v", 2, (0,))
                    prog.wait("v", negbt_sread)
                    ev_negbt = VOP(lambda v: v.tensor_copy(t["NEGBT"][:, 0:1], P[2][:, 0:1]), inc="V")
                    bk.rd_done(2, (0,), ev_negbt)

                    for qh in range(2):
                        ev_exp_by_par = [None, None]
                        for sc in range(16):
                            par = sc % 2
                            c0 = 2047 + 1024 * qh - 128 * sc
                            bk.pe_begin(par, (0, 1))
                            for jj in range(2):
                                last = jj == 1
                                TOP(lambda tt_, par=par, rows=rows, sc=sc, jj=jj, qh=qh, kwc=kwc, pc=pc: tt_.matmul(
                                    P[par][:, jj * 512:(jj + 1) * 512],
                                    KW[rows, kwc * 2048 + sc * 128:kwc * 2048 + (sc + 1) * 128],
                                    QW[rows, pc * 2048 + qh * 1024 + jj * 512:pc * 2048 + qh * 1024 + (jj + 1) * 512],
                                    start=True, stop=True), inc="T" if last else None)
                            ev_mm = ("T", prog.cnt["T"])
                            bk.pe_done(par, (0, 1), ev_mm)
                            bk.rd_begin("v", par, (0, 1))
                            prog.wait("v", ev_msb)
                            ev_mask = VOP(lambda v, par=par, mb=mb, c0=c0: v.tensor_tensor(
                                out=P[par][:, 0:1024], in0=P[par][:, 0:1024],
                                in1=MSB[:, mb * 4224 + c0:mb * 4224 + c0 + 1024],
                                op=OP.add), inc="V")
                            msb_vread[mb] = ev_mask
                            prog.wait("s", ev_mask)
                            prog.wait("s", ev_negbt)
                            prog.wait("s", expt_tread[par])
                            ev_exp = SOP(lambda s, par=par: s.activation(
                                out=EXPT[:, par * 1024:(par + 1) * 1024], in_=P[par][:, 0:1024],
                                func=AF.Exp, bias=t["NEGBT"][:, 0:1]), inc="S")
                            negbt_sread = ev_exp
                            ev_exp_by_par[par] = ev_exp
                            bk.rd_done(par, (0, 1), ev_exp)
                            if sc == 0:
                                bk.pe_begin(pav, (0, 1))
                            prog.wait("t", ev_exp)
                            slot = (h - 4 * w) if L == 1 else (h % 4)
                            for jj in range(2):
                                last = jj == 1
                                TOP(lambda tt_, pav=pav, sc=sc, slot=slot, par=par, jj=jj: tt_.matmul(
                                    P[pav][0:65, jj * 512:(jj + 1) * 512],
                                    VT[:, sc * 260 + slot * 65:sc * 260 + slot * 65 + 65],
                                    EXPT[:, par * 1024 + jj * 512:par * 1024 + (jj + 1) * 512],
                                    start=(sc == 0), stop=(sc == 15)),
                                    inc="T" if last else None)
                            ev_av = ("T", prog.cnt["T"])
                            expt_tread[par] = ev_av
                            if sc == 15:
                                bk.pe_done(pav, (0, 1), ev_av)
                        bk.rd_begin("v", pav, (0, 1))
                        ev_rec = VOP(lambda v, pav=pav: v.reciprocal(
                            out=RECB[64:65, 0:1024], in_=P[pav][64:65, 0:1024]), inc="V")
                        bk.rd_done(pav, (0, 1), ev_rec)
                        psc = 0
                        bk.pe_begin(psc, (0, 1))
                        prog.wait("t", ev_rec)
                        for jj in range(2):
                            last = jj == 1
                            TOP(lambda tt_, psc=psc, jj=jj: tt_.matmul(
                                P[psc][:, jj * 512:(jj + 1) * 512],
                                t["ONESF"][64:65, 0:128],
                                RECB[64:65, jj * 512:(jj + 1) * 512],
                                start=True, stop=True), inc="T" if last else None)
                        ev_rb = ("T", prog.cnt["T"])
                        bk.pe_done(psc, (0, 1), ev_rb)
                        bk.rd_begin("v", psc, (0, 1))
                        ev_rc = VOP(lambda v, psc=psc: v.tensor_copy(
                            RECB[0:128, 0:1024], P[psc][:, 0:1024]), inc="V")
                        bk.rd_done(psc, (0, 1), ev_rc)
                        prog.wait("v", ev_rc)
                        bk.rd_begin("v", pav, (0, 1))
                        if side == 0:
                            ev_ev = VOP(lambda v, pav=pav, oc=oc, qh=qh: v.tensor_tensor(
                                out=OUTC(oc)[0:64, qh * 1024:(qh + 1) * 1024],
                                in0=P[pav][0:64, 0:1024], in1=RECB[0:64, 0:1024],
                                op=OP.mult), inc="V")
                            out_evs.append(ev_ev)
                        else:
                            prog.wait("v", tmps_dread)
                            ev_ev = VOP(lambda v, pav=pav: v.tensor_tensor(
                                out=t["TMPS"][0:64, 0:1024],
                                in0=P[pav][0:64, 0:1024], in1=RECB[0:64, 0:1024],
                                op=OP.mult), inc="V")
                            prog.wait("s", ev_ev)
                            ev_sh = BDMA(lambda s, oc=oc, qh=qh: s.dma_start(
                                out=OUTC(oc)[64:128, qh * 1024:(qh + 1) * 1024],
                                in_=t["TMPS"][0:64, 0:1024]))
                            tmps_dread = ev_sh
                            out_evs.append(ev_sh)
                        bk.rd_done(pav, (0, 1), ev_ev)
                wave_tread = ("T", prog.cnt["T"])

            prog.waitall("t", out_evs)
            prog.wait("t", ev_rows)
            for eh in range(2):
                prog.wait("g", aws_tread[0])
                prog.wait("g", aws_tread[1])
                prog.wait("g", tmps_dread)
                ev_w = GDMA(lambda g, eh=eh: g.dma_start(
                    out=AWS[:, 0:4096].rearrange("p (ci q) -> p ci q", ci=8),
                    in_=wo[:, eh * 512:(eh + 1) * 512].rearrange("(ci p) q -> p ci q", p=128)), "DW0")
                prog.wait("t", ev_w)
                for tt in range(NTT):
                    pi, half = tt % 2, (tt // 2) % 2
                    bk.pe_begin(pi, (half,))
                    TOP(lambda tt_, pi=pi, half=half, eh=eh: tt_.matmul(
                        P[pi][:, half * 512:(half + 1) * 512],
                        t["ONESF"][0:1, 0:128],
                        t["ROWS"][0:1, eh * 512:(eh + 1) * 512],
                        start=True, stop=False))
                    for fc in range(8):
                        last = fc == 7
                        TOP(lambda tt_, pi=pi, half=half, fc=fc, tt=tt, last=last: tt_.matmul(
                            P[pi][:, half * 512:(half + 1) * 512],
                            OUTC(fc)[:, tt * 128:(tt + 1) * 128],
                            AWS[:, fc * 512:(fc + 1) * 512],
                            start=False, stop=last), inc="T" if last else None)
                    bk.pe_done(pi, (half,), ("T", prog.cnt["T"]))
                    bk.rd_begin("v", pi, (half,))
                    ev_e = VOP(lambda v, pi=pi, half=half, tt=tt, eh=eh: v.tensor_tensor(
                        out=XT[:, tt * C + eh * 512:tt * C + (eh + 1) * 512],
                        in0=P[pi][:, half * 512:(half + 1) * 512],
                        in1=XT[:, tt * C + eh * 512:tt * C + (eh + 1) * 512],
                        op=OP.add), inc="V")
                    bk.rd_done(pi, (half,), ev_e)
                aws_tread[0] = ("T", prog.cnt["T"])
                aws_tread[1] = aws_tread[0]

        emit_init()
        prog.barrier()
        for sub in sub_ids:
            lnf_evs = emit_ln()
            if sub in (0, 2):
                ev_g = emit_gates(0, lnf_evs)
                emit_ffn(0, ev_g)
            elif sub in (3, 5):
                ev_g = emit_gates(1, lnf_evs)
                emit_ffn(1, ev_g)
            else:
                emit_attn(0 if sub == 1 else 1, lnf_evs)
            prog.barrier()
        ev_y = YDMA(lambda y: y.dma_start(
            out=yout.rearrange("(tt p) c -> p tt c", p=128),
            in_=XT[:].rearrange("p (tt c) -> p tt c", tt=NTT)))
        prog.wait("y", ev_y)
        for nm in DEBUG_TAPS:
            ap = t[nm][:] if hasattr(t[nm], '__enter__') or 'SBTensor' in type(t[nm]).__name__ else t[nm]
            dt_out = nc.dram_tensor("dbg_" + nm, [128, ap.shape[1]], F32, kind="ExternalOutput").ap()
            if ap.dtype == F32:
                ev_d = YDMA(lambda y, ap=ap, dt_out=dt_out: y.dma_start(out=dt_out, in_=ap))
                prog.wait("y", ev_d)
            else:
                ev_d = prog.op("g", lambda g, ap=ap, dt_out=dt_out: g.dma_start(out=dt_out, in_=ap), "DWI", 16)
                prog.wait("g", ev_d)

        with nc.Block() as block:
            def mk(engname):
                def _(eng):
                    for item in prog.ir[engname]:
                        if item[0] == "wait":
                            eng.wait_ge(sem[item[1]], item[2])
                        elif item[0] == "drain":
                            eng.drain()
                        else:
                            _, fn, inc, k = item
                            r = fn(eng)
                            if inc is not None:
                                r.then_inc(sem[inc], k)
                return _
            block.tensor(mk("t"))
            block.vector(mk("v"))
            block.scalar(mk("s"))
            block.gpsimd(mk("g"))
            block.sync(mk("y"))
    nc._lazy_input_names = list(d.keys())
    return nc


def _host_prepare(inputs):
    f = np.float32
    out = {}
    out["ident"] = np.eye(128, dtype=f)
    n_head = 16
    start = 2.0 ** (-(2.0 ** (-(np.log2(n_head) - 3.0))))
    slopes = (start ** np.arange(1, n_head + 1)).astype(np.float64)
    idx = np.arange(4095, dtype=np.float64)
    rel = 2047.0 - idx
    row = np.minimum(rel, 0.0)
    pb0 = (slopes[:, None] * row[None, :] * (64.0 ** -0.5)).astype(f)
    pb1 = np.asarray(inputs["L1_attn_pos_bias"], dtype=f)
    pbrows = np.stack([pb0, pb1], 0)
    pad = np.zeros((2, 16, 127 + 4224), dtype=f)
    pad[:, :, 127:127 + 4095] = pbrows
    masksh = np.empty((2, 16, 128, 4224), dtype=f)
    for p in range(128):
        masksh[:, :, p, :] = pad[:, :, 127 - p:127 - p + 4224]
    out["masksh"] = masksh
    out["pbrows"] = pbrows
    tr = lambda a: np.ascontiguousarray(np.asarray(a, dtype=f).T)
    out["l0wq_t"] = tr(inputs["L0_attn_Wq"])
    out["l0wkv_t"] = tr(inputs["L0_attn_Wkv"])
    out["l0wproj_t"] = tr(inputs["L0_attn_Wproj"])
    out["l0bproj"] = np.asarray(inputs["L0_attn_bproj"], f)
    out["l0wgate"] = np.asarray(inputs["L0_ffn_w_gate"], f)
    out["l0wgu"] = np.asarray(inputs["L0_ffn_w_gate_up"], f)
    out["l0wdown"] = np.asarray(inputs["L0_ffn_w_down"], f)
    out["l1wq_t"] = tr(inputs["L1_attn_Wq"])
    out["l1wk_t"] = tr(inputs["L1_attn_Wk"])
    out["l1wv_t"] = tr(inputs["L1_attn_Wv"])
    out["l1wo_t"] = tr(inputs["L1_attn_Wo"])
    bqkv = np.stack([
        np.asarray(inputs["L1_attn_bq"], f).reshape(8, 128).T,
        np.asarray(inputs["L1_attn_bk"], f).reshape(8, 128).T,
        np.asarray(inputs["L1_attn_bv"], f).reshape(8, 128).T,
    ], 1)
    out["l1bqkv"] = np.ascontiguousarray(bqkv.reshape(128, 24))
    out["l1bv"] = np.asarray(inputs["L1_attn_bv"], f)
    out["l1bo"] = np.asarray(inputs["L1_attn_bo"], f)
    out["l1routerW"] = np.asarray(inputs["L1_ffn_router_W"], f)
    out["l1routerb"] = np.asarray(inputs["L1_ffn_router_b"], f)
    out["l1w1"] = np.asarray(inputs["L1_ffn_W1"], f)
    out["l1b1t"] = np.ascontiguousarray(
        np.asarray(inputs["L1_ffn_b1"], f).reshape(8, 32, 128).transpose(2, 0, 1).reshape(128, 256))
    out["l1w2"] = np.asarray(inputs["L1_ffn_W2"], f)
    out["l1b2"] = np.asarray(inputs["L1_ffn_b2"], f)
    return out


_PROG_CACHE = {}


def kernel(**inputs):
    aux = _host_prepare(inputs)
    x = np.asarray(inputs["L0_attn_x"], np.float32)
    xs = [np.ascontiguousarray(x[b]) for b in range(NCORES)]
    for grp in GROUPS:
        key = tuple(grp)
        if key not in _PROG_CACHE:
            ncb = build_program(list(grp))
            _PROG_CACHE[key] = (ncb, list(ncb._lazy_input_names))
        nc, names = _PROG_CACHE[key]
        in_maps = []
        for b in range(NCORES):
            m = {nm: aux[nm] for nm in names if nm != "x"}
            m["x"] = xs[b]
            in_maps.append(m)
        res = run_bass_kernel_spmd(nc, in_maps, core_ids=list(range(NCORES)))
        xs = [np.asarray(res.results[b]["y"], np.float32) for b in range(NCORES)]
    return np.stack(xs, 0)


# version 10 (final)
# speedup vs baseline: 1.2722x; 1.2516x over previous
import contextlib
import numpy as np
import concourse.bass as bass
import concourse.mybir as mybir
from concourse.bass_utils import run_bass_kernel_spmd

F32 = mybir.dt.float32
BF16 = mybir.dt.bfloat16
I32 = mybir.dt.int32
AF = mybir.ActivationFunctionType
OP = mybir.AluOpType
AX = mybir.AxisListType

T = 2048
C = 1024
NTT = 16
NFC = 8
NCORES = 8

GROUPS = [[0, 1, 2, 3, 4, 5]]

ENGINES = ("t", "v", "s", "g", "y")
SEMS = ("T", "V", "S", "DWI", "DW0", "DW1", "DWM0", "DWM1", "DA", "DB1", "DB2",
        "DY0", "DY1", "DS0", "DS1")


class Prog:
    def __init__(self):
        self.ir = {e: [] for e in ENGINES}
        self.cnt = {s: 0 for s in SEMS}
        self.lastwait = {}

    def op(self, eng, fn, inc=None, k=1):
        ev = None
        if inc is not None:
            self.cnt[inc] += k
            ev = (inc, self.cnt[inc])
        self.ir[eng].append(("op", fn, inc, k))
        return ev

    def drain(self, eng):
        self.ir[eng].append(("drain",))

    def wait(self, eng, ev):
        if ev is None:
            return
        sem, val = ev
        key = (eng, sem)
        if self.lastwait.get(key, 0) >= val:
            return
        self.lastwait[key] = val
        self.ir[eng].append(("wait", sem, val))

    def waitall(self, eng, evs):
        for ev in evs:
            self.wait(eng, ev)

    def barrier(self):
        evs = [(s, self.cnt[s]) for s in SEMS if self.cnt[s] > 0]
        for e in ENGINES:
            for ev in evs:
                self.wait(e, ev)


class BankTrack:

    def __init__(self, prog):
        self.prog = prog
        self.w = {}
        self.r = {}

    def pe_begin(self, pi, halves):
        for h in halves:
            for ev in self.r.pop((pi, h), []):
                self.prog.wait("t", ev)

    def pe_done(self, pi, halves, ev):
        for h in halves:
            self.w[(pi, h)] = ev

    def rd_begin(self, eng, pi, halves):
        for h in halves:
            self.prog.wait(eng, self.w.get((pi, h)))

    def rd_done(self, pi, halves, ev):
        for h in halves:
            self.r.setdefault((pi, h), []).append(ev)


DEBUG_TAPS = []
DEBUG_XT = False

INPUT_DTYPES = {"masksh": "bf16", "l0wgu": "bf16", "l0wdown": "bf16",
                "l1w1": "bf16", "l1w2": "bf16"}

INPUT_SHAPES = {
    "x": (T, C), "ident": (128, 128), "masksh": (2, 16, 128, 4224), "pbrows": (2, 16, 4095),
    "l0wq_t": (C, C), "l0wkv_t": (C, 512), "l0wproj_t": (C, C), "l0bproj": (C,), "l0wgate": (C, 8),
    "l0wgu": (8, C, 8192), "l0wdown": (8, 4096, C),
    "l1wq_t": (C, C), "l1wk_t": (C, C), "l1wv_t": (C, C), "l1wo_t": (C, C),
    "l1bqkv": (128, 24), "l1bv": (C,), "l1bo": (C,),
    "l1routerW": (C, 8), "l1routerb": (8,),
    "l1w1": (8, C, 4096), "l1b1t": (128, 256), "l1w2": (8, 4096, C), "l1b2": (8, C),
}


class LazyIns(dict):
    def __init__(self, nc):
        super().__init__()
        self.nc = nc

    def __missing__(self, name):
        dt = BF16 if INPUT_DTYPES.get(name) == "bf16" else F32
        ap = self.nc.dram_tensor(name, list(INPUT_SHAPES[name]), dt, kind="ExternalInput").ap()
        self[name] = ap
        return ap


def build_program(sub_ids):
    nc = bass.Bass("TRN2", target_bir_lowering=False)
    d = LazyIns(nc)
    yout = nc.dram_tensor("y", [T, C], F32, kind="ExternalOutput").ap()

    with contextlib.ExitStack() as st:
        t = {}

        def sb(name, cols, dtype):
            t[name] = st.enter_context(nc.sbuf_tensor(name, [128, cols], dtype))

        sb("XT", NTT * C, F32)
        sb("LNF", NFC * T, BF16)
        sb("LNT", 2 * C, BF16)
        sb("XSQ", C, F32)
        sb("STAT", 32, F32)
        sb("ONESB", 128, BF16)
        sb("ONESF", 128, F32)
        sb("IDENT", 128, BF16)
        sb("ROWS", 2176, F32)
        sb("BQKV", 24, F32)
        sb("B1T", 256, F32)
        sb("B2B", 1024, BF16)
        sb("WGATE0", 64, BF16)
        sb("WGATE1", 64, BF16)
        sb("WS0", 12288, BF16)
        sb("WS1", 12288, BF16)
        sb("SCR", 20800, BF16)
        sb("BSC", 16, F32)
        sb("NEGBT", 4, F32)
        sb("NB16", 4, BF16)
        sb("PBMXT", 32, F32)
        sb("PBMB", 32, F32)

        P = [st.enter_context(nc.psum_tensor(f"P{i}", [128, 1024], F32)) for i in range(4)]
        sem = {s: st.enter_context(nc.semaphore(s)) for s in SEMS}

        SCR = t["SCR"]
        t["ACT"] = SCR[:, 0:4096]
        t["SG0"] = SCR[:, 4096:5120].bitcast(F32)
        t["SG1"] = SCR[:, 5120:6144].bitcast(F32)
        t["GT"] = SCR[:, 6144:6400].bitcast(F32)
        t["GTB"] = SCR[:, 6400:6408]
        t["GSC"] = SCR[:, 6464:6592].bitcast(F32)
        t["GTT"] = SCR[:, 6656:8704]
        t["MSB"] = SCR[:, 0:8448]
        t["AWS"] = SCR[:, 8448:12544]
        t["TMPS"] = SCR[:, 8448:9472]
        t["VT"] = SCR[:, 12544:16704]
        t["EXPT"] = SCR[:, 16704:18752]
        t["QSQ"] = SCR[:, 16704:18752]
        t["RECB"] = SCR[:, 18752:20800].bitcast(F32)

        prog = Prog()
        bk = BankTrack(prog)
        XT, LNF, ACT = t["XT"], t["LNF"], t["ACT"]

        def TOP(fn, inc=None):
            return prog.op("t", fn, inc)

        def VOP(fn, inc=None):
            return prog.op("v", fn, inc)

        def SOP(fn, inc=None):
            return prog.op("s", fn, inc)

        def GDMA(fn, sm="DWI"):
            return prog.op("g", fn, sm, 16)

        def YDMA(fn, sm="DA"):
            return prog.op("y", fn, sm, 16)

        def BDMA(fn, sm="DB2"):
            return prog.op("s", fn, sm, 16)

        def emit_init():
            VOP(lambda v: v.memset(t["ONESB"][:], 1.0))
            VOP(lambda v: v.memset(t["ONESF"][:], 1.0))
            VOP(lambda v: v.memset(t["STAT"][:, 16:17], 1e-5))
            ev_id = GDMA(lambda g: g.dma_start(out=t["IDENT"][:], in_=d["ident"]))
            YDMA(lambda y: y.dma_start(
                out=XT[:].rearrange("p (tt c) -> p tt c", tt=NTT),
                in_=d["x"].rearrange("(tt p) c -> p tt c", p=128)))
            if any(s in (3, 4, 5) for s in sub_ids):
                YDMA(lambda y: y.dma_start(out=t["BQKV"][:], in_=d["l1bqkv"]))
            if any(s in (3, 5) for s in sub_ids):
                YDMA(lambda y: y.dma_start(out=t["ROWS"][0:1, 2048:2056], in_=d["l1routerb"][None, :]))
                YDMA(lambda y: y.dma_start(out=t["B1T"][:], in_=d["l1b1t"]))
                GDMA(lambda g: g.dma_start(out=t["B2B"][0:8, :], in_=d["l1b2"]))
                GDMA(lambda g: g.dma_start(
                    out=t["WGATE1"][:].rearrange("p (ci e) -> p ci e", ci=8),
                    in_=d["l1routerW"].rearrange("(ci p) e -> p ci e", p=128)))
            if any(s in (0, 2) for s in sub_ids):
                GDMA(lambda g: g.dma_start(
                    out=t["WGATE0"][:].rearrange("p (ci e) -> p ci e", ci=8),
                    in_=d["l0wgate"].rearrange("(ci p) e -> p ci e", p=128)))
            attn_ls = [0 if s == 1 else 1 for s in sub_ids if s in (1, 4)]
            ev_t_prev = None
            for L in sorted(set(attn_ls)):
                prog.wait("g", ev_t_prev)
                GDMA(lambda g, L=L: g.dma_start(
                    out=t["MSB"][0:16, 0:4095], in_=d["pbrows"][L]))
                prog.wait("v", ("DWI", prog.cnt["DWI"]))
                prog.wait("v", ev_t_prev)
                ev_r = VOP(lambda v: v.tensor_reduce(
                    out=t["MSB"][0:16, 6000:6001], in_=t["MSB"][0:16, 0:4095],
                    axis=AX.X, op=OP.max), inc="V")
                bk.pe_begin(0, (0,))
                prog.wait("t", ev_r)
                prog.wait("t", ("DWI", prog.cnt["DWI"]))
                ev_t = TOP(lambda tt_: tt_.transpose(
                    P[0].bitcast(BF16)[0:1, 0:16], t["MSB"][0:16, 6000:6001],
                    t["IDENT"][0:16, 0:16]), inc="T")
                ev_t_prev = ev_t
                bk.pe_done(0, (0,), ev_t)
                bk.rd_begin("v", 0, (0,))
                ev_v = VOP(lambda v, L=L: v.tensor_copy(
                    t["PBMXT"][0:1, L * 16:L * 16 + 16],
                    P[0].bitcast(BF16)[0:1, 0:16]), inc="V")
                bk.rd_done(0, (0,), ev_v)
                prog.wait("g", ev_v)
            if ev_t_prev is not None:
                bk.pe_begin(1, (0,))
                prog.wait("t", ("V", prog.cnt["V"]))
                ev_bb = TOP(lambda tt_: tt_.matmul(
                    P[1][:, 0:32], t["ONESF"][0:1, 0:128], t["PBMXT"][0:1, 0:32],
                    start=True, stop=True), inc="T")
                bk.pe_done(1, (0,), ev_bb)
                bk.rd_begin("v", 1, (0,))
                ev_pb = VOP(lambda v: v.tensor_copy(t["PBMB"][:], P[1][:, 0:32]), inc="V")
                bk.rd_done(1, (0,), ev_pb)

        def emit_ln():
            lnf_evs = []
            lnt_tread = [None, None]
            stat_sread = [None, None]
            xsq_vread = [None]
            for tt in range(NTT):
                xs = XT[:, tt * C:(tt + 1) * C]
                r = tt % 2
                stat = t["STAT"][:, r * 8:r * 8 + 8]
                prog.wait("v", stat_sread[r])
                VOP(lambda v, xs=xs, stat=stat: v.reduce_sum(out=stat[:, 0:1], in_=xs, axis=AX.X))
                prog.drain("v")
                ev0 = VOP(lambda v, stat=stat: v.tensor_scalar(
                    out=stat[:, 2:3], in0=stat[:, 0:1], scalar1=1.0 / C, scalar2=None,
                    op0=OP.mult), inc="V")
                prog.drain("v")
                VOP(lambda v, stat=stat: v.tensor_tensor(
                    out=stat[:, 4:5], in0=stat[:, 2:3], in1=stat[:, 2:3], op=OP.mult))
                prog.wait("s", xsq_vread[0])
                ev_sq = SOP(lambda s, xs=xs: s.activation(out=t["XSQ"][:], in_=xs, func=AF.Square), inc="S")
                prog.wait("v", ev_sq)
                ev_xv = VOP(lambda v, stat=stat: v.reduce_sum(out=stat[:, 1:2], in_=t["XSQ"][:], axis=AX.X), inc="V")
                xsq_vread[0] = ev_xv
                prog.drain("v")
                VOP(lambda v, stat=stat: v.tensor_scalar(
                    out=stat[:, 3:4], in0=stat[:, 1:2], scalar1=1.0 / C, scalar2=None, op0=OP.mult))
                prog.drain("v")
                ev_var = VOP(lambda v, stat=stat: v.tensor_tensor(
                    out=stat[:, 5:6], in0=stat[:, 3:4], in1=stat[:, 4:5], op=OP.subtract), inc="V")
                prog.wait("s", ev_var)
                SOP(lambda s, stat=stat: s.activation(
                    out=stat[:, 6:7], in_=stat[:, 5:6], func=AF.Ln, bias=t["STAT"][:, 16:17]))
                prog.drain("s")
                ev_rstd = SOP(lambda s, stat=stat: s.activation(
                    out=stat[:, 7:8], in_=stat[:, 6:7], func=AF.Exp, scale=-0.5), inc="S")
                stat_sread[r] = ev_rstd
                prog.wait("v", ev_rstd)
                lt = t["LNT"][:, r * C:(r + 1) * C]
                prog.wait("v", lnt_tread[r])
                ev_lnt = VOP(lambda v, xs=xs, stat=stat, lt=lt: v.tensor_scalar(
                    out=lt, in0=xs, scalar1=stat[:, 2:3], scalar2=stat[:, 7:8],
                    op0=OP.subtract, op1=OP.mult), inc="V")
                pi = tt % 2
                bk.pe_begin(pi, (0,))
                prog.wait("t", ev_lnt)
                pt = P[pi].bitcast(BF16)
                for fc in range(NFC):
                    last = fc == NFC - 1
                    TOP(lambda tt_, pt=pt, lt=lt, fc=fc: tt_.transpose(
                        pt[:, fc * 128:(fc + 1) * 128], lt[:, fc * 128:(fc + 1) * 128],
                        t["IDENT"][:]), inc="T" if last else None)
                ev_tr = ("T", prog.cnt["T"])
                lnt_tread[r] = ev_tr
                bk.pe_done(pi, (0,), ev_tr)
                bk.rd_begin("v", pi, (0,))
                dst = LNF[:].rearrange("p (fc tx) -> p fc tx", fc=NFC)[:, :, tt * 128:(tt + 1) * 128]
                ev_ev = VOP(lambda v, pt=pt, dst=dst: v.tensor_copy(
                    dst, pt[:, 0:1024].rearrange("p (fc q) -> p fc q", fc=8)), inc="V")
                bk.rd_done(pi, (0,), ev_ev)
                lnf_evs.append(ev_ev)
            return lnf_evs

        def emit_gates(L, lnf_evs):
            wg = t["WGATE0"] if L == 0 else t["WGATE1"]
            gts = t["GT"]
            ev_g = None
            gsc_sread = [None]
            gsc_vdone = [None]
            gtb_tread = [None]
            for tt in range(NTT):
                pi = 2 + tt % 2
                bk.pe_begin(pi, (0,))
                prog.wait("t", lnf_evs[tt])
                first = True
                if L == 1:
                    TOP(lambda tt_, pi=pi: tt_.matmul(
                        P[pi][:, 0:8], t["ONESF"][0:1, 0:128], t["ROWS"][0:1, 2048:2056],
                        start=True, stop=False))
                    first = False
                for ci in range(8):
                    last = ci == 7
                    TOP(lambda tt_, pi=pi, ci=ci, tt=tt, wg=wg, first=first, last=last: tt_.matmul(
                        P[pi][:, 0:8],
                        LNF[:, ci * T + tt * 128:ci * T + (tt + 1) * 128],
                        wg[:, ci * 8:(ci + 1) * 8],
                        start=(first and ci == 0), stop=last), inc="T" if last else None)
                bk.pe_done(pi, (0,), ("T", prog.cnt["T"]))
                bk.rd_begin("v", pi, (0,))
                g = t["GSC"]
                prog.wait("v", gsc_sread[0])
                ev_m = VOP(lambda v, pi=pi: v.reduce_max(
                    out=g[:, 0:1], in_=P[pi][:, 0:8], axis=AX.X, negate=True), inc="V")
                bk.rd_done(pi, (0,), ev_m)
                bk.rd_begin("s", pi, (0,))
                prog.wait("s", ev_m)
                prog.wait("s", gsc_vdone[0])
                ev_e = SOP(lambda s, pi=pi: s.activation(
                    out=g[:, 8:16], in_=P[pi][:, 0:8], func=AF.Exp, bias=g[:, 0:1]), inc="S")
                gsc_sread[0] = ev_e
                bk.rd_done(pi, (0,), ev_e)
                prog.wait("v", ev_e)
                VOP(lambda v: v.reduce_sum(out=g[:, 1:2], in_=g[:, 8:16], axis=AX.X))
                prog.drain("v")
                VOP(lambda v: v.reciprocal(out=g[:, 2:3], in_=g[:, 1:2]))
                prog.drain("v")
                VOP(lambda v: v.tensor_scalar(
                    out=g[:, 16:24], in0=g[:, 8:16], scalar1=g[:, 2:3], scalar2=None, op0=OP.mult))
                prog.drain("v")
                pr = g[:, 16:24]
                if L == 0:
                    VOP(lambda v, pr=pr: v.reduce_max(out=g[:, 3:4], in_=pr, axis=AX.X))
                    prog.drain("v")
                    VOP(lambda v, pr=pr: v.tensor_scalar(
                        out=g[:, 24:32], in0=pr, scalar1=g[:, 3:4], scalar2=None, op0=OP.is_ge))
                    prog.drain("v")
                    VOP(lambda v, pr=pr: v.tensor_tensor(out=g[:, 32:40], in0=pr, in1=g[:, 24:32], op=OP.mult))
                    prog.drain("v")
                    VOP(lambda v, pr=pr: v.tensor_tensor(out=g[:, 40:48], in0=pr, in1=g[:, 32:40], op=OP.subtract))
                    prog.drain("v")
                    VOP(lambda v: v.reduce_max(out=g[:, 4:5], in_=g[:, 40:48], axis=AX.X))
                    prog.drain("v")
                    VOP(lambda v, pr=pr: v.tensor_scalar(
                        out=g[:, 48:56], in0=pr, scalar1=g[:, 4:5], scalar2=None, op0=OP.is_ge))
                    prog.drain("v")
                    VOP(lambda v, pr=pr: v.tensor_tensor(out=g[:, 56:64], in0=pr, in1=g[:, 48:56], op=OP.mult))
                    prog.drain("v")
                    VOP(lambda v: v.reduce_sum(out=g[:, 5:6], in_=g[:, 56:64], axis=AX.X))
                    prog.drain("v")
                    VOP(lambda v: v.tensor_scalar(out=g[:, 6:7], in0=g[:, 5:6], scalar1=1e-9, scalar2=None, op0=OP.max))
                    prog.drain("v")
                    VOP(lambda v: v.reciprocal(out=g[:, 7:8], in_=g[:, 6:7]))
                    prog.drain("v")
                    VOP(lambda v: v.tensor_scalar(out=g[:, 7:8], in0=g[:, 7:8], scalar1=0.5, scalar2=None, op0=OP.mult))
                    prog.drain("v")
                    ev_g = VOP(lambda v, tt=tt: v.tensor_scalar(
                        out=gts[:, tt * 8:(tt + 1) * 8], in0=g[:, 56:64],
                        scalar1=g[:, 7:8], scalar2=None, op0=OP.mult), inc="V")
                    gsc_vdone[0] = ev_g
                else:
                    VOP(lambda v, pr=pr: v.reduce_max(out=g[:, 3:4], in_=pr, axis=AX.X))
                    prog.drain("v")
                    VOP(lambda v, pr=pr: v.tensor_scalar(
                        out=g[:, 24:32], in0=pr, scalar1=g[:, 3:4], scalar2=None, op0=OP.is_ge))
                    prog.drain("v")
                    VOP(lambda v, pr=pr: v.tensor_tensor(out=g[:, 32:40], in0=pr, in1=g[:, 24:32], op=OP.mult))
                    prog.drain("v")
                    ev_g = VOP(lambda v, tt=tt: v.tensor_scalar(
                        out=gts[:, tt * 8:(tt + 1) * 8], in0=g[:, 32:40],
                        scalar1=0.5, scalar2=None, op0=OP.mult), inc="V")
                    gsc_vdone[0] = ev_g
                    prog.drain("v")
                    prog.wait("v", gtb_tread[0])
                    ev_c = VOP(lambda v, tt=tt: v.tensor_copy(
                        t["GTB"][:, 0:8], gts[:, tt * 8:(tt + 1) * 8]), inc="V")
                    bk.pe_begin(pi, (1,))
                    prog.wait("t", ev_c)
                    ev_tr = TOP(lambda tt_, pi=pi: tt_.transpose(
                        P[pi].bitcast(BF16)[0:8, 1024:1152], t["GTB"][:, 0:8],
                        t["IDENT"][:]), inc="T")
                    gtb_tread[0] = ev_tr
                    bk.pe_done(pi, (1,), ev_tr)
                    bk.rd_begin("v", pi, (1,))
                    ev_gt = VOP(lambda v, tt=tt, pi=pi: v.tensor_copy(
                        t["GTT"][0:8, tt * 128:(tt + 1) * 128],
                        P[pi].bitcast(BF16)[0:8, 1024:1152]), inc="V")
                    bk.rd_done(pi, (1,), ev_gt)
                    ev_g = ev_gt
            return ev_g

        def emit_ffn(L, ev_gates):
            E = 8
            NWIN, HCW = 8, 4
            ws = [t["WS0"], t["WS1"]]
            ws_consumed = [None, None]
            ws_ready = [None, None]

            def issue_window(idx):
                e, win = idx // NWIN, idx % NWIN
                buf = idx % 2
                sm = "DW%d" % buf
                w = ws[buf]
                prog.wait("g", ws_consumed[buf])
                prog.wait("y", ws_consumed[buf])
                if L == 0:
                    prog.wait("s", ws_consumed[buf])
                    ev1 = GDMA(lambda g, e=e, win=win, w=w: g.dma_start(
                        out=w[:, 0:4096].rearrange("p (ci q) -> p ci q", ci=8),
                        in_=d["l0wgu"][e, :, win * 512:(win + 1) * 512]
                        .rearrange("(ci p) q -> p ci q", p=128)), sm)
                    ev2 = BDMA(lambda s, e=e, win=win, w=w: s.dma_start(
                        out=w[:, 4096:8192].rearrange("p (ci q) -> p ci q", ci=8),
                        in_=d["l0wgu"][e, :, 4096 + win * 512:4096 + (win + 1) * 512]
                        .rearrange("(ci p) q -> p ci q", p=128)), "DS%d" % buf)
                    ev3 = YDMA(lambda y, e=e, win=win, w=w: y.dma_start(
                        out=w[:, 8192:12288].rearrange("p (hc q) -> p hc q", hc=4),
                        in_=d["l0wdown"][e, win * 512:(win + 1) * 512, :]
                        .rearrange("(hc p) q -> p hc q", p=128)), "DY%d" % buf)
                    ws_ready[buf] = [ev1, ev2, ev3]
                else:
                    ev1 = GDMA(lambda g, e=e, win=win, w=w: g.dma_start(
                        out=w[:, 0:4096].rearrange("p (ci q) -> p ci q", ci=8),
                        in_=d["l1w1"][e, :, win * 512:(win + 1) * 512]
                        .rearrange("(ci p) q -> p ci q", p=128)), sm)
                    ev2 = YDMA(lambda y, e=e, win=win, w=w: y.dma_start(
                        out=w[:, 4096:8192].rearrange("p (hc q) -> p hc q", hc=4),
                        in_=d["l1w2"][e, win * 512:(win + 1) * 512, :]
                        .rearrange("(hc p) q -> p hc q", p=128)), "DY%d" % buf)
                    ws_ready[buf] = [ev1, ev2]

            evict_ev = [None] * NTT
            if L == 1:
                prog.wait("t", ev_gates)
                for grp in range(8):
                    for i in range(2):
                        tt = grp * 2 + i
                        bk.pe_begin(i, (0, 1))
                        for dh in range(2):
                            last = dh == 1
                            TOP(lambda tt_, i=i, tt=tt, dh=dh: tt_.matmul(
                                P[i][:, dh * 512:(dh + 1) * 512],
                                t["GTT"][0:8, tt * 128:(tt + 1) * 128],
                                t["B2B"][0:8, dh * 512:(dh + 1) * 512],
                                start=True, stop=True), inc="T" if last else None)
                        bk.pe_done(i, (0, 1), ("T", prog.cnt["T"]))
                        bk.rd_begin("v", i, (0, 1))
                        ev_e = VOP(lambda v, i=i, tt=tt: v.tensor_tensor(
                            out=XT[:, tt * C:(tt + 1) * C], in0=P[i][:, 0:1024],
                            in1=XT[:, tt * C:(tt + 1) * C], op=OP.add), inc="V")
                        evict_ev[tt] = ev_e
                        bk.rd_done(i, (0, 1), ev_e)

            issue_window(0)
            act_read = [None] * 8
            sg_read = [None, None]
            for e in range(E):
                for win in range(NWIN):
                    idx = e * NWIN + win
                    buf = idx % 2
                    if idx + 1 < E * NWIN:
                        issue_window(idx + 1)
                    w = ws[buf]
                    prog.waitall("t", ws_ready[buf])
                    for t4 in range(4):
                        act_evs = [None] * HCW
                        for hc in range(HCW):
                            rot = hc % 2
                            pi = 2 + rot
                            if L == 0:
                                bk.pe_begin(pi, (0,))
                                for ci in range(8):
                                    last = ci == 7
                                    TOP(lambda tt_, w=w, ci=ci, hc=hc, t4=t4, pi=pi, last=last: tt_.matmul(
                                        P[pi][:, 0:512],
                                        w[:, ci * 512 + hc * 128:ci * 512 + (hc + 1) * 128],
                                        LNF[:, ci * T + t4 * 512:ci * T + (t4 + 1) * 512],
                                        start=(ci == 0), stop=last), inc="T" if last else None)
                                bk.pe_done(pi, (0,), ("T", prog.cnt["T"]))
                                bk.pe_begin(pi, (1,))
                                for ci in range(8):
                                    last = ci == 7
                                    TOP(lambda tt_, w=w, ci=ci, hc=hc, t4=t4, pi=pi, last=last: tt_.matmul(
                                        P[pi][:, 512:1024],
                                        w[:, 4096 + ci * 512 + hc * 128:4096 + ci * 512 + (hc + 1) * 128],
                                        LNF[:, ci * T + t4 * 512:ci * T + (t4 + 1) * 512],
                                        start=(ci == 0), stop=last), inc="T" if last else None)
                                bk.pe_done(pi, (1,), ("T", prog.cnt["T"]))
                                bk.rd_begin("s", pi, (0,))
                                prog.wait("s", sg_read[rot])
                                sg = t["SG%d" % rot]
                                ev_sg = SOP(lambda s, pi=pi, sg=sg: s.activation(
                                    out=sg[:], in_=P[pi][:, 0:512], func=AF.Silu), inc="S")
                                bk.rd_done(pi, (0,), ev_sg)
                                bk.rd_begin("v", pi, (1,))
                                prog.wait("v", ev_sg)
                                prog.wait("v", act_read[hc])
                                ev_act = VOP(lambda v, pi=pi, sg=sg, hc=hc: v.tensor_tensor(
                                    out=ACT[:, hc * 512:(hc + 1) * 512],
                                    in0=P[pi][:, 512:1024], in1=sg[:], op=OP.mult), inc="V")
                                sg_read[rot] = ev_act
                                bk.rd_done(pi, (1,), ev_act)
                                act_ready_ev = ev_act
                            else:
                                bk.pe_begin(pi, (0,))
                                for ci in range(8):
                                    last = ci == 7
                                    TOP(lambda tt_, w=w, ci=ci, hc=hc, t4=t4, pi=pi, last=last: tt_.matmul(
                                        P[pi][:, 0:512],
                                        w[:, ci * 512 + hc * 128:ci * 512 + (hc + 1) * 128],
                                        LNF[:, ci * T + t4 * 512:ci * T + (t4 + 1) * 512],
                                        start=(ci == 0), stop=last), inc="T" if last else None)
                                bk.pe_done(pi, (0,), ("T", prog.cnt["T"]))
                                bk.rd_begin("s", pi, (0,))
                                prog.wait("s", act_read[hc])
                                hcg = win * HCW + hc
                                ev_act = SOP(lambda s, pi=pi, hc=hc, e=e, hcg=hcg: s.activation(
                                    out=ACT[:, hc * 512:(hc + 1) * 512],
                                    in_=P[pi][:, 0:512], func=AF.Relu,
                                    bias=t["B1T"][:, e * 32 + hcg:e * 32 + hcg + 1]), inc="S")
                                bk.rd_done(pi, (0,), ev_act)
                                act_ready_ev = ev_act
                            act_evs[hc] = act_ready_ev
                        for half in range(2):
                            bk.pe_begin(0, (0, 1))
                            bk.pe_begin(1, (0, 1))
                            for hc in range(HCW):
                                prog.wait("t", act_evs[hc])
                                for i in range(2):
                                    ttl = half * 2 + i
                                    for dh in range(2):
                                        last = hc == HCW - 1 and i == 1 and dh == 1
                                        wdo = 8192 if L == 0 else 4096
                                        TOP(lambda tt_, w=w, hc=hc, i=i, ttl=ttl, dh=dh, wdo=wdo: tt_.matmul(
                                            P[i][:, dh * 512:(dh + 1) * 512],
                                            ACT[:, hc * 512 + ttl * 128:hc * 512 + (ttl + 1) * 128],
                                            w[:, wdo + hc * 1024 + dh * 512:wdo + hc * 1024 + (dh + 1) * 512],
                                            start=(hc == 0), stop=(hc == HCW - 1)),
                                            inc="T" if last else None)
                            ev_dn = ("T", prog.cnt["T"])
                            bk.pe_done(0, (0, 1), ev_dn)
                            bk.pe_done(1, (0, 1), ev_dn)
                            if half == 1:
                                for hc in range(HCW):
                                    act_read[hc] = ev_dn
                            prog.wait("v", ev_gates)
                            for i in range(2):
                                tt = t4 * 4 + half * 2 + i
                                bk.rd_begin("v", i, (0, 1))
                                prog.wait("v", evict_ev[tt])
                                ev_e = VOP(lambda v, i=i, tt=tt, e=e: v.scalar_tensor_tensor(
                                    out=XT[:, tt * C:(tt + 1) * C], in0=P[i][:, 0:1024],
                                    scalar=t["GT"][:, tt * 8 + e:tt * 8 + e + 1],
                                    in1=XT[:, tt * C:(tt + 1) * C],
                                    op0=OP.mult, op1=OP.add), inc="V")
                                evict_ev[tt] = ev_e
                                bk.rd_done(i, (0, 1), ev_e)
                    ws_consumed[buf] = ("T", prog.cnt["T"])

        def emit_attn(L, lnf_evs):
            AWS, MSB, VT, EXPT, QSQ, RECB = t["AWS"], t["MSB"], t["VT"], t["EXPT"], t["QSQ"], t["RECB"]
            BSC, BSCI = t["BSC"], t["BSC"].bitcast(I32)
            wq = d["l0wq_t"] if L == 0 else d["l1wq_t"]
            wo = d["l0wproj_t"] if L == 0 else d["l1wo_t"]
            def OUTC(fc):
                return t["WS0"][:, fc * 2048:(fc + 1) * 2048] if fc < 6 else t["WS1"][:, (fc - 6) * 2048:(fc - 5) * 2048]
            QW = t["WS1"][:, 4096:8192]
            KW = t["WS1"][:, 8192:12288]

            YDMA(lambda y: y.dma_start(
                out=t["ROWS"][0:1, 0:1024],
                in_=(d["l0bproj"] if L == 0 else d["l1bo"])[None, :]))
            if L == 1:
                YDMA(lambda y: y.dma_start(out=t["ROWS"][0:1, 1024:2048], in_=d["l1bv"][None, :]))
            ev_rows = ("DA", prog.cnt["DA"])
            ev_bv = ev_rows if L == 1 else None

            aws_tread = [None, None]
            tmps_dread = None
            wave_tread = None
            wave_sq_sread = [None]
            negbt_sread = None
            expt_tread = [None, None]
            msb_tread = [None, None]
            msb_pending = [None, None]
            recrow_tread = [None]
            mk_ev = [None]
            out_evs = []

            def load_aws(buf, col0, src_ap, ncols):
                prog.wait("g", aws_tread[buf])
                if buf == 0:
                    prog.wait("g", tmps_dread)
                return GDMA(lambda g, col0=col0, src_ap=src_ap, ncols=ncols: g.dma_start(
                    out=AWS[:, buf * 2048 + col0:buf * 2048 + col0 + ncols]
                    .rearrange("p (ci q) -> p ci q", ci=8),
                    in_=src_ap), "DW%d" % buf)

            def issue_msb(h):
                mb = h % 2
                prog.wait("g", msb_tread[mb])
                msb_pending[mb] = GDMA(lambda g, h=h, mb=mb: g.dma_start(
                    out=MSB[:, mb * 4096:mb * 4096 + 4096],
                    in_=d["masksh"][L, h][:, 64:4160]), "DWM%d" % mb)

            def side_sums(rows, outcol):
                bk.pe_begin(2, (0, 1))
                bk.pe_begin(3, (0, 1))
                for c in range(4):
                    pi = 2 + c // 2
                    TOP(lambda tt_, c=c, pi=pi, rows=rows: tt_.matmul(
                        P[pi][:, (c % 2) * 512:(c % 2 + 1) * 512],
                        t["ONESB"][rows, 0:128], QSQ[rows, c * 512:(c + 1) * 512],
                        start=True, stop=True), inc="T" if c == 3 else None)
                evt = ("T", prog.cnt["T"])
                bk.pe_done(2, (0, 1), evt)
                bk.pe_done(3, (0, 1), evt)
                bk.rd_begin("v", 2, (0, 1))
                bk.rd_begin("v", 3, (0, 1))
                VOP(lambda v: v.reduce_max(
                    out=BSC[:, 14:15], in_=P[2][:, 0:1024], axis=AX.X))
                ev_r = VOP(lambda v: v.reduce_max(
                    out=BSC[:, 15:16], in_=P[3][:, 0:1024], axis=AX.X), inc="V")
                bk.rd_done(2, (0, 1), ev_r)
                bk.rd_done(3, (0, 1), ev_r)
                prog.drain("v")
                VOP(lambda v, outcol=outcol: v.tensor_tensor(
                    out=BSC[:, outcol:outcol + 1], in0=BSC[:, 14:15], in1=BSC[:, 15:16],
                    op=OP.max))
                prog.drain("v")
                return evt

            def wave_bounds(srcbuf, base):
                last_t = None
                for pc in range(2):
                    prog.wait("s", last_t)
                    ev_sq = SOP(lambda s, srcbuf=srcbuf, pc=pc: s.activation(
                        out=QSQ[:, 0:2048], in_=srcbuf[:, pc * 2048:(pc + 1) * 2048],
                        func=AF.Square), inc="S")
                    prog.wait("t", ev_sq)
                    for side in range(2):
                        last_t = side_sums(slice(side * 64, (side + 1) * 64),
                                           base + 2 * pc + side)

            for w in range(4):
                issue_msb(4 * w)
                if L == 1:
                    ev_q = load_aws(0, 0, wq[:, w * 256:(w + 1) * 256].rearrange("(ci p) q -> p ci q", p=128), 2048)
                    ev_k = load_aws(1, 0, d["l1wk_t"][:, w * 256:(w + 1) * 256].rearrange("(ci p) q -> p ci q", p=128), 2048)
                else:
                    ev_q = load_aws(0, 0, wq[:, w * 256:(w + 1) * 256].rearrange("(ci p) q -> p ci q", p=128), 2048)
                    ev_k = load_aws(1, 0, d["l0wkv_t"][:, 0:256].rearrange("(ci p) q -> p ci q", p=128), 2048) if w == 0 else None
                prog.waitall("t", lnf_evs)
                prog.wait("v", wave_tread)
                prog.wait("v", wave_sq_sread[0])
                prog.wait("t", ev_q)
                for pc in range(2):
                    qc = 2 * w + pc
                    for tq in range(4):
                        pi, half = pc, tq % 2
                        bk.pe_begin(pi, (half,))
                        for ci in range(8):
                            last = ci == 7
                            TOP(lambda tt_, pi=pi, half=half, ci=ci, pc=pc, tq=tq, last=last: tt_.matmul(
                                P[pi][:, half * 512:(half + 1) * 512],
                                AWS[:, ci * 256 + pc * 128:ci * 256 + (pc + 1) * 128],
                                LNF[:, ci * T + tq * 512:ci * T + (tq + 1) * 512],
                                start=(ci == 0), stop=last), inc="T" if last else None)
                        bk.pe_done(pi, (half,), ("T", prog.cnt["T"]))
                        bk.rd_begin("v", pi, (half,))
                        if L == 1:
                            ev_e = VOP(lambda v, pi=pi, half=half, pc=pc, tq=tq, qc=qc: v.tensor_scalar(
                                out=QW[:, pc * 2048 + tq * 512:pc * 2048 + (tq + 1) * 512],
                                in0=P[pi][:, half * 512:(half + 1) * 512],
                                scalar1=t["BQKV"][:, qc:qc + 1], scalar2=0.125,
                                op0=OP.add, op1=OP.mult), inc="V")
                        else:
                            ev_e = VOP(lambda v, pi=pi, half=half, pc=pc, tq=tq: v.tensor_scalar(
                                out=QW[:, pc * 2048 + tq * 512:pc * 2048 + (tq + 1) * 512],
                                in0=P[pi][:, half * 512:(half + 1) * 512],
                                scalar1=0.125, scalar2=None, op0=OP.mult), inc="V")
                        bk.rd_done(pi, (half,), ev_e)
                ev_qprod = ("T", prog.cnt["T"])
                aws_tread[0] = ev_qprod
                if L == 1 or w == 0:
                    prog.wait("t", ev_k)
                    for pc in range(2):
                        qc = 2 * w + pc
                        for tq in range(4):
                            pi, half = pc, tq % 2
                            bk.pe_begin(pi, (half,))
                            for ci in range(8):
                                last = ci == 7
                                TOP(lambda tt_, pi=pi, half=half, ci=ci, pc=pc, tq=tq, last=last: tt_.matmul(
                                    P[pi][:, half * 512:(half + 1) * 512],
                                    AWS[:, 2048 + ci * 256 + pc * 128:2048 + ci * 256 + (pc + 1) * 128],
                                    LNF[:, ci * T + tq * 512:ci * T + (tq + 1) * 512],
                                    start=(ci == 0), stop=last), inc="T" if last else None)
                            bk.pe_done(pi, (half,), ("T", prog.cnt["T"]))
                            bk.rd_begin("v", pi, (half,))
                            if L == 1:
                                ev_e = VOP(lambda v, pi=pi, half=half, pc=pc, tq=tq, qc=qc: v.tensor_scalar(
                                    out=KW[:, pc * 2048 + tq * 512:pc * 2048 + (tq + 1) * 512],
                                    in0=P[pi][:, half * 512:(half + 1) * 512],
                                    scalar1=t["BQKV"][:, 8 + qc:8 + qc + 1], scalar2=None,
                                    op0=OP.add), inc="V")
                            else:
                                ev_e = VOP(lambda v, pi=pi, half=half, pc=pc, tq=tq: v.tensor_copy(
                                    KW[:, pc * 2048 + tq * 512:pc * 2048 + (tq + 1) * 512],
                                    P[pi][:, half * 512:(half + 1) * 512]), inc="V")
                            bk.rd_done(pi, (half,), ev_e)
                    aws_tread[1] = ("T", prog.cnt["T"])
                if L == 1:
                    ev_v = load_aws(0, 0, d["l1wv_t"][:, w * 256:(w + 1) * 256].rearrange("(ci p) q -> p ci q", p=128), 2048)
                    dv = 256
                elif w == 0:
                    ev_v = load_aws(1, 0, d["l0wkv_t"][:, 256:512].rearrange("(ci p) q -> p ci q", p=128), 2048)
                    dv = 256
                else:
                    ev_v = None
                if ev_v is not None:
                    vbuf = 0 if L == 1 else 1
                    if ev_bv is not None:
                        prog.wait("t", ev_bv)
                    prog.wait("v", wave_tread)
                    ev_ones = VOP(lambda v: v.memset(
                        VT.rearrange("p (tt s c) -> p tt s c", tt=16, s=4)[:, :, :, 64:65], 1.0), inc="V")
                    prog.wait("t", ev_v)
                    for tt in range(NTT):
                        pi, half = tt % 2, (tt // 2) % 2
                        bk.pe_begin(pi, (half,))
                        first = True
                        if L == 1:
                            TOP(lambda tt_, pi=pi, half=half, w=w: tt_.matmul(
                                P[pi][:, half * 512:half * 512 + 256],
                                t["ONESF"][0:1, 0:128],
                                t["ROWS"][0:1, 1024 + w * 256:1024 + (w + 1) * 256],
                                start=True, stop=False))
                            first = False
                        for ci in range(8):
                            last = ci == 7
                            TOP(lambda tt_, pi=pi, half=half, ci=ci, tt=tt, first=first, vbuf=vbuf, last=last: tt_.matmul(
                                P[pi][:, half * 512:half * 512 + 256],
                                LNF[:, ci * T + tt * 128:ci * T + (tt + 1) * 128],
                                AWS[:, vbuf * 2048 + ci * 256:vbuf * 2048 + (ci + 1) * 256],
                                start=(first and ci == 0), stop=last), inc="T" if last else None)
                        bk.pe_done(pi, (half,), ("T", prog.cnt["T"]))
                        bk.rd_begin("v", pi, (half,))
                        prog.wait("v", ev_ones)
                        dst = VT.rearrange("p (tt2 s c) -> p tt2 s c", tt2=16, s=4)[:, tt, 0:4, 0:64]
                        ev_e = VOP(lambda v, pi=pi, half=half, dst=dst: v.tensor_copy(
                            dst, P[pi][:, half * 512:half * 512 + 256]
                            .rearrange("p (s c) -> p s c", c=64)), inc="V")
                        bk.rd_done(pi, (half,), ev_e)
                    ev_vprod = ("T", prog.cnt["T"])
                    aws_tread[vbuf] = ev_vprod
                ev_qkv_v = ("V", prog.cnt["V"])

                prog.waitall("s", [expt_tread[0], expt_tread[1], ev_qkv_v, recrow_tread[0]])
                wave_bounds(QW, 4)
                if L == 1 or w == 0:
                    wave_bounds(KW, 8)
                wave_sq_sread[0] = ("S", prog.cnt["S"])
                VOP(lambda v: v.tensor_tensor(
                    out=BSC[:, 12:16], in0=BSC[:, 4:8], in1=BSC[:, 8:12], op=OP.mult))
                ev_mm_ = ("V", prog.cnt["V"])
                prog.wait("s", ev_mm_)
                ev_sqrt = SOP(lambda s: s.activation(
                    out=BSC[:, 4:8], in_=BSC[:, 12:16], func=AF.Sqrt, scale=1.1236), inc="S")
                prog.wait("v", ev_sqrt)
                prog.wait("v", negbt_sread)
                VOP(lambda v, L=L, w=w: v.tensor_tensor(
                    out=t["NEGBT"][:, 0:4], in0=t["PBMB"][:, L * 16 + 4 * w:L * 16 + 4 * w + 4],
                    in1=BSC[:, 4:8], op=OP.add))
                prog.drain("v")
                ev_negbt = VOP(lambda v: v.tensor_scalar(
                    out=t["NEGBT"][:, 0:4], in0=t["NEGBT"][:, 0:4],
                    scalar1=-1.0, scalar2=None, op0=OP.mult), inc="V")

                for hh in range(4):
                    h = 4 * w + hh
                    pc = hh // 2
                    side = hh % 2
                    oc = h // 2
                    rows = slice(side * 64, (side + 1) * 64)
                    kwc = pc if L == 1 else (2 * w + pc) % 2
                    pav = 2 + (h % 2)
                    slot = (h - 4 * w) if L == 1 else (h % 4)
                    mb = h % 2
                    prog.wait("t", ev_qkv_v)
                    ev_msb = msb_pending[mb]
                    if hh < 3:
                        issue_msb(h + 1)

                    for qh in range(2):
                        exp_ev = [None, None]

                        def emit_av(scav):
                            parv = scav % 2
                            prog.wait("t", exp_ev[parv])
                            for jj in range(2):
                                last = jj == 1
                                TOP(lambda tt_, pav=pav, scav=scav, slot=slot, parv=parv, jj=jj: tt_.matmul(
                                    P[pav][0:65, jj * 512:(jj + 1) * 512],
                                    VT[:, scav * 260 + slot * 65:scav * 260 + slot * 65 + 65],
                                    EXPT[:, parv * 1024 + jj * 512:parv * 1024 + (jj + 1) * 512],
                                    start=(scav == 0), stop=(scav == 15)),
                                    inc="T" if last else None)
                            expt_tread[parv] = ("T", prog.cnt["T"])

                        bk.pe_begin(pav, (0, 1))
                        for sc in range(16):
                            par = sc % 2
                            c0 = 1983 + 1024 * qh - 128 * sc
                            bk.pe_begin(par, (0, 1))
                            prog.wait("t", ev_msb)
                            for jj in range(2):
                                TOP(lambda tt_, par=par, mb=mb, c0=c0, jj=jj: tt_.matmul(
                                    P[par][:, jj * 512:(jj + 1) * 512],
                                    t["IDENT"][:],
                                    MSB[:, mb * 4096 + c0 + jj * 512:mb * 4096 + c0 + (jj + 1) * 512],
                                    start=True, stop=False))
                            for jj in range(2):
                                last = jj == 1
                                TOP(lambda tt_, par=par, rows=rows, sc=sc, jj=jj, qh=qh, kwc=kwc, pc=pc: tt_.matmul(
                                    P[par][:, jj * 512:(jj + 1) * 512],
                                    KW[rows, kwc * 2048 + sc * 128:kwc * 2048 + (sc + 1) * 128],
                                    QW[rows, pc * 2048 + qh * 1024 + jj * 512:pc * 2048 + qh * 1024 + (jj + 1) * 512],
                                    start=False, stop=True), inc="T" if last else None)
                            bk.pe_done(par, (0, 1), ("T", prog.cnt["T"]))
                            bk.rd_begin("s", par, (0, 1))
                            prog.wait("s", expt_tread[par])
                            prog.wait("s", ev_negbt)
                            ev_exp = SOP(lambda s, par=par, hh=hh: s.activation(
                                out=EXPT[:, par * 1024:(par + 1) * 1024], in_=P[par][:, 0:1024],
                                func=AF.Exp, bias=t["NEGBT"][:, hh:hh + 1]), inc="S")
                            negbt_sread = ev_exp
                            exp_ev[par] = ev_exp
                            bk.rd_done(par, (0, 1), ev_exp)
                            if sc >= 1:
                                emit_av(sc - 1)
                        emit_av(15)
                        bk.pe_done(pav, (0, 1), expt_tread[15 % 2])
                        RECROW = QSQ.bitcast(F32)
                        bk.rd_begin("s", pav, (0, 1))
                        prog.wait("s", recrow_tread[0])
                        ev_recr = SOP(lambda s, pav=pav: s.activation(
                            out=RECROW[0:1, 0:1024], in_=P[pav][64:65, 0:1024],
                            func=AF.Copy), inc="S")
                        bk.rd_done(pav, (0, 1), ev_recr)
                        psc = 0
                        bk.pe_begin(psc, (0, 1))
                        prog.wait("t", ev_recr)
                        for jj in range(2):
                            last = jj == 1
                            TOP(lambda tt_, psc=psc, jj=jj: tt_.matmul(
                                P[psc][:, jj * 512:(jj + 1) * 512],
                                t["ONESF"][0:1, 0:128],
                                RECROW[0:1, jj * 512:(jj + 1) * 512],
                                start=True, stop=True), inc="T" if last else None)
                        ev_rb = ("T", prog.cnt["T"])
                        recrow_tread[0] = ev_rb
                        bk.pe_done(psc, (0, 1), ev_rb)
                        bk.rd_begin("v", psc, (0, 1))
                        ev_rc = VOP(lambda v, psc=psc: v.reciprocal(
                            out=RECB[0:128, 0:1024], in_=P[psc][:, 0:1024]), inc="V")
                        bk.rd_done(psc, (0, 1), ev_rc)
                        prog.wait("v", ev_rc)
                        bk.rd_begin("v", pav, (0, 1))
                        if side == 0:
                            ev_ev = VOP(lambda v, pav=pav, oc=oc, qh=qh: v.tensor_tensor(
                                out=OUTC(oc)[0:64, qh * 1024:(qh + 1) * 1024],
                                in0=P[pav][0:64, 0:1024], in1=RECB[0:64, 0:1024],
                                op=OP.mult), inc="V")
                            out_evs.append(ev_ev)
                        else:
                            prog.wait("v", tmps_dread)
                            ev_ev = VOP(lambda v, pav=pav: v.tensor_tensor(
                                out=t["TMPS"][0:64, 0:1024],
                                in0=P[pav][0:64, 0:1024], in1=RECB[0:64, 0:1024],
                                op=OP.mult), inc="V")
                            prog.wait("s", ev_ev)
                            ev_sh = BDMA(lambda s, oc=oc, qh=qh: s.dma_start(
                                out=OUTC(oc)[64:128, qh * 1024:(qh + 1) * 1024],
                                in_=t["TMPS"][0:64, 0:1024]))
                            tmps_dread = ev_sh
                            out_evs.append(ev_sh)
                        bk.rd_done(pav, (0, 1), ev_ev)
                    msb_tread[mb] = ("T", prog.cnt["T"])
                wave_tread = ("T", prog.cnt["T"])

            prog.waitall("t", out_evs)
            prog.wait("t", ev_rows)
            for eh in range(2):
                prog.wait("g", aws_tread[0])
                prog.wait("g", aws_tread[1])
                prog.wait("g", tmps_dread)
                ev_w = GDMA(lambda g, eh=eh: g.dma_start(
                    out=AWS[:, 0:4096].rearrange("p (ci q) -> p ci q", ci=8),
                    in_=wo[:, eh * 512:(eh + 1) * 512].rearrange("(ci p) q -> p ci q", p=128)), "DW0")
                prog.wait("t", ev_w)
                for tt in range(NTT):
                    pi, half = tt % 2, (tt // 2) % 2
                    bk.pe_begin(pi, (half,))
                    TOP(lambda tt_, pi=pi, half=half, eh=eh: tt_.matmul(
                        P[pi][:, half * 512:(half + 1) * 512],
                        t["ONESF"][0:1, 0:128],
                        t["ROWS"][0:1, eh * 512:(eh + 1) * 512],
                        start=True, stop=False))
                    for fc in range(8):
                        last = fc == 7
                        TOP(lambda tt_, pi=pi, half=half, fc=fc, tt=tt, last=last: tt_.matmul(
                            P[pi][:, half * 512:(half + 1) * 512],
                            OUTC(fc)[:, tt * 128:(tt + 1) * 128],
                            AWS[:, fc * 512:(fc + 1) * 512],
                            start=False, stop=last), inc="T" if last else None)
                    bk.pe_done(pi, (half,), ("T", prog.cnt["T"]))
                    bk.rd_begin("v", pi, (half,))
                    ev_e = VOP(lambda v, pi=pi, half=half, tt=tt, eh=eh: v.tensor_tensor(
                        out=XT[:, tt * C + eh * 512:tt * C + (eh + 1) * 512],
                        in0=P[pi][:, half * 512:(half + 1) * 512],
                        in1=XT[:, tt * C + eh * 512:tt * C + (eh + 1) * 512],
                        op=OP.add), inc="V")
                    bk.rd_done(pi, (half,), ev_e)
                aws_tread[0] = ("T", prog.cnt["T"])
                aws_tread[1] = aws_tread[0]

        emit_init()
        prog.barrier()
        for si, sub in enumerate(sub_ids):
            lnf_evs = emit_ln()
            if sub in (0, 2):
                ev_g = emit_gates(0, lnf_evs)
                emit_ffn(0, ev_g)
            elif sub in (3, 5):
                ev_g = emit_gates(1, lnf_evs)
                emit_ffn(1, ev_g)
            else:
                emit_attn(0 if sub == 1 else 1, lnf_evs)
            prog.barrier()
            if DEBUG_XT:
                dx = nc.dram_tensor("dbgxt%d" % si, [T, C], F32, kind="ExternalOutput").ap()
                ev_d = YDMA(lambda y, dx=dx: y.dma_start(
                    out=dx.rearrange("(tt p) c -> p tt c", p=128),
                    in_=XT[:].rearrange("p (tt c) -> p tt c", tt=NTT)))
                prog.wait("y", ev_d)
                prog.barrier()
        ev_y = YDMA(lambda y: y.dma_start(
            out=yout.rearrange("(tt p) c -> p tt c", p=128),
            in_=XT[:].rearrange("p (tt c) -> p tt c", tt=NTT)))
        prog.wait("y", ev_y)
        for nm in DEBUG_TAPS:
            ap = t[nm][:] if hasattr(t[nm], '__enter__') or 'SBTensor' in type(t[nm]).__name__ else t[nm]
            dt_out = nc.dram_tensor("dbg_" + nm, [128, ap.shape[1]], F32, kind="ExternalOutput").ap()
            if ap.dtype == F32:
                ev_d = YDMA(lambda y, ap=ap, dt_out=dt_out: y.dma_start(out=dt_out, in_=ap))
                prog.wait("y", ev_d)
            else:
                ev_d = prog.op("g", lambda g, ap=ap, dt_out=dt_out: g.dma_start(out=dt_out, in_=ap), "DWI", 16)
                prog.wait("g", ev_d)

        with nc.Block() as block:
            def mk(engname):
                def _(eng):
                    for item in prog.ir[engname]:
                        if item[0] == "wait":
                            eng.wait_ge(sem[item[1]], item[2])
                        elif item[0] == "drain":
                            eng.drain()
                        else:
                            _, fn, inc, k = item
                            r = fn(eng)
                            if inc is not None:
                                r.then_inc(sem[inc], k)
                return _
            block.tensor(mk("t"))
            block.vector(mk("v"))
            block.scalar(mk("s"))
            block.gpsimd(mk("g"))
            block.sync(mk("y"))
    nc._lazy_input_names = list(d.keys())
    return nc


def _host_prepare(inputs):
    f = np.float32
    out = {}
    out["ident"] = np.eye(128, dtype=f)
    n_head = 16
    start = 2.0 ** (-(2.0 ** (-(np.log2(n_head) - 3.0))))
    slopes = (start ** np.arange(1, n_head + 1)).astype(np.float64)
    idx = np.arange(4095, dtype=np.float64)
    rel = 2047.0 - idx
    row = np.minimum(rel, 0.0)
    pb0 = (slopes[:, None] * row[None, :] * (64.0 ** -0.5)).astype(f)
    pb1 = np.asarray(inputs["L1_attn_pos_bias"], dtype=f)
    pbrows = np.stack([pb0, pb1], 0)
    pad = np.zeros((2, 16, 127 + 4224), dtype=f)
    pad[:, :, 127:127 + 4095] = pbrows
    import ml_dtypes
    masksh = np.empty((2, 16, 128, 4224), dtype=f)
    for p in range(128):
        masksh[:, :, p, :] = pad[:, :, 127 - p:127 - p + 4224]
    out["masksh"] = masksh.astype(ml_dtypes.bfloat16)
    out["pbrows"] = pbrows
    tr = lambda a: np.ascontiguousarray(np.asarray(a, dtype=f).T)
    out["l0wq_t"] = tr(inputs["L0_attn_Wq"])
    out["l0wkv_t"] = tr(inputs["L0_attn_Wkv"])
    out["l0wproj_t"] = tr(inputs["L0_attn_Wproj"])
    out["l0bproj"] = np.asarray(inputs["L0_attn_bproj"], f)
    out["l0wgate"] = np.asarray(inputs["L0_ffn_w_gate"], f)
    out["l0wgu"] = np.asarray(inputs["L0_ffn_w_gate_up"], f).astype(ml_dtypes.bfloat16)
    out["l0wdown"] = np.asarray(inputs["L0_ffn_w_down"], f).astype(ml_dtypes.bfloat16)
    out["l1wq_t"] = tr(inputs["L1_attn_Wq"])
    out["l1wk_t"] = tr(inputs["L1_attn_Wk"])
    out["l1wv_t"] = tr(inputs["L1_attn_Wv"])
    out["l1wo_t"] = tr(inputs["L1_attn_Wo"])
    bqkv = np.stack([
        np.asarray(inputs["L1_attn_bq"], f).reshape(8, 128).T,
        np.asarray(inputs["L1_attn_bk"], f).reshape(8, 128).T,
        np.asarray(inputs["L1_attn_bv"], f).reshape(8, 128).T,
    ], 1)
    out["l1bqkv"] = np.ascontiguousarray(bqkv.reshape(128, 24))
    out["l1bv"] = np.asarray(inputs["L1_attn_bv"], f)
    out["l1bo"] = np.asarray(inputs["L1_attn_bo"], f)
    out["l1routerW"] = np.asarray(inputs["L1_ffn_router_W"], f)
    out["l1routerb"] = np.asarray(inputs["L1_ffn_router_b"], f)
    out["l1w1"] = np.asarray(inputs["L1_ffn_W1"], f).astype(ml_dtypes.bfloat16)
    out["l1b1t"] = np.ascontiguousarray(
        np.asarray(inputs["L1_ffn_b1"], f).reshape(8, 32, 128).transpose(2, 0, 1).reshape(128, 256))
    out["l1w2"] = np.asarray(inputs["L1_ffn_W2"], f).astype(ml_dtypes.bfloat16)
    out["l1b2"] = np.asarray(inputs["L1_ffn_b2"], f)
    return out


_PROG_CACHE = {}


def kernel(**inputs):
    aux = _host_prepare(inputs)
    x = np.asarray(inputs["L0_attn_x"], np.float32)
    xs = [np.ascontiguousarray(x[b]) for b in range(NCORES)]
    for grp in GROUPS:
        key = tuple(grp)
        if key not in _PROG_CACHE:
            ncb = build_program(list(grp))
            _PROG_CACHE[key] = (ncb, list(ncb._lazy_input_names))
        nc, names = _PROG_CACHE[key]
        in_maps = []
        for b in range(NCORES):
            m = {nm: aux[nm] for nm in names if nm != "x"}
            m["x"] = xs[b]
            in_maps.append(m)
        res = run_bass_kernel_spmd(nc, in_maps, core_ids=list(range(NCORES)))
        xs = [np.asarray(res.results[b]["y"], np.float32) for b in range(NCORES)]
    return np.stack(xs, 0)
